# version 09
# baseline (speedup 1.0000x reference)
from contextlib import ExitStack

import numpy as np
import ml_dtypes
import concourse.bass as bass
import concourse.mybir as mybir
from concourse.bass_utils import run_bass_kernel_spmd

F32 = mybir.dt.float32
BF16 = mybir.dt.bfloat16
ACT = mybir.ActivationFunctionType
ALU = mybir.AluOpType
AX = mybir.AxisListType

D = 2048
NT = 16
NF = 4
B = 8
HSL = 1024
NHC = HSL // 128
KSTEPS = 4

BFD = ml_dtypes.bfloat16


class Sem:

    def __init__(self, nc, name):
        self.h = nc.alloc_semaphore(name)
        self.n = 0

    def inc(self, inst, k=1):
        inst.then_inc(self.h, k)
        self.n += k
        return self.n

    def wait(self, eng, val=None):
        eng.wait_ge(self.h, self.n if val is None else val)



def _dump_and_stop(nc, sbuf_list, dram_list=()):
    sdd = Sem(nc, "sdd")
    with nc.Block() as blk:
        @blk.sync
        def _(sy):
            for name, t in sbuf_list:
                hb = nc.dram_tensor("dbg_" + name, list(t.shape), t.dtype,
                                    kind="ExternalOutput")
                sdd.inc(sy.dma_start(out=hb[:, :], in_=t[:, :]), 16)
            for name, t in dram_list:
                hb = nc.dram_tensor("dbg_" + name, list(t.shape), t.dtype,
                                    kind="ExternalOutput")
                ap_in = t.ap() if hasattr(t, "ap") else t
                sdd.inc(sy.dma_start(out=hb.ap(), in_=t.ap()), 16)
            sdd.wait(sy)
    return nc


def build_nc(stop_after=None):
    nc = bass.Bass("TRN2", target_bir_lowering=False)

    x_h = nc.dram_tensor("x", [D, D], F32, kind="ExternalInput")
    wq_h = nc.dram_tensor("wq", [D, D], F32, kind="ExternalInput")
    wk_h = nc.dram_tensor("wk", [D, D], F32, kind="ExternalInput")
    wo_h = nc.dram_tensor("wo", [D, D], F32, kind="ExternalInput")
    gatew_h = nc.dram_tensor("gatew", [D, D], F32, kind="ExternalInput")
    aggw_h = nc.dram_tensor("aggw_sl", [D, 256], F32, kind="ExternalInput")
    bcw_h = nc.dram_tensor("bcw_sl", [D, 256], F32, kind="ExternalInput")
    wv_h = nc.dram_tensor("wv_blk", [NT, D, 128], F32, kind="ExternalInput")
    d1s_h = nc.dram_tensor("d1s_pm", [NT, 128, NHC, 128], F32,
                           kind="ExternalInput")
    d2s_h = nc.dram_tensor("d2s_pm", [NHC, 128, NT, 128], F32,
                           kind="ExternalInput")
    lastrows_h = nc.dram_tensor("lastrows", [B, D], F32, kind="ExternalInput")
    in_g8_h = nc.dram_tensor("in_g8", [B, D], F32, kind="ExternalInput")
    in_b8_h = nc.dram_tensor("in_b8", [B, D], F32, kind="ExternalInput")
    pn_g_fm_h = nc.dram_tensor("pn_g_fm", [128, NT], F32, kind="ExternalInput")
    pn_b_fm_h = nc.dram_tensor("pn_b_fm", [128, NT], F32, kind="ExternalInput")
    agg_bsl_h = nc.dram_tensor("agg_bsl", [B, 256], F32, kind="ExternalInput")
    d1b_fm_h = nc.dram_tensor("d1b_fm", [128, NHC], F32, kind="ExternalInput")
    d2b_fm_h = nc.dram_tensor("d2b_fm", [128, NT], F32, kind="ExternalInput")
    bc_b8_h = nc.dram_tensor("bc_b8", [B, D], F32, kind="ExternalInput")
    og128_h = nc.dram_tensor("og128", [128, D], F32, kind="ExternalInput")
    ob128_h = nc.dram_tensor("ob128", [128, D], F32, kind="ExternalInput")
    gate_b_h = nc.dram_tensor("gate_b_row", [1, D], F32, kind="ExternalInput")
    sel_h = nc.dram_tensor("sel_rep", [B, 128], F32, kind="ExternalInput")
    ident_h = nc.dram_tensor("ident", [128, 128], BF16, kind="ExternalInput")
    ident32_h = nc.dram_tensor("ident32", [128, 128], F32, kind="ExternalInput")
    onescol32_h = nc.dram_tensor("ones_col32", [128, 1], F32,
                                 kind="ExternalInput")
    ones1_32_h = nc.dram_tensor("ones1_32", [1, 128], F32, kind="ExternalInput")
    onescol_h = nc.dram_tensor("ones_col", [128, 1], BF16, kind="ExternalInput")
    ones1_h = nc.dram_tensor("ones1", [1, 128], BF16, kind="ExternalInput")

    out_h = nc.dram_tensor("out", [D, D], F32, kind="ExternalOutput")

    gate_sp = nc.dram_tensor("gate_sp", [D, D], BF16)
    wT_sp = nc.dram_tensor("wT_sp", [128, NT, D], BF16)
    attn_sp = nc.dram_tensor("attn_sp", [128, NT, D], BF16)
    cc_in = nc.dram_tensor("cc_in", [B, D], F32)
    cc_out = nc.dram_tensor("cc_out", [B, D], F32)
    ag_in = nc.dram_tensor("ag_in", [B, 256], F32)
    ag_out = nc.dram_tensor("ag_out", [B * B, 256], F32)
    ccf_in = nc.dram_tensor("ccf_in", [128, 128], F32)
    ccf_out = nc.dram_tensor("ccf_out", [128, 128], F32)

    ident = nc.alloc_sbuf_tensor("identS", [128, 128], BF16)
    ones_col = nc.alloc_sbuf_tensor("ones_colS", [128, 1], BF16)
    ones1 = nc.alloc_sbuf_tensor("ones1S", [1, 128], BF16)
    sel = nc.alloc_sbuf_tensor("selS", [B, 128], F32)
    mu_cols = nc.alloc_sbuf_tensor("mu_cols", [128, NT], F32)
    rsig_cols = nc.alloc_sbuf_tensor("rsig_cols", [128, NT], F32)
    r_cols = nc.alloc_sbuf_tensor("r_cols", [128, NT], F32)
    nmr_cols = nc.alloc_sbuf_tensor("nmr_cols", [128, NT], F32)
    bc_rep = nc.alloc_sbuf_tensor("bc_rep", [128, D], F32)

    sv = Sem(nc, "sv")
    st = Sem(nc, "st")
    ss = Sem(nc, "ss")
    sg = Sem(nc, "sg")
    scc = Sem(nc, "scc")

    sdT = Sem(nc, "sdT")
    swW = [Sem(nc, "swW0"), Sem(nc, "swW1")]
    swC = Sem(nc, "swC")
    swB = Sem(nc, "swB")

    thought_fm = nc.alloc_sbuf_tensor("thought_fm", [128, NT * B], F32)
    sq_fm = nc.alloc_sbuf_tensor("sq_fm", [128, NT * B], F32)
    id32s = nc.alloc_sbuf_tensor("id32s", [128, 128], F32)
    t1_bf = nc.alloc_sbuf_tensor("t1_bf", [128, NT * B], BF16)
    t2_bf = nc.alloc_sbuf_tensor("t2_bf", [128, NHC * B], BF16)
    th_bf = nc.alloc_sbuf_tensor("th_bf", [128, NT * B], BF16)
    mu8r = nc.alloc_sbuf_tensor("mu8r", [128, B], F32)
    rs8r = nc.alloc_sbuf_tensor("rs8r", [128, B], F32)
    strow = nc.alloc_sbuf_tensor("strow", [1, 32], F32)
    ccfm = nc.alloc_sbuf_tensor("ccfm", [128, 128], F32)
    pngf = nc.alloc_sbuf_tensor("pngf", [128, NT], F32)
    pnbf = nc.alloc_sbuf_tensor("pnbf", [128, NT], F32)
    d1bf = nc.alloc_sbuf_tensor("d1bf", [128, NHC], F32)
    d2bf = nc.alloc_sbuf_tensor("d2bf", [128, NT], F32)
    onc32 = nc.alloc_sbuf_tensor("onc32", [128, 1], F32)
    on1_32 = nc.alloc_sbuf_tensor("on1_32", [1, 128], F32)

    with ExitStack() as _ctx:
        _e = _ctx.enter_context
        t_last = _e(nc.sbuf_tensor("t_last", [B, D], F32))
        t_ing = _e(nc.sbuf_tensor("t_ing", [B, D], F32))
        t_inb = _e(nc.sbuf_tensor("t_inb", [B, D], F32))
        t_aggb = _e(nc.sbuf_tensor("t_aggb", [B, 256], F32))
        twork = _e(nc.sbuf_tensor("t_work", [B, D], F32))
        thought = _e(nc.sbuf_tensor("t_thought", [B, D], F32))
        tbf = _e(nc.sbuf_tensor("t_bf", [B, D], BF16))
        tfm = _e(nc.sbuf_tensor("t_fm", [128, NT * B], BF16))
        tstats = _e(nc.sbuf_tensor("t_stats", [B, 26], F32))
        wck0 = _e(nc.sbuf_tensor("t_wck0", [128, D], BF16))
        wck1 = _e(nc.sbuf_tensor("t_wck1", [128, D], BF16))
        ccst = _e(nc.sbuf_tensor("t_ccst", [B, D], F32))
        ptrT = [_e(nc.psum_tensor("t_ptr%d" % i, [128, B], BF16))
                for i in range(3)]
        pbig = _e(nc.psum_tensor("t_pbig", [B, D], F32))
        blk = _e(nc.Block())
        wck = [wck0, wck1]

        @blk.sync
        def _(sy):
            for dst, srcb in ((t_last, lastrows_h), (t_ing, in_g8_h),
                              (t_inb, in_b8_h), (t_aggb, agg_bsl_h),
                              (id32s, ident32_h)):
                sdT.inc(sy.dma_start(out=dst[:, :], in_=srcb[:, :]), 16)
            for dst, srcb in ((pngf, pn_g_fm_h), (pnbf, pn_b_fm_h),
                              (d1bf, d1b_fm_h), (d2bf, d2b_fm_h),
                              (onc32, onescol32_h), (on1_32, ones1_32_h)):
                sdT.inc(sy.dma_start(out=dst[:, :], in_=srcb[:, :]), 16)
            sdT.inc(sy.dma_start(out=ident[:, :], in_=ident_h[:, :]), 16)
            sdT.inc(sy.dma_start(out=ones_col[:, :], in_=onescol_h[:, :]), 16)
            sdT.inc(sy.dma_start(out=ones1[:, :], in_=ones1_h[:, :]), 16)
            sdT.inc(sy.dma_start(out=sel[:, :], in_=sel_h[:, :]), 16)
        small_loads = sdT.n

        def ln8(src_ap, dst_ap, W, eps, gamma=None, beta=None, pre_wait=None):
            nchunk = (W + 511) // 512

            @blk.vector
            def _(v):
                if pre_wait is not None:
                    pre_wait(v)
                sv.wait(v)
                for c in range(nchunk):
                    ist = v.bn_stats(tstats[:, 6 * c:6 * c + 6],
                                     src_ap[:, 512 * c:512 * (c + 1)])
                iv = sv.inc(ist)
                v.wait_ge(sv.h, iv)
                iv = sv.inc(v.bn_aggr(tstats[:, 24:26], tstats[:, 0:6 * nchunk]))
                v.wait_ge(sv.h, iv)
                sv.inc(v.tensor_scalar_add(tstats[:, 25:26], tstats[:, 25:26],
                                           eps))

            @blk.scalar
            def _(s):
                sv.wait(s)
                isx = ss.inc(s.activation(tstats[:, 25:26], tstats[:, 25:26],
                                          ACT.Ln))
                s.wait_ge(ss.h, isx)
                ss.inc(s.activation(tstats[:, 25:26], tstats[:, 25:26],
                                    ACT.Exp, scale=-0.5))

            @blk.vector
            def _(v):
                ss.wait(v)
                iv = sv.inc(v.tensor_scalar(dst_ap, src_ap, tstats[:, 24:25],
                                            tstats[:, 25:26], ALU.subtract,
                                            ALU.mult))
                if gamma is not None:
                    v.wait_ge(sv.h, iv)
                    iv = sv.inc(v.tensor_mul(dst_ap, dst_ap, gamma))
                if beta is not None:
                    v.wait_ge(sv.h, iv)
                    sv.inc(v.tensor_add(dst_ap, dst_ap, beta))

        def to_fm(src_ap, W, dst, dst_dtype_ps, idap):
            nchunk = W // 128

            @blk.vector
            def _(v):
                sv.wait(v)
                sv.inc(v.tensor_copy(tbf[:, 0:W], src_ap))
            ev_base = sv.n
            tr_base = st.n

            for g4 in range(0, nchunk, 4):
                @blk.tensor
                def _(t, g4=g4):
                    for c in range(g4, min(nchunk, g4 + 4)):
                        if c == 0:
                            sv.wait(t, ev_base)
                        if c >= 3:
                            sv.wait(t, ev_base + c - 2)
                        st.inc(t.transpose(ptrT[c % 3][:, :],
                                           tbf[:, 128 * c:128 * (c + 1)],
                                           ident[0:B, 0:B]))

                @blk.vector
                def _(v, g4=g4):
                    for c in range(g4, min(nchunk, g4 + 4)):
                        v.wait_ge(st.h, tr_base + c + 1)
                        v.tensor_copy(dst[:, B * c:B * (c + 1)],
                                      ptrT[c % 3][:, :]).then_inc(sv.h, 1)
                        sv.n += 1

        def mm8(whbm, nk, W, psum):
            nN = (W + 511) // 512
            load_val = {}
            mmv = {}
            st_base = st.n

            def emit_load(ck):
                @blk.gpsimd
                def _(g, ck=ck):
                    st.wait(g, mmv[ck - 2] if ck >= 2 else st_base)
                    swW[ck % 2].inc(
                        g.dma_start(out=wck[ck % 2][:, 0:W],
                                    in_=whbm[128 * ck:128 * (ck + 1), 0:W]), 16)
                load_val[ck] = swW[ck % 2].n

            emit_load(0)
            if nk > 1:
                emit_load(1)
            for ck in range(nk):
                @blk.tensor
                def _(t, ck=ck):
                    if ck == 0:
                        sv.wait(t)
                    swW[ck % 2].wait(t, load_val[ck])
                    for n in range(nN):
                        w1 = min(W, 512 * (n + 1))
                        mi = t.matmul(psum[:, 512 * n:w1],
                                      tfm[:, B * ck:B * (ck + 1)],
                                      wck[ck % 2][:, 512 * n:w1],
                                      start=(ck == 0), stop=(ck == nk - 1))
                    st.inc(mi)
                mmv[ck] = st.n
                if ck + 2 < nk:
                    emit_load(ck + 2)

        def w_small(v):
            sdT.wait(v, small_loads)
        ln8(t_last[:, :], twork[:, :], D, 1e-5, pre_wait=w_small)
        ln8(twork[:, :], thought[:, :], D, 1e-6, gamma=t_ing[:, :],
            beta=t_inb[:, :])
        to_fm(thought[:, :], D, tfm, None, None)
        mm8(aggw_h, NT, 256, pbig)

        @blk.vector
        def _(v):
            st.wait(v)
            iv = sv.inc(v.tensor_copy(ccst[:, 0:256], pbig[:, 0:256]))
            v.wait_ge(sv.h, iv)
            sv.inc(v.tensor_add(ccst[:, 0:256], ccst[:, 0:256], t_aggb[:, :]))

        @blk.gpsimd
        def _(g):
            sv.wait(g)
            swC.inc(g.dma_start(out=ag_in[:, :], in_=ccst[:, 0:256]), 16)
            swC.wait(g)
            scc.inc(g.collective_compute(
                "AllGather", ALU.bypass, replica_groups=[list(range(B))],
                ins=[ag_in.ap().opt()], outs=[ag_out.ap().opt()]))
    agval = scc.n

    pT_ctx = ExitStack()
    wckb_ctx = ExitStack()
    pTa = pTb = wckb = bc_tm = bc_b8s = None

    def fm_ln(blk):
        @blk.scalar
        def _(s):
            sv.wait(s)
            ss.inc(s.activation(sq_fm[:, :], thought_fm[:, :], ACT.Square))

        @blk.tensor
        def _(t):
            ss.wait(t)
            sv.wait(t)
            st.inc(t.matmul(pTa[0:1, 128:256], onc32[:, :], thought_fm[:, :],
                            start=True, stop=False))
            st.inc(t.matmul(pTa[0:1, 256:384], onc32[:, :], sq_fm[:, :],
                            start=False, stop=True))

        @blk.vector
        def _(v):
            st.wait(v)
            iv = sv.inc(v.reduce_sum(
                strow[0:1, 0:8],
                bass.AP(pTa.ap().tensor, 128, [[512, 1], [1, 8], [8, 16]]),
                axis=AX.X))
            v.wait_ge(sv.h, iv)
            iv = sv.inc(v.reduce_sum(
                strow[0:1, 8:16],
                bass.AP(pTa.ap().tensor, 256, [[512, 1], [1, 8], [8, 16]]),
                axis=AX.X))
            v.wait_ge(sv.h, iv)
            iv = sv.inc(v.tensor_scalar_mul(strow[0:1, 0:8], strow[0:1, 0:8],
                                            1.0 / D))
            v.wait_ge(sv.h, iv)
            iv = sv.inc(v.tensor_scalar_mul(strow[0:1, 8:16], strow[0:1, 8:16],
                                            1.0 / D))
            v.wait_ge(sv.h, iv)
            iv = sv.inc(v.tensor_mul(strow[0:1, 16:24], strow[0:1, 0:8],
                                     strow[0:1, 0:8]))
            v.wait_ge(sv.h, iv)
            iv = sv.inc(v.tensor_sub(strow[0:1, 8:16], strow[0:1, 8:16],
                                     strow[0:1, 16:24]))
            v.wait_ge(sv.h, iv)
            sv.inc(v.tensor_scalar_add(strow[0:1, 8:16], strow[0:1, 8:16],
                                       1e-6))

        @blk.scalar
        def _(s):
            sv.wait(s)
            isx = ss.inc(s.activation(strow[0:1, 8:16], strow[0:1, 8:16],
                                      ACT.Ln))
            s.wait_ge(ss.h, isx)
            ss.inc(s.activation(strow[0:1, 8:16], strow[0:1, 8:16],
                                ACT.Exp, scale=-0.5))

        @blk.vector
        def _(v):
            ss.wait(v)
            iv = sv.inc(v.tensor_mul(strow[0:1, 16:24], strow[0:1, 0:8],
                                     strow[0:1, 8:16]))
            v.wait_ge(sv.h, iv)
            sv.inc(v.tensor_scalar_mul(strow[0:1, 16:24], strow[0:1, 16:24],
                                       -1.0))

        @blk.tensor
        def _(t):
            sv.wait(t)
            st.inc(t.matmul(pTa[:, 384:392], on1_32[:, :], strow[0:1, 16:24],
                            start=True, stop=False))
            st.inc(t.matmul(pTa[:, 392:400], on1_32[:, :], strow[0:1, 8:16],
                            start=False, stop=True))

        @blk.vector
        def _(v):
            st.wait(v)
            iv = sv.inc(v.tensor_copy(mu8r[:, :], pTa[:, 384:392]))
            v.wait_ge(sv.h, iv)
            iv = sv.inc(v.tensor_copy(rs8r[:, :], pTa[:, 392:400]))
            v.wait_ge(sv.h, iv)
            ist = None
            for c in range(NT):
                ist = v.tensor_mul(sq_fm[:, B * c:B * (c + 1)],
                                   thought_fm[:, B * c:B * (c + 1)],
                                   rs8r[:, :])
            iv = sv.inc(ist)
            v.wait_ge(sv.h, iv)
            for c in range(NT):
                ist = v.tensor_add(sq_fm[:, B * c:B * (c + 1)],
                                   sq_fm[:, B * c:B * (c + 1)], mu8r[:, :])
            iv = sv.inc(ist)
            v.wait_ge(sv.h, iv)
            for c in range(NT):
                ist = v.tensor_scalar(t1_bf[:, B * c:B * (c + 1)],
                                      sq_fm[:, B * c:B * (c + 1)],
                                      pngf[:, c:c + 1], pnbf[:, c:c + 1],
                                      ALU.mult, ALU.add)
            sv.inc(ist)

    fmst = {}

    def fm_d1_loads(blk, k):
        @blk.gpsimd
        def _(g, k=k):
            st.wait(g, fmst.get(("d2m", k - 1), st.n))
            for c in range(NT):
                swB.inc(g.dma_start(
                    out=wckb[:, c * 1024:(c + 1) * 1024],
                    in_=d1s_h[c].rearrange("p c f -> p (c f)")), 16)
        fmst[("d1l", k)] = swB.n

    def fm_d1_mms(blk, k):
        @blk.tensor
        def _(t, k=k):
            sv.wait(t)
            swB.wait(t, fmst[("d1l", k)])
            for c in range(NT):
                for blkh in range(NHC):
                    mi = t.matmul(
                        pTa[:, blkh * 8:(blkh + 1) * 8],
                        wckb[:, c * 1024 + blkh * 128:
                             c * 1024 + blkh * 128 + 128],
                        t1_bf[:, B * c:B * (c + 1)],
                        start=(c == 0 and blkh == 0),
                        stop=(c == NT - 1 and blkh == NHC - 1))
            st.inc(mi)
        fmst[("d1m", k)] = st.n

        @blk.vector
        def _(v, k=k):
            st.wait(v, fmst[("d1m", k)])
            ist = None
            for blkh in range(NHC):
                ist = v.tensor_scalar_add(t2_bf[:, blkh * 8:(blkh + 1) * 8],
                                          pTa[:, blkh * 8:(blkh + 1) * 8],
                                          d1bf[:, blkh:blkh + 1])
            sv.inc(ist)

        @blk.scalar
        def _(s):
            sv.wait(s)
            ss.inc(s.activation(t2_bf[:, :], t2_bf[:, :], ACT.Gelu))
        fmst[("gelu", k)] = ss.n

    def fm_d2_loads(blk, k):
        @blk.gpsimd
        def _(g, k=k):
            st.wait(g, fmst[("d1m", k)])
            for c in range(NHC):
                swB.inc(g.dma_start(
                    out=wckb[:, c * 2048:(c + 1) * 2048],
                    in_=d2s_h[c].rearrange("p c f -> p (c f)")), 16)
        fmst[("d2l", k)] = swB.n

    def fm_d2_mms(blk, k):
        @blk.tensor
        def _(t, k=k):
            ss.wait(t, fmst[("gelu", k)])
            swB.wait(t, fmst[("d2l", k)])
            for c in range(NHC):
                for blkd in range(NT):
                    mi = t.matmul(
                        pTb[:, blkd * 8:(blkd + 1) * 8],
                        wckb[:, c * 2048 + blkd * 128:
                             c * 2048 + blkd * 128 + 128],
                        t2_bf[:, B * c:B * (c + 1)],
                        start=(c == 0 and blkd == 0),
                        stop=(c == NHC - 1 and blkd == NT - 1))
            st.inc(mi)
        fmst[("d2m", k)] = st.n

        @blk.vector
        def _(v, k=k):
            st.wait(v, fmst[("d2m", k)])
            sv.inc(v.tensor_copy(ccfm[:, :], pTb[:, 0:128]))

        @blk.gpsimd
        def _(g):
            sv.wait(g)
            swC.inc(g.dma_start(out=ccf_in[:, :], in_=ccfm[:, :]), 16)
            swC.wait(g)
            scc.inc(g.collective_compute(
                "AllReduce", ALU.add, replica_groups=[list(range(B))],
                ins=[ccf_in.ap().opt()], outs=[ccf_out.ap().opt()]))
        fmst[("ar", k)] = scc.n

    def fm_update(blk, k):
        @blk.gpsimd
        def _(g, k=k):
            scc.wait(g, fmst[("ar", k)])
            swC.inc(g.dma_start(out=ccfm[:, :], in_=ccf_out[:, :]), 16)
        cw = swC.n

        @blk.vector
        def _(v, k=k, cw=cw):
            swC.wait(v, cw)
            iv = sv.inc(v.tensor_add(thought_fm[:, :], thought_fm[:, :],
                                     ccfm[:, :]))
            v.wait_ge(sv.h, iv)
            ist = None
            for c in range(NT):
                ist = v.tensor_scalar_add(thought_fm[:, B * c:B * (c + 1)],
                                          thought_fm[:, B * c:B * (c + 1)],
                                          d2bf[:, c:c + 1])
            sv.inc(ist)

    def fm_bc_a(blk):
        @blk.vector
        def _(v):
            sv.wait(v)
            sv.inc(v.tensor_copy(th_bf[:, :], thought_fm[:, :]))
        thbv = sv.n

        @blk.gpsimd
        def _(g):
            st.wait(g, fmst[("d2m", KSTEPS - 1)])
            for c in range(NT):
                swB.inc(g.dma_start(
                    out=wckb[:, c * 256:(c + 1) * 256],
                    in_=bcw_h[128 * c:128 * (c + 1), :]), 16)
        bcld = swB.n

        @blk.tensor
        def _(t):
            sv.wait(t, thbv)
            swB.wait(t, bcld)
            for c in range(NT):
                mi = t.matmul(pTa[0:B, 0:256], th_bf[:, B * c:B * (c + 1)],
                              wckb[:, c * 256:(c + 1) * 256],
                              start=(c == 0), stop=(c == NT - 1))
            st.inc(mi)

        @blk.vector
        def _(v):
            st.wait(v)
            sv.inc(v.tensor_copy(bc_tm[:, 0:256], pTa[0:B, 0:256]))

        @blk.gpsimd
        def _(g):
            sv.wait(g)
            swC.inc(g.dma_start(out=ag_in[:, :], in_=bc_tm[:, 0:256]), 16)
            swC.wait(g)
            scc.inc(g.collective_compute(
                "AllGather", ALU.bypass, replica_groups=[list(range(B))],
                ins=[ag_in.ap().opt()], outs=[ag_out.ap().opt()]))

    def fm_bc_b(blk):
        @blk.gpsimd
        def _(g):
            scc.wait(g)
            for r in range(B):
                swC.inc(g.dma_start(out=bc_tm[:, 256 * r:256 * (r + 1)],
                                    in_=ag_out[B * r:B * (r + 1), :]), 16)

        @blk.vector
        def _(v):
            swC.wait(v)
            sdT.wait(v)
            sv.inc(v.tensor_add(bc_tm[:, :], bc_tm[:, :], bc_b8s[:, :]))

    seam_plan = {}
    for k in range(KSTEPS):
        base = 6 * k
        seam_plan.setdefault(base + 0, []).append(
            lambda blk, k=k: fm_d1_loads(blk, k))
        seam_plan.setdefault(base + 2, []).append(
            lambda blk, k=k: (fm_update(blk, k - 1) if k > 0 else None,
                              fm_ln(blk)))
        seam_plan.setdefault(base + 3, []).append(
            lambda blk, k=k: (fm_d1_mms(blk, k), fm_d2_loads(blk, k)))
        seam_plan.setdefault(base + 6, []).append(
            lambda blk, k=k: fm_d2_mms(blk, k))
    seam_plan.setdefault(26, []).append(
        lambda blk: fm_update(blk, KSTEPS - 1))
    seam_plan.setdefault(27, []).append(fm_bc_a)
    seam_plan.setdefault(30, []).append(fm_bc_b)
    seam_i = [0]

    def fm_seam(blk):
        for fn in seam_plan.get(seam_i[0], []):
            fn(blk)
        seam_i[0] += 1

    sdX = [Sem(nc, "sdX0"), Sem(nc, "sdX1")]
    swS = [Sem(nc, "swS0"), Sem(nc, "swS1")]
    xn_ctx = nc.sbuf_tensor("xn_fm", [128, NT * D], BF16)
    xn_fm = xn_ctx.__enter__()
    q_ctx = nc.sbuf_tensor("q_tm", [128, NT * D], BF16)
    q_tm = q_ctx.__enter__()
    with ExitStack() as _ctx:
        _e = _ctx.enter_context
        xt0 = _e(nc.sbuf_tensor("b1_x0", [128, D], F32))
        xt1 = _e(nc.sbuf_tensor("b1_x1", [128, D], F32))
        xb0 = _e(nc.sbuf_tensor("b1_bf0", [128, D], BF16))
        xb1 = _e(nc.sbuf_tensor("b1_bf1", [128, D], BF16))
        bst = _e(nc.sbuf_tensor("b1_st", [128, 26], F32))
        w0 = _e(nc.sbuf_tensor("b2_w0", [128, NT * 512], BF16))
        w1 = _e(nc.sbuf_tensor("b2_w1", [128, NT * 512], BF16))
        tp0 = _e(nc.psum_tensor("b1_p0", [128, 128], BF16))
        tp1 = _e(nc.psum_tensor("b1_p1", [128, 128], BF16))
        tp2 = _e(nc.psum_tensor("b1_p2", [128, 128], BF16))
        tp3 = _e(nc.psum_tensor("b1_p3", [128, 128], BF16))
        p0 = _e(nc.psum_tensor("b2_p0", [128, 512], F32))
        p1 = _e(nc.psum_tensor("b2_p1", [128, 512], F32))
        th_stage = _e(nc.sbuf_tensor("b1_thstage", [B, D], F32))
        ptf2 = _e(nc.psum_tensor("b1_ptf2", [128, B], F32))
        blk = _e(nc.Block())
        xts = [xt0, xt1]
        xbs = [xb0, xb1]
        tps = [tp0, tp1, tp2, tp3]
        ws = [w0, w1]
        ps = [p0, p1]
        load_val = {}
        apply_val = {}
        tr_val = {}
        ev_val = {}
        wload = {}
        mmv = {}
        evv = {}

        def wslab_load(fs, whbm):
            @blk.gpsimd
            def _(g, fs=fs):
                if fs >= 2:
                    st.wait(g, mmv[(fs - 2, NT - 1)])
                for c in range(NT):
                    swS[fs % 2].inc(g.dma_start(
                        out=ws[fs % 2][:, c * 512:(c + 1) * 512],
                        in_=whbm[c * 128:(c + 1) * 128,
                                 fs * 512:(fs + 1) * 512]), 16)
            wload[fs] = swS[fs % 2].n

        def emit_load(tt):
            @blk.sync
            def _(sy, tt=tt):
                if tt >= 2:
                    sv.wait(sy, apply_val[tt - 2])
                sdX[tt % 2].inc(sy.dma_start(
                    out=xts[tt % 2][:, :],
                    in_=x_h[128 * tt:128 * (tt + 1), :]), 16)
            load_val[tt] = sdX[tt % 2].n

        def b1_tile(tt):
            @blk.vector
            def _(v, tt=tt):
                sdX[tt % 2].wait(v, load_val[tt])
                if tt >= 2:
                    st.wait(v, tr_val.get((tt - 2, 15), 0))
                sv.wait(v)
                xt = xts[tt % 2]
                for c in range(NF):
                    ist = v.bn_stats(bst[:, 6 * c:6 * c + 6],
                                     xt[:, 512 * c:512 * (c + 1)])
                iv = sv.inc(ist)
                v.wait_ge(sv.h, iv)
                iv = sv.inc(v.bn_aggr(bst[:, 24:26], bst[:, 0:24]))
                v.wait_ge(sv.h, iv)
                v.tensor_copy(mu_cols[:, tt:tt + 1], bst[:, 24:25])
                sv.inc(v.tensor_scalar_add(bst[:, 25:26], bst[:, 25:26], 1e-5))

            @blk.scalar
            def _(s, tt=tt):
                sv.wait(s)
                isx = ss.inc(s.activation(bst[:, 25:26], bst[:, 25:26], ACT.Ln))
                s.wait_ge(ss.h, isx)
                ss.inc(s.activation(rsig_cols[:, tt:tt + 1], bst[:, 25:26],
                                    ACT.Exp, scale=-0.5))

            @blk.vector
            def _(v, tt=tt):
                ss.wait(v)
                v.tensor_scalar(nmr_cols[:, tt:tt + 1], mu_cols[:, tt:tt + 1],
                                rsig_cols[:, tt:tt + 1], -1.0,
                                ALU.mult, ALU.mult)
                sv.inc(v.tensor_scalar(xbs[tt % 2][:, :], xts[tt % 2][:, :],
                                       mu_cols[:, tt:tt + 1],
                                       rsig_cols[:, tt:tt + 1],
                                       ALU.subtract, ALU.mult))
            apply_val[tt] = sv.n
            if tt + 2 < NT:
                emit_load(tt + 2)

            for grp in range(4):
                @blk.tensor
                def _(t, tt=tt, grp=grp):
                    if grp == 0:
                        sv.wait(t, apply_val[tt])
                    for j in range(4):
                        c = grp * 4 + j
                        if (tt, c) >= (0, 4):
                            prev = ev_val.get((tt, c - 4),
                                              ev_val.get((tt - 1, c + 12), 0))
                            ss.wait(t, prev)
                        st.inc(t.transpose(tps[c % 4][:, :],
                                           xbs[tt % 2][:, 128 * c:128 * (c + 1)],
                                           ident[:, :]))
                        tr_val[(tt, c)] = st.n

                @blk.scalar
                def _(s, tt=tt, grp=grp):
                    for j in range(4):
                        c = grp * 4 + j
                        st.wait(s, tr_val[(tt, c)])
                        ss.inc(s.activation(
                            bass.AP(xn_fm, c * D + tt * 128,
                                    [[NT * D, 128], [1, 128]]),
                            tps[c % 4][:, :], ACT.Identity))
                        ev_val[(tt, c)] = ss.n

        def b2_tile(fs, sc):
            @blk.tensor
            def _(t, fs=fs, sc=sc):
                if sc == 0:
                    swS[fs % 2].wait(t, wload[fs])
                ss.wait(t, ev_val[(sc, 15)])
                idx = fs * NT + sc
                if idx >= 2:
                    sv.wait(t, evv[idx - 2])
                for e in range(NT):
                    mi = t.matmul(
                        ps[idx % 2][:, :],
                        bass.AP(xn_fm, e * D + sc * 128,
                                [[NT * D, 128], [1, 128]]),
                        ws[fs % 2][:, e * 512:(e + 1) * 512],
                        start=(e == 0), stop=(e == NT - 1))
                st.inc(mi)
            mmv[(fs, sc)] = st.n

            @blk.vector
            def _(v, fs=fs, sc=sc):
                idx = fs * NT + sc
                st.wait(v, mmv[(fs, sc)])
                sv.inc(v.tensor_copy(
                    q_tm[:, sc * D + fs * 512: sc * D + fs * 512 + 512],
                    ps[idx % 2][:, :]))
            evv[fs * NT + sc] = sv.n

        def fm_init():
            @blk.gpsimd
            def _(g):
                scc.wait(g, agval)
                for r in range(B):
                    swC.inc(g.dma_start(
                        out=th_stage[:, 256 * r:256 * (r + 1)],
                        in_=ag_out[B * r:B * (r + 1), :]), 16)
            reasm = swC.n
            trn = {}
            ev_base = sv.n
            for g4 in range(0, NT, 4):
                @blk.tensor
                def _(t, g4=g4):
                    if g4 == 0:
                        swC.wait(t, reasm)
                    for c in range(g4, min(NT, g4 + 4)):
                        if c > 0:
                            sv.wait(t, ev_base + c)
                        st.inc(t.transpose(ptf2[:, :],
                                           th_stage[:, 128 * c:128 * (c + 1)],
                                           id32s[0:B, 0:B]))
                        trn[c] = st.n

                @blk.vector
                def _(v, g4=g4):
                    for c in range(g4, min(NT, g4 + 4)):
                        st.wait(v, trn[c])
                        v.tensor_copy(thought_fm[:, B * c:B * (c + 1)],
                                      ptf2[:, :]).then_inc(sv.h, 1)
                        sv.n += 1
                        assert sv.n == ev_base + c + 1

        wslab_load(0, wq_h)
        wslab_load(1, wq_h)
        emit_load(0)
        emit_load(1)
        for tt in range(NT):
            b1_tile(tt)
            if tt == 6:
                fm_init()
            if tt >= 1:
                b2_tile(0, tt - 1)
        b2_tile(0, NT - 1)
        wslab_load(2, wq_h)
        for fs in range(1, NF):
            for sc in range(NT):
                b2_tile(fs, sc)
            if fs + 2 < NF:
                wslab_load(fs + 2, wq_h)

    if stop_after == "B2":
        return _dump_and_stop(nc, [("mu_cols", mu_cols), ("rsig_cols", rsig_cols),
                                   ("xn_fm", xn_fm), ("q_tm", q_tm)])

    sdG = [Sem(nc, "sdG0"), Sem(nc, "sdG1")]
    swGB = Sem(nc, "swGB")
    with ExitStack() as _ctx:
        _e = _ctx.enter_context
        w0 = _e(nc.sbuf_tensor("b25_w0", [128, NT * 512], BF16))
        w1 = _e(nc.sbuf_tensor("b25_w1", [128, NT * 512], BF16))
        g0 = _e(nc.sbuf_tensor("b25_g0", [128, 512], BF16))
        gate_b_bf = _e(nc.sbuf_tensor("gate_b_bfS", [1, D], BF16))
        g1 = _e(nc.sbuf_tensor("b25_g1", [128, 512], BF16))
        p0 = _e(nc.psum_tensor("b25_p0", [128, 512], F32))
        p1 = _e(nc.psum_tensor("b25_p1", [128, 512], F32))
        blk = _e(nc.Block())
        ws = [w0, w1]
        gs = [g0, g1]
        ps = [p0, p1]
        wload = {}
        mmv = {}
        actv = {}
        spv = {}

        @blk.gpsimd
        def _(g):
            swGB.inc(g.dma_start(out=gate_b_bf[:, :], in_=gate_b_h[:, :]), 16)

        def wslab_load(fs):
            @blk.gpsimd
            def _(g, fs=fs):
                if fs >= 2:
                    st.wait(g, mmv[(fs - 2, NT - 1)])
                for c in range(NT):
                    swS[fs % 2].inc(g.dma_start(
                        out=ws[fs % 2][:, c * 512:(c + 1) * 512],
                        in_=gatew_h[c * 128:(c + 1) * 128,
                                 fs * 512:(fs + 1) * 512]), 16)
            wload[fs] = swS[fs % 2].n

        wslab_load(0)
        wslab_load(1)
        for fs in range(NF):
            for sc in range(NT):
                idx = fs * NT + sc

                @blk.tensor
                def _(t, fs=fs, sc=sc, idx=idx):
                    if idx == 0:
                        swGB.wait(t)
                    if sc == 0:
                        swS[fs % 2].wait(t, wload[fs])
                    if idx >= 2:
                        ss.wait(t, actv[idx - 2])
                    for e in range(NT):
                        t.matmul(
                            ps[idx % 2][:, :],
                            bass.AP(xn_fm, e * D + sc * 128,
                                    [[NT * D, 128], [1, 128]]),
                            ws[fs % 2][:, e * 512:(e + 1) * 512],
                            start=(e == 0), stop=False)
                    st.inc(t.matmul(ps[idx % 2][:, :], ones1[:, :],
                                    gate_b_bf[:, fs * 512:(fs + 1) * 512],
                                    start=False, stop=True))
                mmv[(fs, sc)] = st.n

                @blk.scalar
                def _(s, fs=fs, sc=sc, idx=idx):
                    st.wait(s, mmv[(fs, sc)])
                    if idx >= 2:
                        sdG[idx % 2].wait(s, spv[idx - 2])
                    ss.inc(s.activation(gs[idx % 2][:, :], ps[idx % 2][:, :],
                                        ACT.Sigmoid))
                actv[idx] = ss.n

                @blk.sync
                def _(sy, fs=fs, sc=sc, idx=idx):
                    ss.wait(sy, actv[idx])
                    sdG[idx % 2].inc(sy.dma_start(
                        out=gate_sp[sc * 128:(sc + 1) * 128,
                                    fs * 512:(fs + 1) * 512],
                        in_=gs[idx % 2][:, :]), 16)
                spv[idx] = sdG[idx % 2].n
            if fs + 2 < NF:
                wslab_load(fs + 2)

        @blk.sync
        def _(sy):
            sdG[0].wait(sy)
            sdG[1].wait(sy)

    if stop_after == "B25":
        return _dump_and_stop(nc, [], [("gate_sp", gate_sp)])

    sdW = [Sem(nc, "sdW0"), Sem(nc, "sdW1")]
    with ExitStack() as _ctx:
        _e = _ctx.enter_context
        w0 = _e(nc.sbuf_tensor("b3_w0", [128, NT * 512], BF16))
        w1 = _e(nc.sbuf_tensor("b3_w1", [128, NT * 512], BF16))
        kslab = _e(nc.sbuf_tensor("b3_k", [128, NT * 512], BF16))
        wt0 = _e(nc.sbuf_tensor("b3_wt0", [128, D], BF16))
        wt1 = _e(nc.sbuf_tensor("b3_wt1", [128, D], BF16))
        l_cols = _e(nc.sbuf_tensor("b3_l", [128, NT], F32))
        pk0 = _e(nc.psum_tensor("b3_pk0", [128, 512], F32))
        pk1 = _e(nc.psum_tensor("b3_pk1", [128, 512], F32))
        ps0 = _e(nc.psum_tensor("b3_ps0", [128, 512], F32))
        ps1 = _e(nc.psum_tensor("b3_ps1", [128, 512], F32))
        pl = _e(nc.psum_tensor("b3_pl", [128, NT], F32))
        blk = _e(nc.Block())
        ws = [w0, w1]
        pks = [pk0, pk1]
        pss = [ps0, ps1]
        wts = [wt0, wt1]
        wload = {}
        kmmv = {}
        kevv = {}
        smmv = {}
        expv = {}
        lmmv = {}
        spv = {}

        def wslab_load(fs):
            @blk.gpsimd
            def _(g, fs=fs):
                if fs >= 2:
                    st.wait(g, kmmv[(fs - 2, NT - 1)])
                for c in range(NT):
                    swS[fs % 2].inc(g.dma_start(
                        out=ws[fs % 2][:, c * 512:(c + 1) * 512],
                        in_=wk_h[c * 128:(c + 1) * 128,
                                 fs * 512:(fs + 1) * 512]), 16)
            wload[fs] = swS[fs % 2].n

        wslab_load(0)
        wslab_load(1)
        for fs in range(NF):
            for sc in range(NT):
                @blk.tensor
                def _(t, fs=fs, sc=sc):
                    if sc == 0:
                        swS[fs % 2].wait(t, wload[fs])
                    idx = fs * NT + sc
                    if idx >= 2:
                        sv.wait(t, kevv[idx - 2])
                    for e in range(NT):
                        mi = t.matmul(
                            pks[idx % 2][:, :],
                            bass.AP(xn_fm, e * D + sc * 128,
                                    [[NT * D, 128], [1, 128]]),
                            ws[fs % 2][:, e * 512:(e + 1) * 512],
                            start=(e == 0), stop=(e == NT - 1))
                    st.inc(mi)
                kmmv[(fs, sc)] = st.n

                @blk.vector
                def _(v, fs=fs, sc=sc):
                    idx = fs * NT + sc
                    st.wait(v, kmmv[(fs, sc)])
                    sv.inc(v.tensor_copy(kslab[:, sc * 512:(sc + 1) * 512],
                                         pks[idx % 2][:, :]))
                kevv[fs * NT + sc] = sv.n
            kslab_ready = sv.n
            if fs + 2 < NF:
                wslab_load(fs + 2)

            for fb in range(4):
                gfb = fs * 4 + fb
                for eN in range(NF):
                    gidx = gfb * NF + eN

                    @blk.tensor
                    def _(t, fs=fs, fb=fb, eN=eN, gidx=gidx,
                          kready=kslab_ready):
                        if fb == 0 and eN == 0:
                            sv.wait(t, kready)
                        if gidx >= 2:
                            ss.wait(t, expv.get(gidx - 2, 0))
                        for sc in range(NT):
                            mi = t.matmul(
                                pss[gidx % 2][:, :],
                                kslab[:, sc * 512 + fb * 128:
                                      sc * 512 + fb * 128 + 128],
                                q_tm[:, sc * D + eN * 512:
                                     sc * D + eN * 512 + 512],
                                start=(sc == 0), stop=(sc == NT - 1))
                        st.inc(mi)
                    smmv[gidx] = st.n

                    @blk.scalar
                    def _(s, gfb=gfb, eN=eN, gidx=gidx):
                        st.wait(s, smmv[gidx])
                        if gfb >= 2:
                            st.wait(s, lmmv.get((gfb - 2) * NF + eN, 0))
                            sdW[gfb % 2].wait(s, spv.get(gfb - 2, 0))
                        ss.inc(s.activation(
                            wts[gfb % 2][:, eN * 512:(eN + 1) * 512],
                            pss[gidx % 2][:, :], ACT.Exp, scale=0.125))
                    expv[gidx] = ss.n

                    @blk.tensor
                    def _(t, gfb=gfb, eN=eN, gidx=gidx):
                        ss.wait(t, expv[gidx])
                        for cc in range(4):
                            ec = eN * 4 + cc
                            mi = t.matmul(
                                pl[:, ec:ec + 1],
                                wts[gfb % 2][:, ec * 128:(ec + 1) * 128],
                                ones_col[:, :],
                                start=(gfb == 0 and ec == 0),
                                stop=(gfb == NT - 1 and ec == NT - 1))
                        st.inc(mi)
                    lmmv[gidx] = st.n

                @blk.sync
                def _(sy, gfb=gfb):
                    ss.wait(sy, expv[gfb * NF + NF - 1])
                    sdW[gfb % 2].inc(sy.dma_start(out=wT_sp[:, gfb, :],
                                                  in_=wts[gfb % 2][:, :]), 16)
                spv[gfb] = sdW[gfb % 2].n

        @blk.vector
        def _(v):
            st.wait(v)
            iv = sv.inc(v.tensor_copy(l_cols[:, :], pl[:, :]))
            v.wait_ge(sv.h, iv)
            sv.inc(v.reciprocal(r_cols[:, :], l_cols[:, :]))

        @blk.sync
        def _(sy):
            sdW[0].wait(sy)
            sdW[1].wait(sy)

    q_ctx.__exit__(None, None, None)
    if stop_after == "B3":
        return _dump_and_stop(nc, [("r_cols", r_cols)], [("wT_sp", wT_sp)])

    pTa = pT_ctx.enter_context(nc.psum_tensor("pTa", [128, 512], F32))
    pTb = pT_ctx.enter_context(nc.psum_tensor("pTb", [128, 512], F32))
    bc_tm = nc.alloc_sbuf_tensor("bc_tm", [B, D], F32, side="right")
    bc_b8s = nc.alloc_sbuf_tensor("bc_b8s", [B, D], F32, side="right")
    wckb = wckb_ctx.enter_context(
        nc.sbuf_tensor("wckb", [128, 16384], BF16, side="right"))
    swV = [Sem(nc, "swV0"), Sem(nc, "swV1")]
    v_ctx = nc.sbuf_tensor("v_fm", [128, NT * D], BF16, side="right")
    v_fm = v_ctx.__enter__()
    with ExitStack() as _ctx:
        _e = _ctx.enter_context
        w0 = _e(nc.sbuf_tensor("b35_w0", [128, NT * 128], BF16))
        w1 = _e(nc.sbuf_tensor("b35_w1", [128, NT * 128], BF16))
        p0 = _e(nc.psum_tensor("b35_p0", [128, 512], F32))
        p1 = _e(nc.psum_tensor("b35_p1", [128, 512], F32))
        blk = _e(nc.Block())
        ws = [w0, w1]
        ps = [p0, p1]
        wload = {}
        mmv = {}
        evv = {}

        def wcol_load(fb):
            @blk.gpsimd
            def _(g, fb=fb):
                if fb >= 2:
                    st.wait(g, mmv[(fb - 2, NF - 1)])
                for c in range(NT):
                    swV[fb % 2].inc(g.dma_start(
                        out=ws[fb % 2][:, c * 128:(c + 1) * 128],
                        in_=wv_h[fb, c * 128:(c + 1) * 128, :]), 16)
            wload[fb] = swV[fb % 2].n

        @blk.sync
        def _(sy):
            sdT.inc(sy.dma_start(out=bc_b8s[:, :], in_=bc_b8_h[:, :]), 16)

        wcol_load(0)
        wcol_load(1)
        for fb in range(NT):
            for sN in range(NF):
                @blk.tensor
                def _(t, fb=fb, sN=sN):
                    if sN == 0:
                        swV[fb % 2].wait(t, wload[fb])
                    idx = fb * NF + sN
                    if idx >= 2:
                        sv.wait(t, evv[idx - 2])
                    for e in range(NT):
                        mi = t.matmul(
                            ps[idx % 2][:, :],
                            ws[fb % 2][:, e * 128:(e + 1) * 128],
                            bass.AP(xn_fm, e * D + sN * 512,
                                    [[NT * D, 128], [1, 512]]),
                            start=(e == 0), stop=(e == NT - 1))
                    st.inc(mi)
                mmv[(fb, sN)] = st.n

                @blk.vector
                def _(v, fb=fb, sN=sN):
                    idx = fb * NF + sN
                    st.wait(v, mmv[(fb, sN)])
                    sv.inc(v.tensor_copy(
                        v_fm[:, fb * D + sN * 512: fb * D + sN * 512 + 512],
                        ps[idx % 2][:, :]))
                evv[fb * NF + sN] = sv.n
            if fb + 2 < NT:
                wcol_load(fb + 2)
            fm_seam(blk)

    xn_ctx.__exit__(None, None, None)
    if stop_after == "B35":
        return _dump_and_stop(nc, [("v_fm", v_fm)])

    swO = Sem(nc, "swO")
    sdC = [Sem(nc, "sdC0"), Sem(nc, "sdC1")]
    sdA = [Sem(nc, "sdA0"), Sem(nc, "sdA1")]
    wo_ctx = nc.sbuf_tensor("wo_bf", [128, NT * D], BF16)
    wo_bf = wo_ctx.__enter__()
    with ExitStack() as _ctx:
        _e = _ctx.enter_context
        c0 = _e(nc.sbuf_tensor("b4_c0", [128, NT * 128], BF16))
        c1 = _e(nc.sbuf_tensor("b4_c1", [128, NT * 128], BF16))
        a0 = _e(nc.sbuf_tensor("b4_a0", [128, D], BF16))
        a1 = _e(nc.sbuf_tensor("b4_a1", [128, D], BF16))
        pp = [_e(nc.psum_tensor("b4_p%d" % i, [128, 1024], F32))
              for i in range(3)]
        blk = _e(nc.Block())
        cs = [c0, c1]
        asb = [a0, a1]
        cload = {}
        mmv = {}
        evv = {}
        spv = {}

        @blk.gpsimd
        def _(g):
            for c in range(NT):
                swO.inc(g.dma_start(
                    out=wo_bf[:, c * D:(c + 1) * D],
                    in_=wo_h[c * 128:(c + 1) * 128, :]), 16)
        wo_loaded = swO.n

        def col_load(ec):
            @blk.sync
            def _(sy, ec=ec):
                if ec >= 2:
                    st.wait(sy, hmm[2 * (ec - 2) + 1])
                sdC[ec % 2].inc(sy.dma_start(
                    out=cs[ec % 2][:, :].rearrange("p (c f) -> p c f", f=128),
                    in_=wT_sp[:, :, ec * 128:(ec + 1) * 128]), 16)
            cload[ec] = sdC[ec % 2].n

        col_load(0)
        col_load(1)
        hmm = {}
        hev = {}
        for ec in range(NT):
            for half in range(2):
                hid = 2 * ec + half

                @blk.tensor
                def _(t, ec=ec, half=half, hid=hid):
                    if half == 0:
                        sdC[ec % 2].wait(t, cload[ec])
                    if hid >= 3:
                        sv.wait(t, hev[hid - 3])
                    pr = pp[hid % 3]
                    for s2 in range(2):
                        sN = half * 2 + s2
                        for fb in range(NT):
                            mi = t.matmul(
                                pr[:, s2 * 512:(s2 + 1) * 512],
                                cs[ec % 2][:, fb * 128:(fb + 1) * 128],
                                v_fm[:, fb * D + sN * 512:
                                     fb * D + sN * 512 + 512],
                                start=(fb == 0), stop=(fb == NT - 1))
                        st.inc(mi)
                hmm[hid] = st.n

                @blk.vector
                def _(v, ec=ec, half=half, hid=hid):
                    st.wait(v, hmm[hid])
                    if half == 0 and ec >= 2:
                        sdA[ec % 2].wait(v, spv[ec - 2])
                    sv.inc(v.tensor_scalar_mul(
                        asb[ec % 2][:, half * 1024:(half + 1) * 1024],
                        pp[hid % 3][:, :], r_cols[:, ec:ec + 1]))
                hev[hid] = sv.n

            @blk.sync
            def _(sy, ec=ec):
                sv.wait(sy, hev[2 * ec + 1])
                sdA[ec % 2].inc(sy.dma_start(out=attn_sp[:, ec, :],
                                             in_=asb[ec % 2][:, :]), 16)
            spv[ec] = sdA[ec % 2].n
            if ec + 2 < NT:
                col_load(ec + 2)
            fm_seam(blk)

        @blk.sync
        def _(sy):
            sdA[0].wait(sy)
            sdA[1].wait(sy)

    v_ctx.__exit__(None, None, None)
    wckb_ctx.close()
    pT_ctx.close()
    if stop_after == "B4":
        return _dump_and_stop(nc, [], [("attn_sp", attn_sp)])

    sdL = [Sem(nc, "sdL0"), Sem(nc, "sdL1")]
    sdO = [Sem(nc, "sdO0"), Sem(nc, "sdO1")]
    sdOG = Sem(nc, "sdOG")
    with ExitStack() as _ctx:
        _e = _ctx.enter_context
        og = _e(nc.sbuf_tensor("b6_og", [128, D], F32))
        ob = _e(nc.sbuf_tensor("b6_ob", [128, D], F32))
        x0 = _e(nc.sbuf_tensor("b6_x0", [128, D], F32))
        x1 = _e(nc.sbuf_tensor("b6_x1", [128, D], F32))
        g0 = _e(nc.sbuf_tensor("b6_g0", [128, D], BF16))
        g1 = _e(nc.sbuf_tensor("b6_g1", [128, D], BF16))
        ac0 = _e(nc.sbuf_tensor("b6_ac0", [128, NT * 128], BF16))
        ac1 = _e(nc.sbuf_tensor("b6_ac1", [128, NT * 128], BF16))
        wk1a = _e(nc.sbuf_tensor("b6_w1a", [128, D], F32))
        wk1b = _e(nc.sbuf_tensor("b6_w1b", [128, D], F32))
        wk2a = _e(nc.sbuf_tensor("b6_w2a", [128, D], F32))
        wk2b = _e(nc.sbuf_tensor("b6_w2b", [128, D], F32))
        o0 = _e(nc.sbuf_tensor("b6_o0", [128, D], F32))
        o1 = _e(nc.sbuf_tensor("b6_o1", [128, D], F32))
        bs0 = _e(nc.sbuf_tensor("b6_st0", [128, 28], F32))
        bs1 = _e(nc.sbuf_tensor("b6_st1", [128, 28], F32))
        pA = _e(nc.psum_tensor("b6_pA", [128, D], F32))
        pB = _e(nc.psum_tensor("b6_pB", [128, D], F32))
        blk = _e(nc.Block())
        xs = [x0, x1]
        gs = [g0, g1]
        acs = [ac0, ac1]
        os_ = [o0, o1]
        wk1s = [wk1a, wk1b]
        wk2s = [wk2a, wk2b]
        bss = [bs0, bs1]
        pp = [pA, pB]
        lval = {}
        mmv = {}
        combv = {}
        outv = {}
        g1v = {}
        g2v = {}
        s4v = {}
        v1v = {}

        @blk.sync
        def _(sy):
            sdOG.inc(sy.dma_start(out=og[:, :], in_=og128_h[:, :]), 16)
            sdOG.inc(sy.dma_start(out=ob[:, :], in_=ob128_h[:, :]), 16)

        @blk.tensor
        def _(t):
            for n in range(NF):
                st.inc(t.matmul(pA[:, 512 * n:512 * (n + 1)], sel[:, :],
                                bc_tm[:, 512 * n:512 * (n + 1)],
                                start=True, stop=True))
        bcmm = st.n

        @blk.vector
        def _(v):
            st.wait(v, bcmm)
            sv.inc(v.tensor_copy(bc_rep[:, :], pA[:, :]))
        bcev = sv.n

        def loads(sc):
            @blk.sync
            def _(sy, sc=sc):
                if sc >= 2:
                    sv.wait(sy, combv[sc - 2])
                sdL[sc % 2].inc(sy.dma_start(
                    out=xs[sc % 2][:, :],
                    in_=x_h[sc * 128:(sc + 1) * 128, :]), 16)
                sdL[sc % 2].inc(sy.dma_start(
                    out=gs[sc % 2][:, :],
                    in_=gate_sp[sc * 128:(sc + 1) * 128, :]), 16)
                sdL[sc % 2].inc(sy.dma_start(
                    out=acs[sc % 2][:, :].rearrange("p (c f) -> p c f", f=128),
                    in_=attn_sp[:, :, sc * 128:(sc + 1) * 128]), 16)
            lval[sc] = sdL[sc % 2].n

        loads(0)
        loads(1)
        for sc in range(NT):
            p = sc % 2
            wk1 = wk1s[p]
            wk2 = wk2s[p]
            bst = bss[p]

            @blk.tensor
            def _(t, sc=sc, p=p):
                if sc == 0:
                    swO.wait(t, wo_loaded)
                    sv.wait(t, bcev)
                sdL[p].wait(t, lval[sc])
                if sc >= 2:
                    sv.wait(t, combv[sc - 2])
                for gN in range(NF):
                    for ec in range(NT):
                        mi = t.matmul(
                            pp[p][:, gN * 512:(gN + 1) * 512],
                            acs[p][:, ec * 128:(ec + 1) * 128],
                            wo_bf[:, ec * D + gN * 512: ec * D + gN * 512 + 512],
                            start=(ec == 0), stop=(ec == NT - 1))
                    st.inc(mi)
            mmv[sc] = st.n

            @blk.scalar
            def _(s, sc=sc, p=p, wk1=wk1, wk2=wk2):
                sdL[p].wait(s, lval[sc])
                if sc >= 2:
                    sg.wait(s, g1v[sc - 2])
                    sv.wait(s, combv[sc - 2])
                i1 = ss.inc(s.copy(wk2[:, :], gs[p][:, :]))
                ss.inc(s.activation(wk1[:, :], xs[p][:, :], ACT.Identity,
                                    bias=nmr_cols[:, sc:sc + 1],
                                    scale=rsig_cols[:, sc:sc + 1]))
            s12 = ss.n

            @blk.gpsimd
            def _(g, sc=sc, p=p, wk2=wk2, s1=s12 - 1):
                ss.wait(g, s1)
                if sc < 2:
                    sv.wait(g, bcev)
                else:
                    sv.wait(g, v1v[sc - 2])
                sg.inc(g.tensor_mul(wk2[:, :], wk2[:, :], bc_rep[:, :]))
            g1v[sc] = sg.n

            @blk.vector
            def _(v, sc=sc, p=p, wk1=wk1, wk2=wk2, bst=bst):
                if sc == 0:
                    sdOG.wait(v)
                ss.wait(v, s12)
                sg.wait(v, g1v[sc])
                if sc >= 2:
                    ss.wait(v, s4v[sc - 2])
                iv = sv.inc(v.tensor_add(wk1[:, :], wk1[:, :], wk2[:, :]))
                v.wait_ge(sv.h, iv)
                for c in range(NF):
                    ist = v.bn_stats(bst[:, 6 * c:6 * c + 6],
                                     wk1[:, 512 * c:512 * (c + 1)])
                iv = sv.inc(ist)
                v.wait_ge(sv.h, iv)
                iv = sv.inc(v.bn_aggr(bst[:, 24:26], bst[:, 0:24]))
                v.wait_ge(sv.h, iv)
                sv.inc(v.tensor_scalar_add(bst[:, 25:26], bst[:, 25:26], 1e-6))
            v1v[sc] = sv.n

            @blk.scalar
            def _(s, sc=sc, bst=bst):
                sv.wait(s, v1v[sc])
                isx = ss.inc(s.activation(bst[:, 25:26], bst[:, 25:26], ACT.Ln))
                s.wait_ge(ss.h, isx)
                ss.inc(s.activation(bst[:, 26:27], bst[:, 25:26], ACT.Exp,
                                    scale=-0.5))
            s3v = ss.n

            @blk.vector
            def _(v, sc=sc, bst=bst, s3=s3v):
                ss.wait(v, s3)
                sv.inc(v.tensor_scalar(bst[:, 27:28], bst[:, 24:25],
                                       bst[:, 26:27], -1.0,
                                       ALU.mult, ALU.mult))
            v3v = sv.n

            @blk.scalar
            def _(s, sc=sc, wk1=wk1, bst=bst, v3=v3v):
                sv.wait(s, v3)
                ss.inc(s.activation(wk1[:, :], wk1[:, :], ACT.Identity,
                                    bias=bst[:, 27:28], scale=bst[:, 26:27]))
            s4v[sc] = ss.n

            @blk.gpsimd
            def _(g, sc=sc, wk1=wk1):
                ss.wait(g, s4v[sc])
                ig = sg.inc(g.tensor_mul(wk1[:, :], wk1[:, :], og[:, :]))
                g.wait_ge(sg.h, ig)
                sg.inc(g.tensor_add(wk1[:, :], wk1[:, :], ob[:, :]))
            g2v[sc] = sg.n

            @blk.vector
            def _(v, sc=sc, p=p, wk1=wk1):
                st.wait(v, mmv[sc])
                if sc >= 2:
                    sdO[p].wait(v, outv[sc - 2])
                iv = sv.inc(v.tensor_add(os_[p][:, :], xs[p][:, :],
                                         pp[p][:, :]))
                sg.wait(v, g2v[sc])
                v.wait_ge(sv.h, iv)
                sv.inc(v.tensor_add(os_[p][:, :], os_[p][:, :], wk1[:, :]))
            combv[sc] = sv.n

            @blk.sync
            def _(sy, sc=sc, p=p):
                sv.wait(sy, combv[sc])
                sdO[p].inc(sy.dma_start(
                    out=out_h[sc * 128:(sc + 1) * 128, :],
                    in_=os_[p][:, :]), 16)
            outv[sc] = sdO[p].n
            if sc + 2 < NT:
                loads(sc + 2)

        @blk.sync
        def _(sy):
            sdO[0].wait(sy)
            sdO[1].wait(sy)

    wo_ctx.__exit__(None, None, None)
    return nc


_CACHE = {}


def kernel(L0_attn_x, L0_attn_Wq, L0_attn_Wk, L0_attn_Wv, L0_attn_Wo,
           L0_ffn_in_gamma, L0_ffn_in_beta, L0_ffn_agg_w, L0_ffn_agg_b,
           L0_ffn_pn_gamma, L0_ffn_pn_beta, L0_ffn_d1_w, L0_ffn_d1_b,
           L0_ffn_d2_w, L0_ffn_d2_b, L0_ffn_bc_w, L0_ffn_bc_b,
           L0_ffn_gate_w, L0_ffn_gate_b, L0_ffn_out_gamma, L0_ffn_out_beta):
    f32 = np.float32
    X = np.ascontiguousarray(np.asarray(L0_attn_x, dtype=f32))
    wq = np.ascontiguousarray(np.asarray(L0_attn_Wq, dtype=f32))
    wk = np.ascontiguousarray(np.asarray(L0_attn_Wk, dtype=f32))
    wv = np.ascontiguousarray(np.asarray(L0_attn_Wv, dtype=f32))
    wo = np.ascontiguousarray(np.asarray(L0_attn_Wo, dtype=f32))
    gatew = np.ascontiguousarray(np.asarray(L0_ffn_gate_w, dtype=f32))
    aggw = np.ascontiguousarray(np.asarray(L0_ffn_agg_w, dtype=f32))
    bcw = np.ascontiguousarray(np.asarray(L0_ffn_bc_w, dtype=f32))
    d1w = np.ascontiguousarray(np.asarray(L0_ffn_d1_w, dtype=f32))
    d2w = np.ascontiguousarray(np.asarray(L0_ffn_d2_w, dtype=f32))

    wv_blk = np.ascontiguousarray(wv.reshape(D, NT, 128).transpose(1, 0, 2))
    lastrows = np.ascontiguousarray(X[:, D - 1, :])

    def rep8(vec, w=D):
        return np.ascontiguousarray(
            np.broadcast_to(np.asarray(vec, dtype=f32).reshape(1, w), (B, w)))

    in_g8 = rep8(L0_ffn_in_gamma)
    in_b8 = rep8(L0_ffn_in_beta)
    pn_g_fm = np.ascontiguousarray(
        np.asarray(L0_ffn_pn_gamma, dtype=f32).reshape(NT, 128).T)
    pn_b_fm = np.ascontiguousarray(
        np.asarray(L0_ffn_pn_beta, dtype=f32).reshape(NT, 128).T)
    agg_b8 = rep8(L0_ffn_agg_b)
    d2b_fm = np.ascontiguousarray(
        np.asarray(L0_ffn_d2_b, dtype=f32).reshape(NT, 128).T)
    bc_b8 = rep8(L0_ffn_bc_b)
    og128 = np.ascontiguousarray(np.broadcast_to(
        np.asarray(L0_ffn_out_gamma, dtype=f32).reshape(1, D), (128, D)))
    ob128 = np.ascontiguousarray(np.broadcast_to(
        np.asarray(L0_ffn_out_beta, dtype=f32).reshape(1, D), (128, D)))
    gate_b_row = np.ascontiguousarray(
        np.asarray(L0_ffn_gate_b, dtype=f32).reshape(1, D))
    d1b = np.asarray(L0_ffn_d1_b, dtype=f32)

    ident = np.eye(128, dtype=BFD)
    ident32 = np.eye(128, dtype=f32)
    ones_col = np.ones((128, 1), dtype=BFD)
    ones_col32 = np.ones((128, 1), dtype=f32)
    ones1 = np.ones((1, 128), dtype=BFD)
    ones1_32 = np.ones((1, 128), dtype=f32)

    if "nc" not in _CACHE:
        _CACHE["nc"] = build_nc()
    nc = _CACHE["nc"]

    in_maps = []
    for b in range(B):
        selb = np.zeros((B, 128), dtype=f32)
        selb[b, :] = 1.0
        in_maps.append({
            "x": X[b],
            "wq": wq, "wk": wk, "wo": wo, "gatew": gatew,
            "aggw_sl": np.ascontiguousarray(aggw[:, b * 256:(b + 1) * 256]),
            "bcw_sl": np.ascontiguousarray(bcw[:, b * 256:(b + 1) * 256]),
            "wv_blk": wv_blk,
            "d1s_pm": np.ascontiguousarray(
                d1w[:, b * HSL:(b + 1) * HSL].reshape(NT, 128, NHC, 128)),
            "d2s_pm": np.ascontiguousarray(
                d2w[b * HSL:(b + 1) * HSL, :].reshape(NHC, 128, NT, 128)),
            "lastrows": lastrows,
            "in_g8": in_g8, "in_b8": in_b8,
            "pn_g_fm": pn_g_fm, "pn_b_fm": pn_b_fm,
            "agg_bsl": np.ascontiguousarray(
                np.broadcast_to(agg_b8[0:1, b * 256:(b + 1) * 256], (B, 256))),
            "d1b_fm": np.ascontiguousarray(
                d1b[b * HSL:(b + 1) * HSL].reshape(NHC, 128).T),
            "d2b_fm": d2b_fm, "bc_b8": bc_b8,
            "og128": og128, "ob128": ob128, "gate_b_row": gate_b_row,
            "sel_rep": selb, "ident": ident, "ident32": ident32,
            "ones_col": ones_col, "ones_col32": ones_col32,
            "ones1": ones1, "ones1_32": ones1_32,
        })

    res = run_bass_kernel_spmd(nc, in_maps, core_ids=list(range(B)))
    out = np.stack([res.results[b]["out"] for b in range(B)], axis=0)
    return out


# version 10
# speedup vs baseline: 1.0858x; 1.0858x over previous
from contextlib import ExitStack

import numpy as np
import ml_dtypes
import concourse.bass as bass
import concourse.mybir as mybir
from concourse.bass_utils import run_bass_kernel_spmd

F32 = mybir.dt.float32
BF16 = mybir.dt.bfloat16
ACT = mybir.ActivationFunctionType
ALU = mybir.AluOpType
AX = mybir.AxisListType

D = 2048
NT = 16
NF = 4
B = 8
HSL = 1024
NHC = HSL // 128
KSTEPS = 4

BFD = ml_dtypes.bfloat16


class Sem:

    def __init__(self, nc, name):
        self.h = nc.alloc_semaphore(name)
        self.n = 0

    def inc(self, inst, k=1):
        inst.then_inc(self.h, k)
        self.n += k
        return self.n

    def wait(self, eng, val=None):
        eng.wait_ge(self.h, self.n if val is None else val)



def _dump_and_stop(nc, sbuf_list, dram_list=()):
    sdd = Sem(nc, "sdd")
    with nc.Block() as blk:
        @blk.sync
        def _(sy):
            for name, t in sbuf_list:
                hb = nc.dram_tensor("dbg_" + name, list(t.shape), t.dtype,
                                    kind="ExternalOutput")
                sdd.inc(sy.dma_start(out=hb[:, :], in_=t[:, :]), 16)
            for name, t in dram_list:
                hb = nc.dram_tensor("dbg_" + name, list(t.shape), t.dtype,
                                    kind="ExternalOutput")
                ap_in = t.ap() if hasattr(t, "ap") else t
                sdd.inc(sy.dma_start(out=hb.ap(), in_=t.ap()), 16)
            sdd.wait(sy)
    return nc


def build_nc(stop_after=None):
    nc = bass.Bass("TRN2", target_bir_lowering=False)

    x_h = nc.dram_tensor("x", [D, D], F32, kind="ExternalInput")
    wq_h = nc.dram_tensor("wq", [D, D], F32, kind="ExternalInput")
    wk_h = nc.dram_tensor("wk", [D, D], F32, kind="ExternalInput")
    wo_h = nc.dram_tensor("wo", [D, D], F32, kind="ExternalInput")
    gatew_h = nc.dram_tensor("gatew", [D, D], F32, kind="ExternalInput")
    aggw_h = nc.dram_tensor("aggw_sl", [D, 256], F32, kind="ExternalInput")
    bcw_h = nc.dram_tensor("bcw_sl", [D, 256], F32, kind="ExternalInput")
    wv_h = nc.dram_tensor("wv_blk", [NT, D, 128], F32, kind="ExternalInput")
    d1s_h = nc.dram_tensor("d1s_pm", [NT, 128, NHC, 128], F32,
                           kind="ExternalInput")
    d2s_h = nc.dram_tensor("d2s_pm", [NHC, 128, NT, 128], F32,
                           kind="ExternalInput")
    lastrows_h = nc.dram_tensor("lastrows", [B, D], F32, kind="ExternalInput")
    in_g8_h = nc.dram_tensor("in_g8", [B, D], F32, kind="ExternalInput")
    in_b8_h = nc.dram_tensor("in_b8", [B, D], F32, kind="ExternalInput")
    pn_g_fm_h = nc.dram_tensor("pn_g_fm", [128, NT], F32, kind="ExternalInput")
    pn_b_fm_h = nc.dram_tensor("pn_b_fm", [128, NT], F32, kind="ExternalInput")
    agg_bsl_h = nc.dram_tensor("agg_bsl", [B, 256], F32, kind="ExternalInput")
    d1b_fm_h = nc.dram_tensor("d1b_fm", [128, NHC], F32, kind="ExternalInput")
    d2b_fm_h = nc.dram_tensor("d2b_fm", [128, NT], F32, kind="ExternalInput")
    bc_b8_h = nc.dram_tensor("bc_b8", [B, D], F32, kind="ExternalInput")
    og128_h = nc.dram_tensor("og128", [128, D], F32, kind="ExternalInput")
    ob128_h = nc.dram_tensor("ob128", [128, D], F32, kind="ExternalInput")
    gate_b_h = nc.dram_tensor("gate_b_row", [1, D], F32, kind="ExternalInput")
    sel_h = nc.dram_tensor("sel_rep", [B, 128], F32, kind="ExternalInput")
    ident_h = nc.dram_tensor("ident", [128, 128], BF16, kind="ExternalInput")
    ident32_h = nc.dram_tensor("ident32", [128, 128], F32, kind="ExternalInput")
    onescol32_h = nc.dram_tensor("ones_col32", [128, 1], F32,
                                 kind="ExternalInput")
    ones1_32_h = nc.dram_tensor("ones1_32", [1, 128], F32, kind="ExternalInput")
    onescol_h = nc.dram_tensor("ones_col", [128, 1], BF16, kind="ExternalInput")
    ones1_h = nc.dram_tensor("ones1", [1, 128], BF16, kind="ExternalInput")

    out_h = nc.dram_tensor("out", [D, D], F32, kind="ExternalOutput")

    gate_sp = nc.dram_tensor("gate_sp", [D, D], BF16)
    wT_sp = nc.dram_tensor("wT_sp", [128, NT, D], BF16)
    attn_sp = nc.dram_tensor("attn_sp", [128, NT, D], BF16)
    cc_in = nc.dram_tensor("cc_in", [B, D], F32)
    cc_out = nc.dram_tensor("cc_out", [B, D], F32)
    ag_in = nc.dram_tensor("ag_in", [B, 256], F32)
    ag_out = nc.dram_tensor("ag_out", [B * B, 256], F32)
    ccf_in = nc.dram_tensor("ccf_in", [128, 128], F32)
    ccf_out = nc.dram_tensor("ccf_out", [128, 128], F32)

    ident = nc.alloc_sbuf_tensor("identS", [128, 128], BF16)
    ones_col = nc.alloc_sbuf_tensor("ones_colS", [128, 1], BF16)
    ones1 = nc.alloc_sbuf_tensor("ones1S", [1, 128], BF16)
    sel = nc.alloc_sbuf_tensor("selS", [B, 128], F32)
    mu_cols = nc.alloc_sbuf_tensor("mu_cols", [128, NT], F32)
    rsig_cols = nc.alloc_sbuf_tensor("rsig_cols", [128, NT], F32)
    r_cols = nc.alloc_sbuf_tensor("r_cols", [128, NT], F32)
    nmr_cols = nc.alloc_sbuf_tensor("nmr_cols", [128, NT], F32)
    bc_rep = nc.alloc_sbuf_tensor("bc_rep", [128, D], F32)

    sv = Sem(nc, "sv")
    st = Sem(nc, "st")
    ss = Sem(nc, "ss")
    sg = Sem(nc, "sg")
    scc = Sem(nc, "scc")

    sdT = Sem(nc, "sdT")
    swW = [Sem(nc, "swW0"), Sem(nc, "swW1")]
    swC = Sem(nc, "swC")
    swB = Sem(nc, "swB")

    thought_fm = nc.alloc_sbuf_tensor("thought_fm", [128, NT * B], F32)
    sq_fm = nc.alloc_sbuf_tensor("sq_fm", [128, NT * B], F32)
    id32s = nc.alloc_sbuf_tensor("id32s", [128, 128], F32)
    t1_bf = nc.alloc_sbuf_tensor("t1_bf", [128, NT * B], BF16)
    t2_bf = nc.alloc_sbuf_tensor("t2_bf", [128, NHC * B], BF16)
    th_bf = nc.alloc_sbuf_tensor("th_bf", [128, NT * B], BF16)
    mu8r = nc.alloc_sbuf_tensor("mu8r", [128, B], F32)
    rs8r = nc.alloc_sbuf_tensor("rs8r", [128, B], F32)
    strow = nc.alloc_sbuf_tensor("strow", [1, 32], F32)
    ccfm = nc.alloc_sbuf_tensor("ccfm", [128, 128], F32)
    pngf = nc.alloc_sbuf_tensor("pngf", [128, NT], F32)
    pnbf = nc.alloc_sbuf_tensor("pnbf", [128, NT], F32)
    d1bf = nc.alloc_sbuf_tensor("d1bf", [128, NHC], F32)
    d2bf = nc.alloc_sbuf_tensor("d2bf", [128, NT], F32)
    onc32 = nc.alloc_sbuf_tensor("onc32", [128, 1], F32)
    on1_32 = nc.alloc_sbuf_tensor("on1_32", [1, 128], F32)

    with ExitStack() as _ctx:
        _e = _ctx.enter_context
        t_last = _e(nc.sbuf_tensor("t_last", [B, D], F32))
        t_ing = _e(nc.sbuf_tensor("t_ing", [B, D], F32))
        t_inb = _e(nc.sbuf_tensor("t_inb", [B, D], F32))
        t_aggb = _e(nc.sbuf_tensor("t_aggb", [B, 256], F32))
        twork = _e(nc.sbuf_tensor("t_work", [B, D], F32))
        thought = _e(nc.sbuf_tensor("t_thought", [B, D], F32))
        tbf = _e(nc.sbuf_tensor("t_bf", [B, D], BF16))
        tfm = _e(nc.sbuf_tensor("t_fm", [128, NT * B], BF16))
        tstats = _e(nc.sbuf_tensor("t_stats", [B, 26], F32))
        wck0 = _e(nc.sbuf_tensor("t_wck0", [128, D], BF16))
        wck1 = _e(nc.sbuf_tensor("t_wck1", [128, D], BF16))
        ccst = _e(nc.sbuf_tensor("t_ccst", [B, D], F32))
        ptrT = [_e(nc.psum_tensor("t_ptr%d" % i, [128, B], BF16))
                for i in range(3)]
        pbig = _e(nc.psum_tensor("t_pbig", [B, D], F32))
        blk = _e(nc.Block())
        wck = [wck0, wck1]

        @blk.sync
        def _(sy):
            for dst, srcb in ((t_last, lastrows_h), (t_ing, in_g8_h),
                              (t_inb, in_b8_h), (t_aggb, agg_bsl_h),
                              (id32s, ident32_h)):
                sdT.inc(sy.dma_start(out=dst[:, :], in_=srcb[:, :]), 16)
            for dst, srcb in ((pngf, pn_g_fm_h), (pnbf, pn_b_fm_h),
                              (d1bf, d1b_fm_h), (d2bf, d2b_fm_h),
                              (onc32, onescol32_h), (on1_32, ones1_32_h)):
                sdT.inc(sy.dma_start(out=dst[:, :], in_=srcb[:, :]), 16)
            sdT.inc(sy.dma_start(out=ident[:, :], in_=ident_h[:, :]), 16)
            sdT.inc(sy.dma_start(out=ones_col[:, :], in_=onescol_h[:, :]), 16)
            sdT.inc(sy.dma_start(out=ones1[:, :], in_=ones1_h[:, :]), 16)
            sdT.inc(sy.dma_start(out=sel[:, :], in_=sel_h[:, :]), 16)
        small_loads = sdT.n

        def ln8(src_ap, dst_ap, W, eps, gamma=None, beta=None, pre_wait=None):
            nchunk = (W + 511) // 512

            @blk.vector
            def _(v):
                if pre_wait is not None:
                    pre_wait(v)
                sv.wait(v)
                for c in range(nchunk):
                    ist = v.bn_stats(tstats[:, 6 * c:6 * c + 6],
                                     src_ap[:, 512 * c:512 * (c + 1)])
                iv = sv.inc(ist)
                v.wait_ge(sv.h, iv)
                iv = sv.inc(v.bn_aggr(tstats[:, 24:26], tstats[:, 0:6 * nchunk]))
                v.wait_ge(sv.h, iv)
                sv.inc(v.tensor_scalar_add(tstats[:, 25:26], tstats[:, 25:26],
                                           eps))

            @blk.scalar
            def _(s):
                sv.wait(s)
                isx = ss.inc(s.activation(tstats[:, 25:26], tstats[:, 25:26],
                                          ACT.Ln))
                s.wait_ge(ss.h, isx)
                ss.inc(s.activation(tstats[:, 25:26], tstats[:, 25:26],
                                    ACT.Exp, scale=-0.5))

            @blk.vector
            def _(v):
                ss.wait(v)
                iv = sv.inc(v.tensor_scalar(dst_ap, src_ap, tstats[:, 24:25],
                                            tstats[:, 25:26], ALU.subtract,
                                            ALU.mult))
                if gamma is not None:
                    v.wait_ge(sv.h, iv)
                    iv = sv.inc(v.tensor_mul(dst_ap, dst_ap, gamma))
                if beta is not None:
                    v.wait_ge(sv.h, iv)
                    sv.inc(v.tensor_add(dst_ap, dst_ap, beta))

        def to_fm(src_ap, W, dst, dst_dtype_ps, idap):
            nchunk = W // 128

            @blk.vector
            def _(v):
                sv.wait(v)
                sv.inc(v.tensor_copy(tbf[:, 0:W], src_ap))
            ev_base = sv.n
            tr_base = st.n

            for g4 in range(0, nchunk, 4):
                @blk.tensor
                def _(t, g4=g4):
                    for c in range(g4, min(nchunk, g4 + 4)):
                        if c == 0:
                            sv.wait(t, ev_base)
                        if c >= 3:
                            sv.wait(t, ev_base + c - 2)
                        st.inc(t.transpose(ptrT[c % 3][:, :],
                                           tbf[:, 128 * c:128 * (c + 1)],
                                           ident[0:B, 0:B]))

                @blk.vector
                def _(v, g4=g4):
                    for c in range(g4, min(nchunk, g4 + 4)):
                        v.wait_ge(st.h, tr_base + c + 1)
                        v.tensor_copy(dst[:, B * c:B * (c + 1)],
                                      ptrT[c % 3][:, :]).then_inc(sv.h, 1)
                        sv.n += 1

        def mm8(whbm, nk, W, psum):
            nN = (W + 511) // 512
            load_val = {}
            mmv = {}
            st_base = st.n

            def emit_load(ck):
                @blk.gpsimd
                def _(g, ck=ck):
                    st.wait(g, mmv[ck - 2] if ck >= 2 else st_base)
                    swW[ck % 2].inc(
                        g.dma_start(out=wck[ck % 2][:, 0:W],
                                    in_=whbm[128 * ck:128 * (ck + 1), 0:W]), 16)
                load_val[ck] = swW[ck % 2].n

            emit_load(0)
            if nk > 1:
                emit_load(1)
            for ck in range(nk):
                @blk.tensor
                def _(t, ck=ck):
                    if ck == 0:
                        sv.wait(t)
                    swW[ck % 2].wait(t, load_val[ck])
                    for n in range(nN):
                        w1 = min(W, 512 * (n + 1))
                        mi = t.matmul(psum[:, 512 * n:w1],
                                      tfm[:, B * ck:B * (ck + 1)],
                                      wck[ck % 2][:, 512 * n:w1],
                                      start=(ck == 0), stop=(ck == nk - 1))
                    st.inc(mi)
                mmv[ck] = st.n
                if ck + 2 < nk:
                    emit_load(ck + 2)

        def w_small(v):
            sdT.wait(v, small_loads)
        ln8(t_last[:, :], twork[:, :], D, 1e-5, pre_wait=w_small)
        ln8(twork[:, :], thought[:, :], D, 1e-6, gamma=t_ing[:, :],
            beta=t_inb[:, :])
        to_fm(thought[:, :], D, tfm, None, None)
        mm8(aggw_h, NT, 256, pbig)

        @blk.vector
        def _(v):
            st.wait(v)
            iv = sv.inc(v.tensor_copy(ccst[:, 0:256], pbig[:, 0:256]))
            v.wait_ge(sv.h, iv)
            sv.inc(v.tensor_add(ccst[:, 0:256], ccst[:, 0:256], t_aggb[:, :]))

        @blk.gpsimd
        def _(g):
            sv.wait(g)
            swC.inc(g.dma_start(out=ag_in[:, :], in_=ccst[:, 0:256]), 16)
            swC.wait(g)
            scc.inc(g.collective_compute(
                "AllGather", ALU.bypass, replica_groups=[list(range(B))],
                ins=[ag_in.ap().opt()], outs=[ag_out.ap().opt()]))
    agval = scc.n

    pT_ctx = ExitStack()
    wckb_ctx = ExitStack()
    pTa = pTb = wckb = bc_tm = bc_b8s = None

    def fm_ln(blk):
        @blk.scalar
        def _(s):
            sv.wait(s)
            ss.inc(s.activation(sq_fm[:, :], thought_fm[:, :], ACT.Square))

        @blk.tensor
        def _(t):
            ss.wait(t)
            sv.wait(t)
            st.inc(t.matmul(pTa[0:1, 128:256], onc32[:, :], thought_fm[:, :],
                            start=True, stop=False))
            st.inc(t.matmul(pTa[0:1, 256:384], onc32[:, :], sq_fm[:, :],
                            start=False, stop=True))

        @blk.vector
        def _(v):
            st.wait(v)
            iv = sv.inc(v.reduce_sum(
                strow[0:1, 0:8],
                bass.AP(pTa.ap().tensor, 128, [[512, 1], [1, 8], [8, 16]]),
                axis=AX.X))
            v.wait_ge(sv.h, iv)
            iv = sv.inc(v.reduce_sum(
                strow[0:1, 8:16],
                bass.AP(pTa.ap().tensor, 256, [[512, 1], [1, 8], [8, 16]]),
                axis=AX.X))
            v.wait_ge(sv.h, iv)
            iv = sv.inc(v.tensor_scalar_mul(strow[0:1, 0:8], strow[0:1, 0:8],
                                            1.0 / D))
            v.wait_ge(sv.h, iv)
            iv = sv.inc(v.tensor_scalar_mul(strow[0:1, 8:16], strow[0:1, 8:16],
                                            1.0 / D))
            v.wait_ge(sv.h, iv)
            iv = sv.inc(v.tensor_mul(strow[0:1, 16:24], strow[0:1, 0:8],
                                     strow[0:1, 0:8]))
            v.wait_ge(sv.h, iv)
            iv = sv.inc(v.tensor_sub(strow[0:1, 8:16], strow[0:1, 8:16],
                                     strow[0:1, 16:24]))
            v.wait_ge(sv.h, iv)
            sv.inc(v.tensor_scalar_add(strow[0:1, 8:16], strow[0:1, 8:16],
                                       1e-6))

        @blk.scalar
        def _(s):
            sv.wait(s)
            isx = ss.inc(s.activation(strow[0:1, 8:16], strow[0:1, 8:16],
                                      ACT.Ln))
            s.wait_ge(ss.h, isx)
            ss.inc(s.activation(strow[0:1, 8:16], strow[0:1, 8:16],
                                ACT.Exp, scale=-0.5))

        @blk.vector
        def _(v):
            ss.wait(v)
            iv = sv.inc(v.tensor_mul(strow[0:1, 16:24], strow[0:1, 0:8],
                                     strow[0:1, 8:16]))
            v.wait_ge(sv.h, iv)
            sv.inc(v.tensor_scalar_mul(strow[0:1, 16:24], strow[0:1, 16:24],
                                       -1.0))

        @blk.tensor
        def _(t):
            sv.wait(t)
            st.inc(t.matmul(pTa[:, 384:392], on1_32[:, :], strow[0:1, 16:24],
                            start=True, stop=False))
            st.inc(t.matmul(pTa[:, 392:400], on1_32[:, :], strow[0:1, 8:16],
                            start=False, stop=True))

        @blk.vector
        def _(v):
            st.wait(v)
            iv = sv.inc(v.tensor_copy(mu8r[:, :], pTa[:, 384:392]))
            v.wait_ge(sv.h, iv)
            iv = sv.inc(v.tensor_copy(rs8r[:, :], pTa[:, 392:400]))
            v.wait_ge(sv.h, iv)
            ist = None
            for c in range(NT):
                ist = v.tensor_mul(sq_fm[:, B * c:B * (c + 1)],
                                   thought_fm[:, B * c:B * (c + 1)],
                                   rs8r[:, :])
            iv = sv.inc(ist)
            v.wait_ge(sv.h, iv)
            for c in range(NT):
                ist = v.tensor_add(sq_fm[:, B * c:B * (c + 1)],
                                   sq_fm[:, B * c:B * (c + 1)], mu8r[:, :])
            iv = sv.inc(ist)
            v.wait_ge(sv.h, iv)
            for c in range(NT):
                ist = v.tensor_scalar(t1_bf[:, B * c:B * (c + 1)],
                                      sq_fm[:, B * c:B * (c + 1)],
                                      pngf[:, c:c + 1], pnbf[:, c:c + 1],
                                      ALU.mult, ALU.add)
            sv.inc(ist)

    fmst = {}

    def fm_d1_loads(blk, k):
        @blk.gpsimd
        def _(g, k=k):
            st.wait(g, fmst.get(("d2m", k - 1), st.n))
            for c in range(NT):
                swB.inc(g.dma_start(
                    out=wckb[:, c * 1024:(c + 1) * 1024],
                    in_=d1s_h[c].rearrange("p c f -> p (c f)")), 16)
        fmst[("d1l", k)] = swB.n

    def fm_d1_mms(blk, k):
        @blk.tensor
        def _(t, k=k):
            sv.wait(t)
            swB.wait(t, fmst[("d1l", k)])
            for c in range(NT):
                for blkh in range(NHC):
                    mi = t.matmul(
                        pTa[:, blkh * 8:(blkh + 1) * 8],
                        wckb[:, c * 1024 + blkh * 128:
                             c * 1024 + blkh * 128 + 128],
                        t1_bf[:, B * c:B * (c + 1)],
                        start=(c == 0 and blkh == 0),
                        stop=(c == NT - 1 and blkh == NHC - 1))
            st.inc(mi)
        fmst[("d1m", k)] = st.n

        @blk.vector
        def _(v, k=k):
            st.wait(v, fmst[("d1m", k)])
            ist = None
            for blkh in range(NHC):
                ist = v.tensor_scalar_add(t2_bf[:, blkh * 8:(blkh + 1) * 8],
                                          pTa[:, blkh * 8:(blkh + 1) * 8],
                                          d1bf[:, blkh:blkh + 1])
            sv.inc(ist)

        @blk.scalar
        def _(s):
            sv.wait(s)
            ss.inc(s.activation(t2_bf[:, :], t2_bf[:, :], ACT.Gelu))
        fmst[("gelu", k)] = ss.n

    def fm_d2_loads(blk, k):
        @blk.gpsimd
        def _(g, k=k):
            st.wait(g, fmst[("d1m", k)])
            for c in range(NHC):
                swB.inc(g.dma_start(
                    out=wckb[:, c * 2048:(c + 1) * 2048],
                    in_=d2s_h[c].rearrange("p c f -> p (c f)")), 16)
        fmst[("d2l", k)] = swB.n

    def fm_d2_mms(blk, k):
        @blk.tensor
        def _(t, k=k):
            ss.wait(t, fmst[("gelu", k)])
            swB.wait(t, fmst[("d2l", k)])
            for c in range(NHC):
                for blkd in range(NT):
                    mi = t.matmul(
                        pTb[:, blkd * 8:(blkd + 1) * 8],
                        wckb[:, c * 2048 + blkd * 128:
                             c * 2048 + blkd * 128 + 128],
                        t2_bf[:, B * c:B * (c + 1)],
                        start=(c == 0 and blkd == 0),
                        stop=(c == NHC - 1 and blkd == NT - 1))
            st.inc(mi)
        fmst[("d2m", k)] = st.n

        @blk.vector
        def _(v, k=k):
            st.wait(v, fmst[("d2m", k)])
            sv.inc(v.tensor_copy(ccfm[:, :], pTb[:, 0:128]))

        @blk.gpsimd
        def _(g):
            sv.wait(g)
            swC.inc(g.dma_start(out=ccf_in[:, :], in_=ccfm[:, :]), 16)
            swC.wait(g)
            scc.inc(g.collective_compute(
                "AllReduce", ALU.add, replica_groups=[list(range(B))],
                ins=[ccf_in.ap().opt()], outs=[ccf_out.ap().opt()]))
        fmst[("ar", k)] = scc.n

    def fm_update(blk, k):
        @blk.gpsimd
        def _(g, k=k):
            scc.wait(g, fmst[("ar", k)])
            swC.inc(g.dma_start(out=ccfm[:, :], in_=ccf_out[:, :]), 16)
        cw = swC.n

        @blk.vector
        def _(v, k=k, cw=cw):
            swC.wait(v, cw)
            iv = sv.inc(v.tensor_add(thought_fm[:, :], thought_fm[:, :],
                                     ccfm[:, :]))
            v.wait_ge(sv.h, iv)
            ist = None
            for c in range(NT):
                ist = v.tensor_scalar_add(thought_fm[:, B * c:B * (c + 1)],
                                          thought_fm[:, B * c:B * (c + 1)],
                                          d2bf[:, c:c + 1])
            sv.inc(ist)

    def fm_bc_a(blk):
        @blk.vector
        def _(v):
            sv.wait(v)
            sv.inc(v.tensor_copy(th_bf[:, :], thought_fm[:, :]))
        thbv = sv.n

        @blk.gpsimd
        def _(g):
            st.wait(g, fmst[("d2m", KSTEPS - 1)])
            for c in range(NT):
                swB.inc(g.dma_start(
                    out=wckb[:, c * 256:(c + 1) * 256],
                    in_=bcw_h[128 * c:128 * (c + 1), :]), 16)
        bcld = swB.n

        @blk.tensor
        def _(t):
            sv.wait(t, thbv)
            swB.wait(t, bcld)
            for c in range(NT):
                mi = t.matmul(pTa[0:B, 0:256], th_bf[:, B * c:B * (c + 1)],
                              wckb[:, c * 256:(c + 1) * 256],
                              start=(c == 0), stop=(c == NT - 1))
            st.inc(mi)

        @blk.vector
        def _(v):
            st.wait(v)
            sv.inc(v.tensor_copy(bc_tm[:, 0:256], pTa[0:B, 0:256]))

        @blk.gpsimd
        def _(g):
            sv.wait(g)
            swC.inc(g.dma_start(out=ag_in[:, :], in_=bc_tm[:, 0:256]), 16)
            swC.wait(g)
            scc.inc(g.collective_compute(
                "AllGather", ALU.bypass, replica_groups=[list(range(B))],
                ins=[ag_in.ap().opt()], outs=[ag_out.ap().opt()]))

    def fm_bc_b(blk):
        @blk.gpsimd
        def _(g):
            scc.wait(g)
            for r in range(B):
                swC.inc(g.dma_start(out=bc_tm[:, 256 * r:256 * (r + 1)],
                                    in_=ag_out[B * r:B * (r + 1), :]), 16)

        @blk.vector
        def _(v):
            swC.wait(v)
            sdT.wait(v)
            sv.inc(v.tensor_add(bc_tm[:, :], bc_tm[:, :], bc_b8s[:, :]))

    seam_plan = {}
    for k in range(KSTEPS):
        base = 6 * k
        seam_plan.setdefault(base + 0, []).append(
            lambda blk, k=k: fm_d1_loads(blk, k))
        seam_plan.setdefault(base + 2, []).append(
            lambda blk, k=k: (fm_update(blk, k - 1) if k > 0 else None,
                              fm_ln(blk)))
        seam_plan.setdefault(base + 3, []).append(
            lambda blk, k=k: (fm_d1_mms(blk, k), fm_d2_loads(blk, k)))
        seam_plan.setdefault(base + 6, []).append(
            lambda blk, k=k: fm_d2_mms(blk, k))
    seam_plan.setdefault(26, []).append(
        lambda blk: fm_update(blk, KSTEPS - 1))
    seam_plan.setdefault(27, []).append(fm_bc_a)
    seam_plan.setdefault(30, []).append(fm_bc_b)
    seam_i = [0]

    def fm_seam(blk):
        for fn in seam_plan.get(seam_i[0], []):
            fn(blk)
        seam_i[0] += 1

    sdX = [Sem(nc, "sdX0"), Sem(nc, "sdX1")]
    swS = [Sem(nc, "swS0"), Sem(nc, "swS1")]
    xn_ctx = nc.sbuf_tensor("xn_fm", [128, NT * D], BF16)
    xn_fm = xn_ctx.__enter__()
    q_ctx = nc.sbuf_tensor("q_tm", [128, NT * D], BF16)
    q_tm = q_ctx.__enter__()
    with ExitStack() as _ctx:
        _e = _ctx.enter_context
        xt0 = _e(nc.sbuf_tensor("b1_x0", [128, D], F32))
        xt1 = _e(nc.sbuf_tensor("b1_x1", [128, D], F32))
        xb0 = _e(nc.sbuf_tensor("b1_bf0", [128, D], BF16))
        xb1 = _e(nc.sbuf_tensor("b1_bf1", [128, D], BF16))
        bst = _e(nc.sbuf_tensor("b1_st", [128, 26], F32))
        w0 = _e(nc.sbuf_tensor("b2_w0", [128, NT * 512], BF16))
        w1 = _e(nc.sbuf_tensor("b2_w1", [128, NT * 512], BF16))
        tp0 = _e(nc.psum_tensor("b1_p0", [128, 128], BF16))
        tp1 = _e(nc.psum_tensor("b1_p1", [128, 128], BF16))
        tp2 = _e(nc.psum_tensor("b1_p2", [128, 128], BF16))
        tp3 = _e(nc.psum_tensor("b1_p3", [128, 128], BF16))
        p0 = _e(nc.psum_tensor("b2_p0", [128, 512], F32))
        p1 = _e(nc.psum_tensor("b2_p1", [128, 512], F32))
        th_stage = _e(nc.sbuf_tensor("b1_thstage", [B, D], F32))
        ptf2 = _e(nc.psum_tensor("b1_ptf2", [128, B], F32))
        blk = _e(nc.Block())
        xts = [xt0, xt1]
        xbs = [xb0, xb1]
        tps = [tp0, tp1, tp2, tp3]
        ws = [w0, w1]
        ps = [p0, p1]
        load_val = {}
        apply_val = {}
        tr_val = {}
        ev_val = {}
        wload = {}
        mmv = {}
        evv = {}

        def wslab_load(fs, whbm):
            @blk.gpsimd
            def _(g, fs=fs):
                if fs >= 2:
                    st.wait(g, mmv[(fs - 2, NT - 1)])
                for c in range(NT):
                    swS[fs % 2].inc(g.dma_start(
                        out=ws[fs % 2][:, c * 512:(c + 1) * 512],
                        in_=whbm[c * 128:(c + 1) * 128,
                                 fs * 512:(fs + 1) * 512]), 16)
            wload[fs] = swS[fs % 2].n

        def emit_load(tt):
            @blk.sync
            def _(sy, tt=tt):
                if tt >= 2:
                    sv.wait(sy, apply_val[tt - 2])
                for q in range(4):
                    sdX[tt % 2].inc(sy.dma_start(
                        out=xts[tt % 2][:, 512 * q:512 * (q + 1)],
                        in_=x_h[128 * tt:128 * (tt + 1),
                                512 * q:512 * (q + 1)]), 16)
            load_val[tt] = sdX[tt % 2].n

        def b1_tile(tt):
            @blk.vector
            def _(v, tt=tt):
                sdX[tt % 2].wait(v, load_val[tt])
                if tt >= 2:
                    st.wait(v, tr_val.get((tt - 2, 15), 0))
                sv.wait(v)
                xt = xts[tt % 2]
                for c in range(NF):
                    ist = v.bn_stats(bst[:, 6 * c:6 * c + 6],
                                     xt[:, 512 * c:512 * (c + 1)])
                iv = sv.inc(ist)
                v.wait_ge(sv.h, iv)
                iv = sv.inc(v.bn_aggr(bst[:, 24:26], bst[:, 0:24]))
                v.wait_ge(sv.h, iv)
                v.tensor_copy(mu_cols[:, tt:tt + 1], bst[:, 24:25])
                sv.inc(v.tensor_scalar_add(bst[:, 25:26], bst[:, 25:26], 1e-5))

            @blk.scalar
            def _(s, tt=tt):
                sv.wait(s)
                isx = ss.inc(s.activation(bst[:, 25:26], bst[:, 25:26], ACT.Ln))
                s.wait_ge(ss.h, isx)
                ss.inc(s.activation(rsig_cols[:, tt:tt + 1], bst[:, 25:26],
                                    ACT.Exp, scale=-0.5))

            @blk.vector
            def _(v, tt=tt):
                ss.wait(v)
                v.tensor_scalar(nmr_cols[:, tt:tt + 1], mu_cols[:, tt:tt + 1],
                                rsig_cols[:, tt:tt + 1], -1.0,
                                ALU.mult, ALU.mult)
                sv.inc(v.tensor_scalar(xbs[tt % 2][:, :], xts[tt % 2][:, :],
                                       mu_cols[:, tt:tt + 1],
                                       rsig_cols[:, tt:tt + 1],
                                       ALU.subtract, ALU.mult))
            apply_val[tt] = sv.n
            if tt + 2 < NT:
                emit_load(tt + 2)

            for grp in range(4):
                @blk.tensor
                def _(t, tt=tt, grp=grp):
                    if grp == 0:
                        sv.wait(t, apply_val[tt])
                    for j in range(4):
                        c = grp * 4 + j
                        if (tt, c) >= (0, 4):
                            prev = ev_val.get((tt, c - 4),
                                              ev_val.get((tt - 1, c + 12), 0))
                            ss.wait(t, prev)
                        st.inc(t.transpose(tps[c % 4][:, :],
                                           xbs[tt % 2][:, 128 * c:128 * (c + 1)],
                                           ident[:, :]))
                        tr_val[(tt, c)] = st.n

                @blk.scalar
                def _(s, tt=tt, grp=grp):
                    for j in range(4):
                        c = grp * 4 + j
                        st.wait(s, tr_val[(tt, c)])
                        ss.inc(s.activation(
                            bass.AP(xn_fm, c * D + tt * 128,
                                    [[NT * D, 128], [1, 128]]),
                            tps[c % 4][:, :], ACT.Identity))
                        ev_val[(tt, c)] = ss.n

        def b2_tile(fs, sc):
            @blk.tensor
            def _(t, fs=fs, sc=sc):
                if sc == 0:
                    swS[fs % 2].wait(t, wload[fs])
                ss.wait(t, ev_val[(sc, 15)])
                idx = fs * NT + sc
                if idx >= 2:
                    sv.wait(t, evv[idx - 2])
                for e in range(NT):
                    mi = t.matmul(
                        ps[idx % 2][:, :],
                        bass.AP(xn_fm, e * D + sc * 128,
                                [[NT * D, 128], [1, 128]]),
                        ws[fs % 2][:, e * 512:(e + 1) * 512],
                        start=(e == 0), stop=(e == NT - 1))
                st.inc(mi)
            mmv[(fs, sc)] = st.n

            @blk.vector
            def _(v, fs=fs, sc=sc):
                idx = fs * NT + sc
                st.wait(v, mmv[(fs, sc)])
                sv.inc(v.tensor_copy(
                    q_tm[:, sc * D + fs * 512: sc * D + fs * 512 + 512],
                    ps[idx % 2][:, :]))
            evv[fs * NT + sc] = sv.n

        def fm_init():
            @blk.gpsimd
            def _(g):
                scc.wait(g, agval)
                for r in range(B):
                    swC.inc(g.dma_start(
                        out=th_stage[:, 256 * r:256 * (r + 1)],
                        in_=ag_out[B * r:B * (r + 1), :]), 16)
            reasm = swC.n
            trn = {}
            ev_base = sv.n
            for g4 in range(0, NT, 4):
                @blk.tensor
                def _(t, g4=g4):
                    if g4 == 0:
                        swC.wait(t, reasm)
                    for c in range(g4, min(NT, g4 + 4)):
                        if c > 0:
                            sv.wait(t, ev_base + c)
                        st.inc(t.transpose(ptf2[:, :],
                                           th_stage[:, 128 * c:128 * (c + 1)],
                                           id32s[0:B, 0:B]))
                        trn[c] = st.n

                @blk.vector
                def _(v, g4=g4):
                    for c in range(g4, min(NT, g4 + 4)):
                        st.wait(v, trn[c])
                        v.tensor_copy(thought_fm[:, B * c:B * (c + 1)],
                                      ptf2[:, :]).then_inc(sv.h, 1)
                        sv.n += 1
                        assert sv.n == ev_base + c + 1

        wslab_load(0, wq_h)
        wslab_load(1, wq_h)
        emit_load(0)
        emit_load(1)
        for tt in range(NT):
            b1_tile(tt)
            if tt == 6:
                fm_init()
            if tt >= 1:
                b2_tile(0, tt - 1)
        b2_tile(0, NT - 1)
        wslab_load(2, wq_h)
        for fs in range(1, NF):
            for sc in range(NT):
                b2_tile(fs, sc)
            if fs + 2 < NF:
                wslab_load(fs + 2, wq_h)

    if stop_after == "B2":
        return _dump_and_stop(nc, [("mu_cols", mu_cols), ("rsig_cols", rsig_cols),
                                   ("xn_fm", xn_fm), ("q_tm", q_tm)])

    sdG = [Sem(nc, "sdG0"), Sem(nc, "sdG1")]
    swGB = Sem(nc, "swGB")
    with ExitStack() as _ctx:
        _e = _ctx.enter_context
        w0 = _e(nc.sbuf_tensor("b25_w0", [128, NT * 512], BF16))
        w1 = _e(nc.sbuf_tensor("b25_w1", [128, NT * 512], BF16))
        g0 = _e(nc.sbuf_tensor("b25_g0", [128, 512], BF16))
        gate_b_bf = _e(nc.sbuf_tensor("gate_b_bfS", [1, D], BF16))
        g1 = _e(nc.sbuf_tensor("b25_g1", [128, 512], BF16))
        p0 = _e(nc.psum_tensor("b25_p0", [128, 512], F32))
        p1 = _e(nc.psum_tensor("b25_p1", [128, 512], F32))
        blk = _e(nc.Block())
        ws = [w0, w1]
        gs = [g0, g1]
        ps = [p0, p1]
        wload = {}
        mmv = {}
        actv = {}
        spv = {}

        @blk.gpsimd
        def _(g):
            swGB.inc(g.dma_start(out=gate_b_bf[:, :], in_=gate_b_h[:, :]), 16)

        def wslab_load(fs):
            @blk.gpsimd
            def _(g, fs=fs):
                if fs >= 2:
                    st.wait(g, mmv[(fs - 2, NT - 1)])
                for c in range(NT):
                    swS[fs % 2].inc(g.dma_start(
                        out=ws[fs % 2][:, c * 512:(c + 1) * 512],
                        in_=gatew_h[c * 128:(c + 1) * 128,
                                 fs * 512:(fs + 1) * 512]), 16)
            wload[fs] = swS[fs % 2].n

        wslab_load(0)
        wslab_load(1)
        for fs in range(NF):
            for sc in range(NT):
                idx = fs * NT + sc

                @blk.tensor
                def _(t, fs=fs, sc=sc, idx=idx):
                    if idx == 0:
                        swGB.wait(t)
                    if sc == 0:
                        swS[fs % 2].wait(t, wload[fs])
                    if idx >= 2:
                        ss.wait(t, actv[idx - 2])
                    for e in range(NT):
                        t.matmul(
                            ps[idx % 2][:, :],
                            bass.AP(xn_fm, e * D + sc * 128,
                                    [[NT * D, 128], [1, 128]]),
                            ws[fs % 2][:, e * 512:(e + 1) * 512],
                            start=(e == 0), stop=False)
                    st.inc(t.matmul(ps[idx % 2][:, :], ones1[:, :],
                                    gate_b_bf[:, fs * 512:(fs + 1) * 512],
                                    start=False, stop=True))
                mmv[(fs, sc)] = st.n

                @blk.scalar
                def _(s, fs=fs, sc=sc, idx=idx):
                    st.wait(s, mmv[(fs, sc)])
                    if idx >= 2:
                        sdG[idx % 2].wait(s, spv[idx - 2])
                    ss.inc(s.activation(gs[idx % 2][:, :], ps[idx % 2][:, :],
                                        ACT.Sigmoid))
                actv[idx] = ss.n

                @blk.sync
                def _(sy, fs=fs, sc=sc, idx=idx):
                    ss.wait(sy, actv[idx])
                    sdG[idx % 2].inc(sy.dma_start(
                        out=gate_sp[sc * 128:(sc + 1) * 128,
                                    fs * 512:(fs + 1) * 512],
                        in_=gs[idx % 2][:, :]), 16)
                spv[idx] = sdG[idx % 2].n
            if fs + 2 < NF:
                wslab_load(fs + 2)

        @blk.sync
        def _(sy):
            sdG[0].wait(sy)
            sdG[1].wait(sy)

    if stop_after == "B25":
        return _dump_and_stop(nc, [], [("gate_sp", gate_sp)])

    sdW = [Sem(nc, "sdW0"), Sem(nc, "sdW1"), Sem(nc, "sdW2")]
    with ExitStack() as _ctx:
        _e = _ctx.enter_context
        w0 = _e(nc.sbuf_tensor("b3_w0", [128, NT * 512], BF16))
        w1 = _e(nc.sbuf_tensor("b3_w1", [128, NT * 512], BF16))
        kslab = _e(nc.sbuf_tensor("b3_k", [128, NT * 512], BF16))
        wt0 = _e(nc.sbuf_tensor("b3_wt0", [128, D], BF16))
        wt1 = _e(nc.sbuf_tensor("b3_wt1", [128, D], BF16))
        wt2 = _e(nc.sbuf_tensor("b3_wt2", [128, D], BF16))
        l_cols = _e(nc.sbuf_tensor("b3_l", [128, NT], F32))
        pk0 = _e(nc.psum_tensor("b3_pk0", [128, 512], F32))
        pk1 = _e(nc.psum_tensor("b3_pk1", [128, 512], F32))
        ps0 = _e(nc.psum_tensor("b3_ps0", [128, 512], F32))
        ps1 = _e(nc.psum_tensor("b3_ps1", [128, 512], F32))
        ps2 = _e(nc.psum_tensor("b3_ps2", [128, 512], F32))
        pl = _e(nc.psum_tensor("b3_pl", [128, NT], F32))
        blk = _e(nc.Block())
        ws = [w0, w1]
        pks = [pk0, pk1]
        pss = [ps0, ps1, ps2]
        wts = [wt0, wt1, wt2]
        wload = {}
        kmmv = {}
        kevv = {}
        smmv = {}
        expv = {}
        lmmv = {}
        spv = {}

        def wslab_load(fs):
            @blk.gpsimd
            def _(g, fs=fs):
                if fs >= 2:
                    st.wait(g, kmmv[(fs - 2, NT - 1)])
                for c in range(NT):
                    swS[fs % 2].inc(g.dma_start(
                        out=ws[fs % 2][:, c * 512:(c + 1) * 512],
                        in_=wk_h[c * 128:(c + 1) * 128,
                                 fs * 512:(fs + 1) * 512]), 16)
            wload[fs] = swS[fs % 2].n

        wslab_load(0)
        wslab_load(1)
        for fs in range(NF):
            for sc in range(NT):
                @blk.tensor
                def _(t, fs=fs, sc=sc):
                    if sc == 0:
                        swS[fs % 2].wait(t, wload[fs])
                    idx = fs * NT + sc
                    if idx >= 2:
                        sv.wait(t, kevv[idx - 2])
                    for e in range(NT):
                        mi = t.matmul(
                            pks[idx % 2][:, :],
                            bass.AP(xn_fm, e * D + sc * 128,
                                    [[NT * D, 128], [1, 128]]),
                            ws[fs % 2][:, e * 512:(e + 1) * 512],
                            start=(e == 0), stop=(e == NT - 1))
                    st.inc(mi)
                kmmv[(fs, sc)] = st.n

                @blk.vector
                def _(v, fs=fs, sc=sc):
                    idx = fs * NT + sc
                    st.wait(v, kmmv[(fs, sc)])
                    sv.inc(v.tensor_copy(kslab[:, sc * 512:(sc + 1) * 512],
                                         pks[idx % 2][:, :]))
                kevv[fs * NT + sc] = sv.n
            kslab_ready = sv.n
            if fs + 2 < NF:
                wslab_load(fs + 2)

            for fb in range(4):
                gfb = fs * 4 + fb
                for eN in range(NF):
                    gidx = gfb * NF + eN

                    def emit_lmm(gidx2):
                        gfb2, eN2 = gidx2 // NF, gidx2 % NF

                        @blk.tensor
                        def _(t, gfb2=gfb2, eN2=eN2, gidx2=gidx2):
                            ss.wait(t, expv[gidx2])
                            for cc in range(4):
                                ec = eN2 * 4 + cc
                                mi = t.matmul(
                                    pl[:, ec:ec + 1],
                                    wts[gfb2 % 3][:, ec * 128:(ec + 1) * 128],
                                    ones_col[:, :],
                                    start=(gfb2 == 0 and ec == 0),
                                    stop=(gfb2 == NT - 1 and ec == NT - 1))
                            st.inc(mi)
                        lmmv[gidx2] = st.n

                    @blk.tensor
                    def _(t, fs=fs, fb=fb, eN=eN, gidx=gidx,
                          kready=kslab_ready):
                        if fb == 0 and eN == 0:
                            sv.wait(t, kready)
                        if gidx >= 3:
                            ss.wait(t, expv.get(gidx - 3, 0))
                        for sc in range(NT):
                            mi = t.matmul(
                                pss[gidx % 3][:, :],
                                kslab[:, sc * 512 + fb * 128:
                                      sc * 512 + fb * 128 + 128],
                                q_tm[:, sc * D + eN * 512:
                                     sc * D + eN * 512 + 512],
                                start=(sc == 0), stop=(sc == NT - 1))
                        st.inc(mi)
                    smmv[gidx] = st.n

                    @blk.scalar
                    def _(s, gfb=gfb, eN=eN, gidx=gidx):
                        st.wait(s, smmv[gidx])
                        if gfb >= 3:
                            st.wait(s, lmmv.get((gfb - 3) * NF + eN, 0))
                            sdW[(gfb - 3) % 3].wait(s, spv.get(gfb - 3, 0))
                        ss.inc(s.activation(
                            wts[gfb % 3][:, eN * 512:(eN + 1) * 512],
                            pss[gidx % 3][:, :], ACT.Exp, scale=0.125))
                    expv[gidx] = ss.n

                    if gidx >= 2:
                        emit_lmm(gidx - 2)
                    if gidx == NT * NF - 1:
                        emit_lmm(gidx - 1)
                        emit_lmm(gidx)

                @blk.sync
                def _(sy, gfb=gfb):
                    ss.wait(sy, expv[gfb * NF + NF - 1])
                    sdW[gfb % 3].inc(sy.dma_start(out=wT_sp[:, gfb, :],
                                                  in_=wts[gfb % 3][:, :]), 16)
                spv[gfb] = sdW[gfb % 3].n

        @blk.vector
        def _(v):
            st.wait(v)
            iv = sv.inc(v.tensor_copy(l_cols[:, :], pl[:, :]))
            v.wait_ge(sv.h, iv)
            sv.inc(v.reciprocal(r_cols[:, :], l_cols[:, :]))

        @blk.sync
        def _(sy):
            sdW[0].wait(sy)
            sdW[1].wait(sy)
            sdW[2].wait(sy)

    q_ctx.__exit__(None, None, None)
    if stop_after == "B3":
        return _dump_and_stop(nc, [("r_cols", r_cols)], [("wT_sp", wT_sp)])

    pTa = pT_ctx.enter_context(nc.psum_tensor("pTa", [128, 512], F32))
    pTb = pT_ctx.enter_context(nc.psum_tensor("pTb", [128, 512], F32))
    bc_tm = nc.alloc_sbuf_tensor("bc_tm", [B, D], F32, side="right")
    bc_b8s = nc.alloc_sbuf_tensor("bc_b8s", [B, D], F32, side="right")
    wckb = wckb_ctx.enter_context(
        nc.sbuf_tensor("wckb", [128, 16384], BF16, side="right"))
    swV = [Sem(nc, "swV0"), Sem(nc, "swV1")]
    v_ctx = nc.sbuf_tensor("v_fm", [128, NT * D], BF16, side="right")
    v_fm = v_ctx.__enter__()
    with ExitStack() as _ctx:
        _e = _ctx.enter_context
        w0 = _e(nc.sbuf_tensor("b35_w0", [128, NT * 128], BF16))
        w1 = _e(nc.sbuf_tensor("b35_w1", [128, NT * 128], BF16))
        p0 = _e(nc.psum_tensor("b35_p0", [128, 512], F32))
        p1 = _e(nc.psum_tensor("b35_p1", [128, 512], F32))
        blk = _e(nc.Block())
        ws = [w0, w1]
        ps = [p0, p1]
        wload = {}
        mmv = {}
        evv = {}

        def wcol_load(fb):
            @blk.gpsimd
            def _(g, fb=fb):
                if fb >= 2:
                    st.wait(g, mmv[(fb - 2, NF - 1)])
                for c in range(NT):
                    swV[fb % 2].inc(g.dma_start(
                        out=ws[fb % 2][:, c * 128:(c + 1) * 128],
                        in_=wv_h[fb, c * 128:(c + 1) * 128, :]), 16)
            wload[fb] = swV[fb % 2].n

        @blk.sync
        def _(sy):
            sdT.inc(sy.dma_start(out=bc_b8s[:, :], in_=bc_b8_h[:, :]), 16)

        wcol_load(0)
        wcol_load(1)
        for fb in range(NT):
            for sN in range(NF):
                @blk.tensor
                def _(t, fb=fb, sN=sN):
                    if sN == 0:
                        swV[fb % 2].wait(t, wload[fb])
                    idx = fb * NF + sN
                    if idx >= 2:
                        sv.wait(t, evv[idx - 2])
                    for e in range(NT):
                        mi = t.matmul(
                            ps[idx % 2][:, :],
                            ws[fb % 2][:, e * 128:(e + 1) * 128],
                            bass.AP(xn_fm, e * D + sN * 512,
                                    [[NT * D, 128], [1, 512]]),
                            start=(e == 0), stop=(e == NT - 1))
                    st.inc(mi)
                mmv[(fb, sN)] = st.n

                @blk.vector
                def _(v, fb=fb, sN=sN):
                    idx = fb * NF + sN
                    st.wait(v, mmv[(fb, sN)])
                    sv.inc(v.tensor_copy(
                        v_fm[:, fb * D + sN * 512: fb * D + sN * 512 + 512],
                        ps[idx % 2][:, :]))
                evv[fb * NF + sN] = sv.n
            if fb + 2 < NT:
                wcol_load(fb + 2)
            fm_seam(blk)

    xn_ctx.__exit__(None, None, None)
    if stop_after == "B35":
        return _dump_and_stop(nc, [("v_fm", v_fm)])

    swO = Sem(nc, "swO")
    sdC = [Sem(nc, "sdC0"), Sem(nc, "sdC1")]
    sdA = [Sem(nc, "sdA0"), Sem(nc, "sdA1")]
    wo_ctx = nc.sbuf_tensor("wo_bf", [128, NT * D], BF16)
    wo_bf = wo_ctx.__enter__()
    with ExitStack() as _ctx:
        _e = _ctx.enter_context
        c0 = _e(nc.sbuf_tensor("b4_c0", [128, NT * 128], BF16))
        c1 = _e(nc.sbuf_tensor("b4_c1", [128, NT * 128], BF16))
        a0 = _e(nc.sbuf_tensor("b4_a0", [128, D], BF16))
        a1 = _e(nc.sbuf_tensor("b4_a1", [128, D], BF16))
        pp = [_e(nc.psum_tensor("b4_p%d" % i, [128, 1024], F32))
              for i in range(3)]
        blk = _e(nc.Block())
        cs = [c0, c1]
        asb = [a0, a1]
        cload = {}
        mmv = {}
        evv = {}
        spv = {}

        @blk.gpsimd
        def _(g):
            for c in range(NT):
                swO.inc(g.dma_start(
                    out=wo_bf[:, c * D:(c + 1) * D],
                    in_=wo_h[c * 128:(c + 1) * 128, :]), 16)
        wo_loaded = swO.n

        def col_load(ec):
            @blk.sync
            def _(sy, ec=ec):
                if ec >= 2:
                    st.wait(sy, hmm[2 * (ec - 2) + 1])
                sdC[ec % 2].inc(sy.dma_start(
                    out=cs[ec % 2][:, :].rearrange("p (c f) -> p c f", f=128),
                    in_=wT_sp[:, :, ec * 128:(ec + 1) * 128]), 16)
            cload[ec] = sdC[ec % 2].n

        col_load(0)
        col_load(1)
        hmm = {}
        hev = {}
        for ec in range(NT):
            for half in range(2):
                hid = 2 * ec + half

                @blk.tensor
                def _(t, ec=ec, half=half, hid=hid):
                    if half == 0:
                        sdC[ec % 2].wait(t, cload[ec])
                    if hid >= 3:
                        sv.wait(t, hev[hid - 3])
                    pr = pp[hid % 3]
                    for s2 in range(2):
                        sN = half * 2 + s2
                        for fb in range(NT):
                            mi = t.matmul(
                                pr[:, s2 * 512:(s2 + 1) * 512],
                                cs[ec % 2][:, fb * 128:(fb + 1) * 128],
                                v_fm[:, fb * D + sN * 512:
                                     fb * D + sN * 512 + 512],
                                start=(fb == 0), stop=(fb == NT - 1))
                        st.inc(mi)
                hmm[hid] = st.n

                @blk.vector
                def _(v, ec=ec, half=half, hid=hid):
                    st.wait(v, hmm[hid])
                    if half == 0 and ec >= 2:
                        sdA[ec % 2].wait(v, spv[ec - 2])
                    sv.inc(v.tensor_scalar_mul(
                        asb[ec % 2][:, half * 1024:(half + 1) * 1024],
                        pp[hid % 3][:, :], r_cols[:, ec:ec + 1]))
                hev[hid] = sv.n

            @blk.sync
            def _(sy, ec=ec):
                sv.wait(sy, hev[2 * ec + 1])
                sdA[ec % 2].inc(sy.dma_start(out=attn_sp[:, ec, :],
                                             in_=asb[ec % 2][:, :]), 16)
            spv[ec] = sdA[ec % 2].n
            if ec + 2 < NT:
                col_load(ec + 2)
            fm_seam(blk)

        @blk.sync
        def _(sy):
            sdA[0].wait(sy)
            sdA[1].wait(sy)

    v_ctx.__exit__(None, None, None)
    wckb_ctx.close()
    pT_ctx.close()
    if stop_after == "B4":
        return _dump_and_stop(nc, [], [("attn_sp", attn_sp)])

    sdL = [Sem(nc, "sdL0"), Sem(nc, "sdL1")]
    sdO = [Sem(nc, "sdO0"), Sem(nc, "sdO1")]
    sdOG = Sem(nc, "sdOG")
    with ExitStack() as _ctx:
        _e = _ctx.enter_context
        og = _e(nc.sbuf_tensor("b6_og", [128, D], F32))
        ob = _e(nc.sbuf_tensor("b6_ob", [128, D], F32))
        x0 = _e(nc.sbuf_tensor("b6_x0", [128, D], F32))
        x1 = _e(nc.sbuf_tensor("b6_x1", [128, D], F32))
        g0 = _e(nc.sbuf_tensor("b6_g0", [128, D], BF16))
        g1 = _e(nc.sbuf_tensor("b6_g1", [128, D], BF16))
        ac0 = _e(nc.sbuf_tensor("b6_ac0", [128, NT * 128], BF16))
        ac1 = _e(nc.sbuf_tensor("b6_ac1", [128, NT * 128], BF16))
        wk1a = _e(nc.sbuf_tensor("b6_w1a", [128, D], F32))
        wk1b = _e(nc.sbuf_tensor("b6_w1b", [128, D], F32))
        wk2a = _e(nc.sbuf_tensor("b6_w2a", [128, D], F32))
        wk2b = _e(nc.sbuf_tensor("b6_w2b", [128, D], F32))
        o0 = _e(nc.sbuf_tensor("b6_o0", [128, D], F32))
        o1 = _e(nc.sbuf_tensor("b6_o1", [128, D], F32))
        bs0 = _e(nc.sbuf_tensor("b6_st0", [128, 28], F32))
        bs1 = _e(nc.sbuf_tensor("b6_st1", [128, 28], F32))
        pA = _e(nc.psum_tensor("b6_pA", [128, D], F32))
        pB = _e(nc.psum_tensor("b6_pB", [128, D], F32))
        blk = _e(nc.Block())
        xs = [x0, x1]
        gs = [g0, g1]
        acs = [ac0, ac1]
        os_ = [o0, o1]
        wk1s = [wk1a, wk1b]
        wk2s = [wk2a, wk2b]
        bss = [bs0, bs1]
        pp = [pA, pB]
        lval = {}
        mmv = {}
        combv = {}
        outv = {}
        g1v = {}
        g2v = {}
        s4v = {}
        v1v = {}

        @blk.sync
        def _(sy):
            sdOG.inc(sy.dma_start(out=og[:, :], in_=og128_h[:, :]), 16)
            sdOG.inc(sy.dma_start(out=ob[:, :], in_=ob128_h[:, :]), 16)

        @blk.tensor
        def _(t):
            for n in range(NF):
                st.inc(t.matmul(pA[:, 512 * n:512 * (n + 1)], sel[:, :],
                                bc_tm[:, 512 * n:512 * (n + 1)],
                                start=True, stop=True))
        bcmm = st.n

        @blk.vector
        def _(v):
            st.wait(v, bcmm)
            sv.inc(v.tensor_copy(bc_rep[:, :], pA[:, :]))
        bcev = sv.n

        def loads(sc):
            @blk.sync
            def _(sy, sc=sc):
                if sc >= 2:
                    sv.wait(sy, combv[sc - 2])
                sdL[sc % 2].inc(sy.dma_start(
                    out=xs[sc % 2][:, :],
                    in_=x_h[sc * 128:(sc + 1) * 128, :]), 16)
                sdL[sc % 2].inc(sy.dma_start(
                    out=gs[sc % 2][:, :],
                    in_=gate_sp[sc * 128:(sc + 1) * 128, :]), 16)
                sdL[sc % 2].inc(sy.dma_start(
                    out=acs[sc % 2][:, :].rearrange("p (c f) -> p c f", f=128),
                    in_=attn_sp[:, :, sc * 128:(sc + 1) * 128]), 16)
            lval[sc] = sdL[sc % 2].n

        loads(0)
        loads(1)
        for sc in range(NT):
            p = sc % 2
            wk1 = wk1s[p]
            wk2 = wk2s[p]
            bst = bss[p]

            @blk.tensor
            def _(t, sc=sc, p=p):
                if sc == 0:
                    swO.wait(t, wo_loaded)
                    sv.wait(t, bcev)
                sdL[p].wait(t, lval[sc])
                if sc >= 2:
                    sv.wait(t, combv[sc - 2])
                for gN in range(NF):
                    for ec in range(NT):
                        mi = t.matmul(
                            pp[p][:, gN * 512:(gN + 1) * 512],
                            acs[p][:, ec * 128:(ec + 1) * 128],
                            wo_bf[:, ec * D + gN * 512: ec * D + gN * 512 + 512],
                            start=(ec == 0), stop=(ec == NT - 1))
                    st.inc(mi)
            mmv[sc] = st.n

            @blk.scalar
            def _(s, sc=sc, p=p, wk1=wk1, wk2=wk2):
                sdL[p].wait(s, lval[sc])
                if sc >= 2:
                    sg.wait(s, g1v[sc - 2])
                    sv.wait(s, combv[sc - 2])
                i1 = ss.inc(s.copy(wk2[:, :], gs[p][:, :]))
                ss.inc(s.activation(wk1[:, :], xs[p][:, :], ACT.Identity,
                                    bias=nmr_cols[:, sc:sc + 1],
                                    scale=rsig_cols[:, sc:sc + 1]))
            s12 = ss.n

            @blk.gpsimd
            def _(g, sc=sc, p=p, wk2=wk2, s1=s12 - 1):
                ss.wait(g, s1)
                if sc < 2:
                    sv.wait(g, bcev)
                else:
                    sv.wait(g, v1v[sc - 2])
                sg.inc(g.tensor_mul(wk2[:, :], wk2[:, :], bc_rep[:, :]))
            g1v[sc] = sg.n

            @blk.vector
            def _(v, sc=sc, p=p, wk1=wk1, wk2=wk2, bst=bst):
                if sc == 0:
                    sdOG.wait(v)
                ss.wait(v, s12)
                sg.wait(v, g1v[sc])
                if sc >= 2:
                    ss.wait(v, s4v[sc - 2])
                iv = sv.inc(v.tensor_add(wk1[:, :], wk1[:, :], wk2[:, :]))
                v.wait_ge(sv.h, iv)
                for c in range(NF):
                    ist = v.bn_stats(bst[:, 6 * c:6 * c + 6],
                                     wk1[:, 512 * c:512 * (c + 1)])
                iv = sv.inc(ist)
                v.wait_ge(sv.h, iv)
                iv = sv.inc(v.bn_aggr(bst[:, 24:26], bst[:, 0:24]))
                v.wait_ge(sv.h, iv)
                sv.inc(v.tensor_scalar_add(bst[:, 25:26], bst[:, 25:26], 1e-6))
            v1v[sc] = sv.n

            @blk.scalar
            def _(s, sc=sc, bst=bst):
                sv.wait(s, v1v[sc])
                isx = ss.inc(s.activation(bst[:, 25:26], bst[:, 25:26], ACT.Ln))
                s.wait_ge(ss.h, isx)
                ss.inc(s.activation(bst[:, 26:27], bst[:, 25:26], ACT.Exp,
                                    scale=-0.5))
            s3v = ss.n

            @blk.vector
            def _(v, sc=sc, bst=bst, s3=s3v):
                ss.wait(v, s3)
                sv.inc(v.tensor_scalar(bst[:, 27:28], bst[:, 24:25],
                                       bst[:, 26:27], -1.0,
                                       ALU.mult, ALU.mult))
            v3v = sv.n

            @blk.scalar
            def _(s, sc=sc, wk1=wk1, bst=bst, v3=v3v):
                sv.wait(s, v3)
                ss.inc(s.activation(wk1[:, :], wk1[:, :], ACT.Identity,
                                    bias=bst[:, 27:28], scale=bst[:, 26:27]))
            s4v[sc] = ss.n

            @blk.gpsimd
            def _(g, sc=sc, wk1=wk1):
                ss.wait(g, s4v[sc])
                ig = sg.inc(g.tensor_mul(wk1[:, :], wk1[:, :], og[:, :]))
                g.wait_ge(sg.h, ig)
                sg.inc(g.tensor_add(wk1[:, :], wk1[:, :], ob[:, :]))
            g2v[sc] = sg.n

            @blk.vector
            def _(v, sc=sc, p=p, wk1=wk1):
                st.wait(v, mmv[sc])
                if sc >= 2:
                    sdO[p].wait(v, outv[sc - 2])
                iv = sv.inc(v.tensor_add(os_[p][:, :], xs[p][:, :],
                                         pp[p][:, :]))
                sg.wait(v, g2v[sc])
                v.wait_ge(sv.h, iv)
                sv.inc(v.tensor_add(os_[p][:, :], os_[p][:, :], wk1[:, :]))
            combv[sc] = sv.n

            @blk.sync
            def _(sy, sc=sc, p=p):
                sv.wait(sy, combv[sc])
                sdO[p].inc(sy.dma_start(
                    out=out_h[sc * 128:(sc + 1) * 128, :],
                    in_=os_[p][:, :]), 16)
            outv[sc] = sdO[p].n
            if sc + 2 < NT:
                loads(sc + 2)

        @blk.sync
        def _(sy):
            sdO[0].wait(sy)
            sdO[1].wait(sy)

    wo_ctx.__exit__(None, None, None)
    return nc


_CACHE = {}


def kernel(L0_attn_x, L0_attn_Wq, L0_attn_Wk, L0_attn_Wv, L0_attn_Wo,
           L0_ffn_in_gamma, L0_ffn_in_beta, L0_ffn_agg_w, L0_ffn_agg_b,
           L0_ffn_pn_gamma, L0_ffn_pn_beta, L0_ffn_d1_w, L0_ffn_d1_b,
           L0_ffn_d2_w, L0_ffn_d2_b, L0_ffn_bc_w, L0_ffn_bc_b,
           L0_ffn_gate_w, L0_ffn_gate_b, L0_ffn_out_gamma, L0_ffn_out_beta):
    f32 = np.float32
    X = np.ascontiguousarray(np.asarray(L0_attn_x, dtype=f32))
    wq = np.ascontiguousarray(np.asarray(L0_attn_Wq, dtype=f32))
    wk = np.ascontiguousarray(np.asarray(L0_attn_Wk, dtype=f32))
    wv = np.ascontiguousarray(np.asarray(L0_attn_Wv, dtype=f32))
    wo = np.ascontiguousarray(np.asarray(L0_attn_Wo, dtype=f32))
    gatew = np.ascontiguousarray(np.asarray(L0_ffn_gate_w, dtype=f32))
    aggw = np.ascontiguousarray(np.asarray(L0_ffn_agg_w, dtype=f32))
    bcw = np.ascontiguousarray(np.asarray(L0_ffn_bc_w, dtype=f32))
    d1w = np.ascontiguousarray(np.asarray(L0_ffn_d1_w, dtype=f32))
    d2w = np.ascontiguousarray(np.asarray(L0_ffn_d2_w, dtype=f32))

    wv_blk = np.ascontiguousarray(wv.reshape(D, NT, 128).transpose(1, 0, 2))
    lastrows = np.ascontiguousarray(X[:, D - 1, :])

    def rep8(vec, w=D):
        return np.ascontiguousarray(
            np.broadcast_to(np.asarray(vec, dtype=f32).reshape(1, w), (B, w)))

    in_g8 = rep8(L0_ffn_in_gamma)
    in_b8 = rep8(L0_ffn_in_beta)
    pn_g_fm = np.ascontiguousarray(
        np.asarray(L0_ffn_pn_gamma, dtype=f32).reshape(NT, 128).T)
    pn_b_fm = np.ascontiguousarray(
        np.asarray(L0_ffn_pn_beta, dtype=f32).reshape(NT, 128).T)
    agg_b8 = rep8(L0_ffn_agg_b)
    d2b_fm = np.ascontiguousarray(
        np.asarray(L0_ffn_d2_b, dtype=f32).reshape(NT, 128).T)
    bc_b8 = rep8(L0_ffn_bc_b)
    og128 = np.ascontiguousarray(np.broadcast_to(
        np.asarray(L0_ffn_out_gamma, dtype=f32).reshape(1, D), (128, D)))
    ob128 = np.ascontiguousarray(np.broadcast_to(
        np.asarray(L0_ffn_out_beta, dtype=f32).reshape(1, D), (128, D)))
    gate_b_row = np.ascontiguousarray(
        np.asarray(L0_ffn_gate_b, dtype=f32).reshape(1, D))
    d1b = np.asarray(L0_ffn_d1_b, dtype=f32)

    ident = np.eye(128, dtype=BFD)
    ident32 = np.eye(128, dtype=f32)
    ones_col = np.ones((128, 1), dtype=BFD)
    ones_col32 = np.ones((128, 1), dtype=f32)
    ones1 = np.ones((1, 128), dtype=BFD)
    ones1_32 = np.ones((1, 128), dtype=f32)

    if "nc" not in _CACHE:
        _CACHE["nc"] = build_nc()
    nc = _CACHE["nc"]

    in_maps = []
    for b in range(B):
        selb = np.zeros((B, 128), dtype=f32)
        selb[b, :] = 1.0
        in_maps.append({
            "x": X[b],
            "wq": wq, "wk": wk, "wo": wo, "gatew": gatew,
            "aggw_sl": np.ascontiguousarray(aggw[:, b * 256:(b + 1) * 256]),
            "bcw_sl": np.ascontiguousarray(bcw[:, b * 256:(b + 1) * 256]),
            "wv_blk": wv_blk,
            "d1s_pm": np.ascontiguousarray(
                d1w[:, b * HSL:(b + 1) * HSL].reshape(NT, 128, NHC, 128)),
            "d2s_pm": np.ascontiguousarray(
                d2w[b * HSL:(b + 1) * HSL, :].reshape(NHC, 128, NT, 128)),
            "lastrows": lastrows,
            "in_g8": in_g8, "in_b8": in_b8,
            "pn_g_fm": pn_g_fm, "pn_b_fm": pn_b_fm,
            "agg_bsl": np.ascontiguousarray(
                np.broadcast_to(agg_b8[0:1, b * 256:(b + 1) * 256], (B, 256))),
            "d1b_fm": np.ascontiguousarray(
                d1b[b * HSL:(b + 1) * HSL].reshape(NHC, 128).T),
            "d2b_fm": d2b_fm, "bc_b8": bc_b8,
            "og128": og128, "ob128": ob128, "gate_b_row": gate_b_row,
            "sel_rep": selb, "ident": ident, "ident32": ident32,
            "ones_col": ones_col, "ones_col32": ones_col32,
            "ones1": ones1, "ones1_32": ones1_32,
        })

    res = run_bass_kernel_spmd(nc, in_maps, core_ids=list(range(B)))
    out = np.stack([res.results[b]["out"] for b in range(B)], axis=0)
    return out


# version 11
# speedup vs baseline: 1.1015x; 1.0144x over previous
from contextlib import ExitStack

import numpy as np
import ml_dtypes
import concourse.bass as bass
import concourse.mybir as mybir
from concourse.bass_utils import run_bass_kernel_spmd

F32 = mybir.dt.float32
BF16 = mybir.dt.bfloat16
ACT = mybir.ActivationFunctionType
ALU = mybir.AluOpType
AX = mybir.AxisListType

D = 2048
NT = 16
NF = 4
B = 8
HSL = 1024
NHC = HSL // 128
KSTEPS = 4

BFD = ml_dtypes.bfloat16


class Sem:

    def __init__(self, nc, name):
        self.h = nc.alloc_semaphore(name)
        self.n = 0

    def inc(self, inst, k=1):
        inst.then_inc(self.h, k)
        self.n += k
        return self.n

    def wait(self, eng, val=None):
        eng.wait_ge(self.h, self.n if val is None else val)



def _dump_and_stop(nc, sbuf_list, dram_list=()):
    sdd = Sem(nc, "sdd")
    with nc.Block() as blk:
        @blk.sync
        def _(sy):
            for name, t in sbuf_list:
                hb = nc.dram_tensor("dbg_" + name, list(t.shape), t.dtype,
                                    kind="ExternalOutput")
                sdd.inc(sy.dma_start(out=hb[:, :], in_=t[:, :]), 16)
            for name, t in dram_list:
                hb = nc.dram_tensor("dbg_" + name, list(t.shape), t.dtype,
                                    kind="ExternalOutput")
                ap_in = t.ap() if hasattr(t, "ap") else t
                sdd.inc(sy.dma_start(out=hb.ap(), in_=t.ap()), 16)
            sdd.wait(sy)
    return nc


def build_nc(stop_after=None):
    nc = bass.Bass("TRN2", target_bir_lowering=False)

    x_h = nc.dram_tensor("x", [D, D], F32, kind="ExternalInput")
    wq_h = nc.dram_tensor("wq", [D, D], F32, kind="ExternalInput")
    wk_h = nc.dram_tensor("wk", [D, D], F32, kind="ExternalInput")
    wo_h = nc.dram_tensor("wo", [D, D], F32, kind="ExternalInput")
    gatew_h = nc.dram_tensor("gatew", [D, D], F32, kind="ExternalInput")
    aggw_h = nc.dram_tensor("aggw_sl", [D, 256], F32, kind="ExternalInput")
    bcw_h = nc.dram_tensor("bcw_sl", [D, 256], F32, kind="ExternalInput")
    wv_h = nc.dram_tensor("wv_blk", [NT, D, 128], F32, kind="ExternalInput")
    d1s_h = nc.dram_tensor("d1s_pm", [NT, 128, NHC, 128], F32,
                           kind="ExternalInput")
    d2s_h = nc.dram_tensor("d2s_pm", [NHC, 128, NT, 128], F32,
                           kind="ExternalInput")
    d1c_sp = nc.dram_tensor("d1c_sp", [NT, 128, NHC * 128], BF16)
    d2c_sp = nc.dram_tensor("d2c_sp", [NHC, 128, NT * 128], BF16)
    lastrows_h = nc.dram_tensor("lastrows", [B, D], F32, kind="ExternalInput")
    in_g8_h = nc.dram_tensor("in_g8", [B, D], F32, kind="ExternalInput")
    in_b8_h = nc.dram_tensor("in_b8", [B, D], F32, kind="ExternalInput")
    pn_g_fm_h = nc.dram_tensor("pn_g_fm", [128, NT], F32, kind="ExternalInput")
    pn_b_fm_h = nc.dram_tensor("pn_b_fm", [128, NT], F32, kind="ExternalInput")
    agg_bsl_h = nc.dram_tensor("agg_bsl", [B, 256], F32, kind="ExternalInput")
    d1b_fm_h = nc.dram_tensor("d1b_fm", [128, NHC], F32, kind="ExternalInput")
    d2b_fm_h = nc.dram_tensor("d2b_fm", [128, NT], F32, kind="ExternalInput")
    bc_b8_h = nc.dram_tensor("bc_b8", [B, D], F32, kind="ExternalInput")
    og128_h = nc.dram_tensor("og128", [128, D], F32, kind="ExternalInput")
    ob128_h = nc.dram_tensor("ob128", [128, D], F32, kind="ExternalInput")
    gate_b_h = nc.dram_tensor("gate_b_row", [1, D], F32, kind="ExternalInput")
    sel_h = nc.dram_tensor("sel_rep", [B, 128], F32, kind="ExternalInput")
    ident_h = nc.dram_tensor("ident", [128, 128], BF16, kind="ExternalInput")
    ident32_h = nc.dram_tensor("ident32", [128, 128], F32, kind="ExternalInput")
    onescol32_h = nc.dram_tensor("ones_col32", [128, 1], F32,
                                 kind="ExternalInput")
    ones1_32_h = nc.dram_tensor("ones1_32", [1, 128], F32, kind="ExternalInput")
    onescol_h = nc.dram_tensor("ones_col", [128, 1], BF16, kind="ExternalInput")
    ones1_h = nc.dram_tensor("ones1", [1, 128], BF16, kind="ExternalInput")

    out_h = nc.dram_tensor("out", [D, D], F32, kind="ExternalOutput")

    gate_sp = nc.dram_tensor("gate_sp", [D, D], BF16)
    wT_sp = nc.dram_tensor("wT_sp", [128, NT, D], BF16)
    attn_sp = nc.dram_tensor("attn_sp", [128, NT, D], BF16)
    cc_in = nc.dram_tensor("cc_in", [B, D], F32)
    cc_out = nc.dram_tensor("cc_out", [B, D], F32)
    ag_in = nc.dram_tensor("ag_in", [B, 256], F32)
    ag_out = nc.dram_tensor("ag_out", [B * B, 256], F32)
    ccf_in = nc.dram_tensor("ccf_in", [128, 128], F32)
    ccf_out = nc.dram_tensor("ccf_out", [128, 128], F32)

    ident = nc.alloc_sbuf_tensor("identS", [128, 128], BF16)
    ones_col = nc.alloc_sbuf_tensor("ones_colS", [128, 1], BF16)
    ones1 = nc.alloc_sbuf_tensor("ones1S", [1, 128], BF16)
    sel = nc.alloc_sbuf_tensor("selS", [B, 128], F32)
    mu_cols = nc.alloc_sbuf_tensor("mu_cols", [128, NT], F32)
    rsig_cols = nc.alloc_sbuf_tensor("rsig_cols", [128, NT], F32)
    r_cols = nc.alloc_sbuf_tensor("r_cols", [128, NT], F32)
    nmr_cols = nc.alloc_sbuf_tensor("nmr_cols", [128, NT], F32)
    bc_rep = nc.alloc_sbuf_tensor("bc_rep", [128, D], F32)

    sv = Sem(nc, "sv")
    st = Sem(nc, "st")
    ss = Sem(nc, "ss")
    sg = Sem(nc, "sg")
    scc = Sem(nc, "scc")

    sdT = Sem(nc, "sdT")
    swW = [Sem(nc, "swW0"), Sem(nc, "swW1")]
    swC = Sem(nc, "swC")
    swB = Sem(nc, "swB")

    thought_fm = nc.alloc_sbuf_tensor("thought_fm", [128, NT * B], F32)
    sq_fm = nc.alloc_sbuf_tensor("sq_fm", [128, NT * B], F32)
    id32s = nc.alloc_sbuf_tensor("id32s", [128, 128], F32)
    t1_bf = nc.alloc_sbuf_tensor("t1_bf", [128, NT * B], BF16)
    t2_bf = nc.alloc_sbuf_tensor("t2_bf", [128, NHC * B], BF16)
    th_bf = nc.alloc_sbuf_tensor("th_bf", [128, NT * B], BF16)
    mu8r = nc.alloc_sbuf_tensor("mu8r", [128, B], F32)
    rs8r = nc.alloc_sbuf_tensor("rs8r", [128, B], F32)
    strow = nc.alloc_sbuf_tensor("strow", [1, 32], F32)
    ccfm = nc.alloc_sbuf_tensor("ccfm", [128, 128], F32)
    pngf = nc.alloc_sbuf_tensor("pngf", [128, NT], F32)
    pnbf = nc.alloc_sbuf_tensor("pnbf", [128, NT], F32)
    d1bf = nc.alloc_sbuf_tensor("d1bf", [128, NHC], F32)
    d2bf = nc.alloc_sbuf_tensor("d2bf", [128, NT], F32)
    onc32 = nc.alloc_sbuf_tensor("onc32", [128, 1], F32)
    on1_32 = nc.alloc_sbuf_tensor("on1_32", [1, 128], F32)

    with ExitStack() as _ctx:
        _e = _ctx.enter_context
        t_last = _e(nc.sbuf_tensor("t_last", [B, D], F32))
        t_ing = _e(nc.sbuf_tensor("t_ing", [B, D], F32))
        t_inb = _e(nc.sbuf_tensor("t_inb", [B, D], F32))
        t_aggb = _e(nc.sbuf_tensor("t_aggb", [B, 256], F32))
        twork = _e(nc.sbuf_tensor("t_work", [B, D], F32))
        thought = _e(nc.sbuf_tensor("t_thought", [B, D], F32))
        tbf = _e(nc.sbuf_tensor("t_bf", [B, D], BF16))
        tfm = _e(nc.sbuf_tensor("t_fm", [128, NT * B], BF16))
        tstats = _e(nc.sbuf_tensor("t_stats", [B, 26], F32))
        wck0 = _e(nc.sbuf_tensor("t_wck0", [128, D], BF16))
        wck1 = _e(nc.sbuf_tensor("t_wck1", [128, D], BF16))
        ccst = _e(nc.sbuf_tensor("t_ccst", [B, D], F32))
        ptrT = [_e(nc.psum_tensor("t_ptr%d" % i, [128, B], BF16))
                for i in range(3)]
        pbig = _e(nc.psum_tensor("t_pbig", [B, D], F32))
        blk = _e(nc.Block())
        wck = [wck0, wck1]

        @blk.sync
        def _(sy):
            for dst, srcb in ((t_last, lastrows_h), (t_ing, in_g8_h),
                              (t_inb, in_b8_h), (t_aggb, agg_bsl_h),
                              (id32s, ident32_h)):
                sdT.inc(sy.dma_start(out=dst[:, :], in_=srcb[:, :]), 16)
            for dst, srcb in ((pngf, pn_g_fm_h), (pnbf, pn_b_fm_h),
                              (d1bf, d1b_fm_h), (d2bf, d2b_fm_h),
                              (onc32, onescol32_h), (on1_32, ones1_32_h)):
                sdT.inc(sy.dma_start(out=dst[:, :], in_=srcb[:, :]), 16)
            sdT.inc(sy.dma_start(out=ident[:, :], in_=ident_h[:, :]), 16)
            sdT.inc(sy.dma_start(out=ones_col[:, :], in_=onescol_h[:, :]), 16)
            sdT.inc(sy.dma_start(out=ones1[:, :], in_=ones1_h[:, :]), 16)
            sdT.inc(sy.dma_start(out=sel[:, :], in_=sel_h[:, :]), 16)
        small_loads = sdT.n

        def ln8(src_ap, dst_ap, W, eps, gamma=None, beta=None, pre_wait=None):
            nchunk = (W + 511) // 512

            @blk.vector
            def _(v):
                if pre_wait is not None:
                    pre_wait(v)
                sv.wait(v)
                for c in range(nchunk):
                    ist = v.bn_stats(tstats[:, 6 * c:6 * c + 6],
                                     src_ap[:, 512 * c:512 * (c + 1)])
                iv = sv.inc(ist)
                v.wait_ge(sv.h, iv)
                iv = sv.inc(v.bn_aggr(tstats[:, 24:26], tstats[:, 0:6 * nchunk]))
                v.wait_ge(sv.h, iv)
                sv.inc(v.tensor_scalar_add(tstats[:, 25:26], tstats[:, 25:26],
                                           eps))

            @blk.scalar
            def _(s):
                sv.wait(s)
                isx = ss.inc(s.activation(tstats[:, 25:26], tstats[:, 25:26],
                                          ACT.Ln))
                s.wait_ge(ss.h, isx)
                ss.inc(s.activation(tstats[:, 25:26], tstats[:, 25:26],
                                    ACT.Exp, scale=-0.5))

            @blk.vector
            def _(v):
                ss.wait(v)
                iv = sv.inc(v.tensor_scalar(dst_ap, src_ap, tstats[:, 24:25],
                                            tstats[:, 25:26], ALU.subtract,
                                            ALU.mult))
                if gamma is not None:
                    v.wait_ge(sv.h, iv)
                    iv = sv.inc(v.tensor_mul(dst_ap, dst_ap, gamma))
                if beta is not None:
                    v.wait_ge(sv.h, iv)
                    sv.inc(v.tensor_add(dst_ap, dst_ap, beta))

        def to_fm(src_ap, W, dst, dst_dtype_ps, idap):
            nchunk = W // 128

            @blk.vector
            def _(v):
                sv.wait(v)
                sv.inc(v.tensor_copy(tbf[:, 0:W], src_ap))
            ev_base = sv.n
            tr_base = st.n

            for g4 in range(0, nchunk, 4):
                @blk.tensor
                def _(t, g4=g4):
                    for c in range(g4, min(nchunk, g4 + 4)):
                        if c == 0:
                            sv.wait(t, ev_base)
                        if c >= 3:
                            sv.wait(t, ev_base + c - 2)
                        st.inc(t.transpose(ptrT[c % 3][:, :],
                                           tbf[:, 128 * c:128 * (c + 1)],
                                           ident[0:B, 0:B]))

                @blk.vector
                def _(v, g4=g4):
                    for c in range(g4, min(nchunk, g4 + 4)):
                        v.wait_ge(st.h, tr_base + c + 1)
                        v.tensor_copy(dst[:, B * c:B * (c + 1)],
                                      ptrT[c % 3][:, :]).then_inc(sv.h, 1)
                        sv.n += 1

        def mm8(whbm, nk, W, psum):
            nN = (W + 511) // 512
            load_val = {}
            mmv = {}
            st_base = st.n

            def emit_load(ck):
                @blk.gpsimd
                def _(g, ck=ck):
                    st.wait(g, mmv[ck - 2] if ck >= 2 else st_base)
                    swW[ck % 2].inc(
                        g.dma_start(out=wck[ck % 2][:, 0:W],
                                    in_=whbm[128 * ck:128 * (ck + 1), 0:W]), 16)
                load_val[ck] = swW[ck % 2].n

            emit_load(0)
            if nk > 1:
                emit_load(1)
            for ck in range(nk):
                @blk.tensor
                def _(t, ck=ck):
                    if ck == 0:
                        sv.wait(t)
                    swW[ck % 2].wait(t, load_val[ck])
                    for n in range(nN):
                        w1 = min(W, 512 * (n + 1))
                        mi = t.matmul(psum[:, 512 * n:w1],
                                      tfm[:, B * ck:B * (ck + 1)],
                                      wck[ck % 2][:, 512 * n:w1],
                                      start=(ck == 0), stop=(ck == nk - 1))
                    st.inc(mi)
                mmv[ck] = st.n
                if ck + 2 < nk:
                    emit_load(ck + 2)

        def w_small(v):
            sdT.wait(v, small_loads)
        ln8(t_last[:, :], twork[:, :], D, 1e-5, pre_wait=w_small)
        ln8(twork[:, :], thought[:, :], D, 1e-6, gamma=t_ing[:, :],
            beta=t_inb[:, :])
        to_fm(thought[:, :], D, tfm, None, None)
        mm8(aggw_h, NT, 256, pbig)

        @blk.vector
        def _(v):
            st.wait(v)
            iv = sv.inc(v.tensor_copy(ccst[:, 0:256], pbig[:, 0:256]))
            v.wait_ge(sv.h, iv)
            sv.inc(v.tensor_add(ccst[:, 0:256], ccst[:, 0:256], t_aggb[:, :]))

        @blk.gpsimd
        def _(g):
            sv.wait(g)
            swC.inc(g.dma_start(out=ag_in[:, :], in_=ccst[:, 0:256]), 16)
            swC.wait(g)
            scc.inc(g.collective_compute(
                "AllGather", ALU.bypass, replica_groups=[list(range(B))],
                ins=[ag_in.ap().opt()], outs=[ag_out.ap().opt()]))
    agval = scc.n

    pT_ctx = ExitStack()
    wckb_ctx = ExitStack()
    pTa = pTb = wckb = bc_tm = bc_b8s = None

    def fm_ln(blk):
        @blk.scalar
        def _(s):
            sv.wait(s)
            ss.inc(s.activation(sq_fm[:, :], thought_fm[:, :], ACT.Square))

        @blk.tensor
        def _(t):
            ss.wait(t)
            sv.wait(t)
            st.inc(t.matmul(pTa[0:1, 128:256], onc32[:, :], thought_fm[:, :],
                            start=True, stop=False))
            st.inc(t.matmul(pTa[0:1, 256:384], onc32[:, :], sq_fm[:, :],
                            start=False, stop=True))

        @blk.vector
        def _(v):
            st.wait(v)
            iv = sv.inc(v.reduce_sum(
                strow[0:1, 0:8],
                bass.AP(pTa.ap().tensor, 128, [[512, 1], [1, 8], [8, 16]]),
                axis=AX.X))
            v.wait_ge(sv.h, iv)
            iv = sv.inc(v.reduce_sum(
                strow[0:1, 8:16],
                bass.AP(pTa.ap().tensor, 256, [[512, 1], [1, 8], [8, 16]]),
                axis=AX.X))
            v.wait_ge(sv.h, iv)
            iv = sv.inc(v.tensor_scalar_mul(strow[0:1, 0:8], strow[0:1, 0:8],
                                            1.0 / D))
            v.wait_ge(sv.h, iv)
            iv = sv.inc(v.tensor_scalar_mul(strow[0:1, 8:16], strow[0:1, 8:16],
                                            1.0 / D))
            v.wait_ge(sv.h, iv)
            iv = sv.inc(v.tensor_mul(strow[0:1, 16:24], strow[0:1, 0:8],
                                     strow[0:1, 0:8]))
            v.wait_ge(sv.h, iv)
            iv = sv.inc(v.tensor_sub(strow[0:1, 8:16], strow[0:1, 8:16],
                                     strow[0:1, 16:24]))
            v.wait_ge(sv.h, iv)
            sv.inc(v.tensor_scalar_add(strow[0:1, 8:16], strow[0:1, 8:16],
                                       1e-6))

        @blk.scalar
        def _(s):
            sv.wait(s)
            isx = ss.inc(s.activation(strow[0:1, 8:16], strow[0:1, 8:16],
                                      ACT.Ln))
            s.wait_ge(ss.h, isx)
            ss.inc(s.activation(strow[0:1, 8:16], strow[0:1, 8:16],
                                ACT.Exp, scale=-0.5))

        @blk.vector
        def _(v):
            ss.wait(v)
            iv = sv.inc(v.tensor_mul(strow[0:1, 16:24], strow[0:1, 0:8],
                                     strow[0:1, 8:16]))
            v.wait_ge(sv.h, iv)
            sv.inc(v.tensor_scalar_mul(strow[0:1, 16:24], strow[0:1, 16:24],
                                       -1.0))

        @blk.tensor
        def _(t):
            sv.wait(t)
            st.inc(t.matmul(pTa[:, 384:392], on1_32[:, :], strow[0:1, 16:24],
                            start=True, stop=False))
            st.inc(t.matmul(pTa[:, 392:400], on1_32[:, :], strow[0:1, 8:16],
                            start=False, stop=True))

        @blk.vector
        def _(v):
            st.wait(v)
            iv = sv.inc(v.tensor_copy(mu8r[:, :], pTa[:, 384:392]))
            v.wait_ge(sv.h, iv)
            iv = sv.inc(v.tensor_copy(rs8r[:, :], pTa[:, 392:400]))
            v.wait_ge(sv.h, iv)
            ist = None
            for c in range(NT):
                ist = v.tensor_mul(sq_fm[:, B * c:B * (c + 1)],
                                   thought_fm[:, B * c:B * (c + 1)],
                                   rs8r[:, :])
            iv = sv.inc(ist)
            v.wait_ge(sv.h, iv)
            for c in range(NT):
                ist = v.tensor_add(sq_fm[:, B * c:B * (c + 1)],
                                   sq_fm[:, B * c:B * (c + 1)], mu8r[:, :])
            iv = sv.inc(ist)
            v.wait_ge(sv.h, iv)
            for c in range(NT):
                ist = v.tensor_scalar(t1_bf[:, B * c:B * (c + 1)],
                                      sq_fm[:, B * c:B * (c + 1)],
                                      pngf[:, c:c + 1], pnbf[:, c:c + 1],
                                      ALU.mult, ALU.add)
            sv.inc(ist)

    fmst = {}

    def fm_d1_loads(blk, k):
        @blk.gpsimd
        def _(g, k=k):
            st.wait(g, fmst.get(("d2m", k - 1), st.n))
            swP1.wait(g, NT * 16)
            for c in range(NT):
                swB.inc(g.dma_start(
                    out=wckb[:, c * 1024:(c + 1) * 1024],
                    in_=d1c_sp[c]), 16)
        fmst[("d1l", k)] = swB.n

    def fm_d1_mms(blk, k):
        @blk.tensor
        def _(t, k=k):
            sv.wait(t)
            swB.wait(t, fmst[("d1l", k)])
            for c in range(NT):
                for blkh in range(NHC):
                    mi = t.matmul(
                        pTa[:, blkh * 8:(blkh + 1) * 8],
                        wckb[:, c * 1024 + blkh * 128:
                             c * 1024 + blkh * 128 + 128],
                        t1_bf[:, B * c:B * (c + 1)],
                        start=(c == 0 and blkh == 0),
                        stop=(c == NT - 1 and blkh == NHC - 1))
            st.inc(mi)
        fmst[("d1m", k)] = st.n

        @blk.vector
        def _(v, k=k):
            st.wait(v, fmst[("d1m", k)])
            ist = None
            for blkh in range(NHC):
                ist = v.tensor_scalar_add(t2_bf[:, blkh * 8:(blkh + 1) * 8],
                                          pTa[:, blkh * 8:(blkh + 1) * 8],
                                          d1bf[:, blkh:blkh + 1])
            sv.inc(ist)

        @blk.scalar
        def _(s):
            sv.wait(s)
            ss.inc(s.activation(t2_bf[:, :], t2_bf[:, :], ACT.Gelu))
        fmst[("gelu", k)] = ss.n

    def fm_d2_loads(blk, k):
        @blk.gpsimd
        def _(g, k=k):
            st.wait(g, fmst[("d1m", k)])
            swP2.wait(g, NHC * 16)
            for c in range(NHC):
                swB.inc(g.dma_start(
                    out=wckb[:, c * 2048:(c + 1) * 2048],
                    in_=d2c_sp[c]), 16)
        fmst[("d2l", k)] = swB.n

    def fm_d2_mms(blk, k):
        @blk.tensor
        def _(t, k=k):
            ss.wait(t, fmst[("gelu", k)])
            swB.wait(t, fmst[("d2l", k)])
            for c in range(NHC):
                for blkd in range(NT):
                    mi = t.matmul(
                        pTb[:, blkd * 8:(blkd + 1) * 8],
                        wckb[:, c * 2048 + blkd * 128:
                             c * 2048 + blkd * 128 + 128],
                        t2_bf[:, B * c:B * (c + 1)],
                        start=(c == 0 and blkd == 0),
                        stop=(c == NHC - 1 and blkd == NT - 1))
            st.inc(mi)
        fmst[("d2m", k)] = st.n

        @blk.vector
        def _(v, k=k):
            st.wait(v, fmst[("d2m", k)])
            sv.inc(v.tensor_copy(ccfm[:, :], pTb[:, 0:128]))

        @blk.gpsimd
        def _(g):
            sv.wait(g)
            swC.inc(g.dma_start(out=ccf_in[:, :], in_=ccfm[:, :]), 16)
            swC.wait(g)
            scc.inc(g.collective_compute(
                "AllReduce", ALU.add, replica_groups=[list(range(B))],
                ins=[ccf_in.ap().opt()], outs=[ccf_out.ap().opt()]))
        fmst[("ar", k)] = scc.n

    def fm_update(blk, k):
        @blk.gpsimd
        def _(g, k=k):
            scc.wait(g, fmst[("ar", k)])
            swC.inc(g.dma_start(out=ccfm[:, :], in_=ccf_out[:, :]), 16)
        cw = swC.n

        @blk.vector
        def _(v, k=k, cw=cw):
            swC.wait(v, cw)
            iv = sv.inc(v.tensor_add(thought_fm[:, :], thought_fm[:, :],
                                     ccfm[:, :]))
            v.wait_ge(sv.h, iv)
            ist = None
            for c in range(NT):
                ist = v.tensor_scalar_add(thought_fm[:, B * c:B * (c + 1)],
                                          thought_fm[:, B * c:B * (c + 1)],
                                          d2bf[:, c:c + 1])
            sv.inc(ist)

    def fm_bc_a(blk):
        @blk.vector
        def _(v):
            sv.wait(v)
            sv.inc(v.tensor_copy(th_bf[:, :], thought_fm[:, :]))
        thbv = sv.n

        @blk.gpsimd
        def _(g):
            st.wait(g, fmst[("d2m", KSTEPS - 1)])
            for c in range(NT):
                swB.inc(g.dma_start(
                    out=wckb[:, c * 256:(c + 1) * 256],
                    in_=bcw_h[128 * c:128 * (c + 1), :]), 16)
        bcld = swB.n

        @blk.tensor
        def _(t):
            sv.wait(t, thbv)
            swB.wait(t, bcld)
            for c in range(NT):
                mi = t.matmul(pTa[0:B, 0:256], th_bf[:, B * c:B * (c + 1)],
                              wckb[:, c * 256:(c + 1) * 256],
                              start=(c == 0), stop=(c == NT - 1))
            st.inc(mi)

        @blk.vector
        def _(v):
            st.wait(v)
            sv.inc(v.tensor_copy(bc_tm[:, 0:256], pTa[0:B, 0:256]))

        @blk.gpsimd
        def _(g):
            sv.wait(g)
            swC.inc(g.dma_start(out=ag_in[:, :], in_=bc_tm[:, 0:256]), 16)
            swC.wait(g)
            scc.inc(g.collective_compute(
                "AllGather", ALU.bypass, replica_groups=[list(range(B))],
                ins=[ag_in.ap().opt()], outs=[ag_out.ap().opt()]))

    def fm_bc_b(blk):
        @blk.gpsimd
        def _(g):
            scc.wait(g)
            for r in range(B):
                swC.inc(g.dma_start(out=bc_tm[:, 256 * r:256 * (r + 1)],
                                    in_=ag_out[B * r:B * (r + 1), :]), 16)

        @blk.vector
        def _(v):
            swC.wait(v)
            sdT.wait(v)
            sv.inc(v.tensor_add(bc_tm[:, :], bc_tm[:, :], bc_b8s[:, :]))

    seam_plan = {}
    for k in range(KSTEPS):
        base = 6 * k
        seam_plan.setdefault(base + 0, []).append(
            lambda blk, k=k: fm_d1_loads(blk, k))
        seam_plan.setdefault(base + 2, []).append(
            lambda blk, k=k: (fm_update(blk, k - 1) if k > 0 else None,
                              fm_ln(blk)))
        seam_plan.setdefault(base + 3, []).append(
            lambda blk, k=k: (fm_d1_mms(blk, k), fm_d2_loads(blk, k)))
        seam_plan.setdefault(base + 6, []).append(
            lambda blk, k=k: fm_d2_mms(blk, k))
    seam_plan.setdefault(26, []).append(
        lambda blk: fm_update(blk, KSTEPS - 1))
    seam_plan.setdefault(27, []).append(fm_bc_a)
    seam_plan.setdefault(30, []).append(fm_bc_b)
    seam_i = [0]

    def fm_seam(blk):
        for fn in seam_plan.get(seam_i[0], []):
            fn(blk)
        seam_i[0] += 1

    sdX = [Sem(nc, "sdX0"), Sem(nc, "sdX1")]
    swS = [Sem(nc, "swS0"), Sem(nc, "swS1")]
    swP1 = Sem(nc, "swP1")
    swP2 = Sem(nc, "swP2")
    xn_ctx = nc.sbuf_tensor("xn_fm", [128, NT * D], BF16)
    xn_fm = xn_ctx.__enter__()
    q_ctx = nc.sbuf_tensor("q_tm", [128, NT * D], BF16)
    q_tm = q_ctx.__enter__()
    with ExitStack() as _ctx:
        _e = _ctx.enter_context
        xt0 = _e(nc.sbuf_tensor("b1_x0", [128, D], F32))
        xt1 = _e(nc.sbuf_tensor("b1_x1", [128, D], F32))
        xb0 = _e(nc.sbuf_tensor("b1_bf0", [128, D], BF16))
        xb1 = _e(nc.sbuf_tensor("b1_bf1", [128, D], BF16))
        bst = _e(nc.sbuf_tensor("b1_st", [128, 26], F32))
        w0 = _e(nc.sbuf_tensor("b2_w0", [128, NT * 512], BF16))
        w1 = _e(nc.sbuf_tensor("b2_w1", [128, NT * 512], BF16))
        tp0 = _e(nc.psum_tensor("b1_p0", [128, 128], BF16))
        tp1 = _e(nc.psum_tensor("b1_p1", [128, 128], BF16))
        tp2 = _e(nc.psum_tensor("b1_p2", [128, 128], BF16))
        tp3 = _e(nc.psum_tensor("b1_p3", [128, 128], BF16))
        p0 = _e(nc.psum_tensor("b2_p0", [128, 512], F32))
        p1 = _e(nc.psum_tensor("b2_p1", [128, 512], F32))
        th_stage = _e(nc.sbuf_tensor("b1_thstage", [B, D], F32))
        ptf2 = _e(nc.psum_tensor("b1_ptf2", [128, B], F32))
        blk = _e(nc.Block())
        xts = [xt0, xt1]
        xbs = [xb0, xb1]
        tps = [tp0, tp1, tp2, tp3]
        ws = [w0, w1]
        ps = [p0, p1]
        load_val = {}
        apply_val = {}
        tr_val = {}
        ev_val = {}
        wload = {}
        mmv = {}
        evv = {}

        def wslab_load(fs, whbm):
            @blk.gpsimd
            def _(g, fs=fs):
                if fs >= 2:
                    st.wait(g, mmv[(fs - 2, NT - 1)])
                for c in range(NT):
                    swS[fs % 2].inc(g.dma_start(
                        out=ws[fs % 2][:, c * 512:(c + 1) * 512],
                        in_=whbm[c * 128:(c + 1) * 128,
                                 fs * 512:(fs + 1) * 512]), 16)
            wload[fs] = swS[fs % 2].n

        def emit_load(tt):
            @blk.sync
            def _(sy, tt=tt):
                if tt >= 2:
                    sv.wait(sy, apply_val[tt - 2])
                for q in range(4):
                    sdX[tt % 2].inc(sy.dma_start(
                        out=xts[tt % 2][:, 512 * q:512 * (q + 1)],
                        in_=x_h[128 * tt:128 * (tt + 1),
                                512 * q:512 * (q + 1)]), 16)
            load_val[tt] = sdX[tt % 2].n

        def b1_tile(tt):
            @blk.vector
            def _(v, tt=tt):
                sdX[tt % 2].wait(v, load_val[tt])
                if tt >= 2:
                    st.wait(v, tr_val.get((tt - 2, 15), 0))
                sv.wait(v)
                xt = xts[tt % 2]
                for c in range(NF):
                    ist = v.bn_stats(bst[:, 6 * c:6 * c + 6],
                                     xt[:, 512 * c:512 * (c + 1)])
                iv = sv.inc(ist)
                v.wait_ge(sv.h, iv)
                iv = sv.inc(v.bn_aggr(bst[:, 24:26], bst[:, 0:24]))
                v.wait_ge(sv.h, iv)
                v.tensor_copy(mu_cols[:, tt:tt + 1], bst[:, 24:25])
                sv.inc(v.tensor_scalar_add(bst[:, 25:26], bst[:, 25:26], 1e-5))

            @blk.scalar
            def _(s, tt=tt):
                sv.wait(s)
                isx = ss.inc(s.activation(bst[:, 25:26], bst[:, 25:26], ACT.Ln))
                s.wait_ge(ss.h, isx)
                ss.inc(s.activation(rsig_cols[:, tt:tt + 1], bst[:, 25:26],
                                    ACT.Exp, scale=-0.5))

            @blk.vector
            def _(v, tt=tt):
                ss.wait(v)
                v.tensor_scalar(nmr_cols[:, tt:tt + 1], mu_cols[:, tt:tt + 1],
                                rsig_cols[:, tt:tt + 1], -1.0,
                                ALU.mult, ALU.mult)
                sv.inc(v.tensor_scalar(xbs[tt % 2][:, :], xts[tt % 2][:, :],
                                       mu_cols[:, tt:tt + 1],
                                       rsig_cols[:, tt:tt + 1],
                                       ALU.subtract, ALU.mult))
            apply_val[tt] = sv.n
            if tt + 2 < NT:
                emit_load(tt + 2)

            for grp in range(4):
                @blk.tensor
                def _(t, tt=tt, grp=grp):
                    if grp == 0:
                        sv.wait(t, apply_val[tt])
                    for j in range(4):
                        c = grp * 4 + j
                        if (tt, c) >= (0, 4):
                            psem, pval = ev_val.get(
                                (tt, c - 4), ev_val.get((tt - 1, c + 12),
                                                        (ss, 0)))
                            psem.wait(t, pval)
                        st.inc(t.transpose(tps[c % 4][:, :],
                                           xbs[tt % 2][:, 128 * c:128 * (c + 1)],
                                           ident[:, :]))
                        tr_val[(tt, c)] = st.n

                @blk.scalar
                def _(s, tt=tt, grp=grp):
                    for j in range(0, 4, 2):
                        c = grp * 4 + j
                        st.wait(s, tr_val[(tt, c)])
                        ss.inc(s.activation(
                            bass.AP(xn_fm, c * D + tt * 128,
                                    [[NT * D, 128], [1, 128]]),
                            tps[c % 4][:, :], ACT.Identity))
                        ev_val[(tt, c)] = (ss, ss.n)

                @blk.vector
                def _(v, tt=tt, grp=grp):
                    for j in range(1, 4, 2):
                        c = grp * 4 + j
                        st.wait(v, tr_val[(tt, c)])
                        sv.inc(v.tensor_copy(
                            bass.AP(xn_fm, c * D + tt * 128,
                                    [[NT * D, 128], [1, 128]]),
                            tps[c % 4][:, :]))
                        ev_val[(tt, c)] = (sv, sv.n)

        def b2_tile(fs, sc):
            @blk.tensor
            def _(t, fs=fs, sc=sc):
                if sc == 0:
                    swS[fs % 2].wait(t, wload[fs])
                xsem, xval = ev_val[(sc, 15)]
                xsem.wait(t, xval)
                e0sem, e0val = ev_val[(sc, 14)]
                e0sem.wait(t, e0val)
                idx = fs * NT + sc
                if idx >= 2:
                    sv.wait(t, evv[idx - 2])
                for e in range(NT):
                    mi = t.matmul(
                        ps[idx % 2][:, :],
                        bass.AP(xn_fm, e * D + sc * 128,
                                [[NT * D, 128], [1, 128]]),
                        ws[fs % 2][:, e * 512:(e + 1) * 512],
                        start=(e == 0), stop=(e == NT - 1))
                st.inc(mi)
            mmv[(fs, sc)] = st.n

            @blk.vector
            def _(v, fs=fs, sc=sc):
                idx = fs * NT + sc
                st.wait(v, mmv[(fs, sc)])
                sv.inc(v.tensor_copy(
                    q_tm[:, sc * D + fs * 512: sc * D + fs * 512 + 512],
                    ps[idx % 2][:, :]))
            evv[fs * NT + sc] = sv.n

        def fm_init():
            @blk.gpsimd
            def _(g):
                scc.wait(g, agval)
                for r in range(B):
                    swC.inc(g.dma_start(
                        out=th_stage[:, 256 * r:256 * (r + 1)],
                        in_=ag_out[B * r:B * (r + 1), :]), 16)
            reasm = swC.n
            trn = {}
            ev_base = sv.n
            for g4 in range(0, NT, 4):
                @blk.tensor
                def _(t, g4=g4):
                    if g4 == 0:
                        swC.wait(t, reasm)
                    for c in range(g4, min(NT, g4 + 4)):
                        if c > 0:
                            sv.wait(t, ev_base + c)
                        st.inc(t.transpose(ptf2[:, :],
                                           th_stage[:, 128 * c:128 * (c + 1)],
                                           id32s[0:B, 0:B]))
                        trn[c] = st.n

                @blk.vector
                def _(v, g4=g4):
                    for c in range(g4, min(NT, g4 + 4)):
                        st.wait(v, trn[c])
                        v.tensor_copy(thought_fm[:, B * c:B * (c + 1)],
                                      ptf2[:, :]).then_inc(sv.h, 1)
                        sv.n += 1
                        assert sv.n == ev_base + c + 1

        wslab_load(0, wq_h)
        wslab_load(1, wq_h)
        emit_load(0)
        emit_load(1)
        def precast():
            @blk.gpsimd
            def _(g):
                for c in range(NT):
                    swP1.inc(g.dma_start(
                        out=d1c_sp[c],
                        in_=d1s_h[c].rearrange("p c f -> p (c f)")), 16)
                for c in range(NHC):
                    swP2.inc(g.dma_start(
                        out=d2c_sp[c],
                        in_=d2s_h[c].rearrange("p c f -> p (c f)")), 16)

        for tt in range(NT):
            b1_tile(tt)
            if tt == 2:
                precast()
            if tt == 6:
                fm_init()
            if tt >= 1:
                b2_tile(0, tt - 1)
        b2_tile(0, NT - 1)
        wslab_load(2, wq_h)
        for fs in range(1, NF):
            for sc in range(NT):
                b2_tile(fs, sc)
            if fs + 2 < NF:
                wslab_load(fs + 2, wq_h)

    if stop_after == "B2":
        return _dump_and_stop(nc, [("mu_cols", mu_cols), ("rsig_cols", rsig_cols),
                                   ("xn_fm", xn_fm), ("q_tm", q_tm)])

    sdG = [Sem(nc, "sdG0"), Sem(nc, "sdG1")]
    swGB = Sem(nc, "swGB")
    with ExitStack() as _ctx:
        _e = _ctx.enter_context
        w0 = _e(nc.sbuf_tensor("b25_w0", [128, NT * 512], BF16))
        w1 = _e(nc.sbuf_tensor("b25_w1", [128, NT * 512], BF16))
        g0 = _e(nc.sbuf_tensor("b25_g0", [128, 512], BF16))
        gate_b_bf = _e(nc.sbuf_tensor("gate_b_bfS", [1, D], BF16))
        g1 = _e(nc.sbuf_tensor("b25_g1", [128, 512], BF16))
        p0 = _e(nc.psum_tensor("b25_p0", [128, 512], F32))
        p1 = _e(nc.psum_tensor("b25_p1", [128, 512], F32))
        blk = _e(nc.Block())
        ws = [w0, w1]
        gs = [g0, g1]
        ps = [p0, p1]
        wload = {}
        mmv = {}
        actv = {}
        spv = {}

        @blk.gpsimd
        def _(g):
            swGB.inc(g.dma_start(out=gate_b_bf[:, :], in_=gate_b_h[:, :]), 16)

        def wslab_load(fs):
            @blk.gpsimd
            def _(g, fs=fs):
                if fs >= 2:
                    st.wait(g, mmv[(fs - 2, NT - 1)])
                for c in range(NT):
                    swS[fs % 2].inc(g.dma_start(
                        out=ws[fs % 2][:, c * 512:(c + 1) * 512],
                        in_=gatew_h[c * 128:(c + 1) * 128,
                                 fs * 512:(fs + 1) * 512]), 16)
            wload[fs] = swS[fs % 2].n

        wslab_load(0)
        wslab_load(1)
        for fs in range(NF):
            for sc in range(NT):
                idx = fs * NT + sc

                @blk.tensor
                def _(t, fs=fs, sc=sc, idx=idx):
                    if idx == 0:
                        swGB.wait(t)
                    if sc == 0:
                        swS[fs % 2].wait(t, wload[fs])
                    if idx >= 2:
                        ss.wait(t, actv[idx - 2])
                    for e in range(NT):
                        t.matmul(
                            ps[idx % 2][:, :],
                            bass.AP(xn_fm, e * D + sc * 128,
                                    [[NT * D, 128], [1, 128]]),
                            ws[fs % 2][:, e * 512:(e + 1) * 512],
                            start=(e == 0), stop=False)
                    st.inc(t.matmul(ps[idx % 2][:, :], ones1[:, :],
                                    gate_b_bf[:, fs * 512:(fs + 1) * 512],
                                    start=False, stop=True))
                mmv[(fs, sc)] = st.n

                @blk.scalar
                def _(s, fs=fs, sc=sc, idx=idx):
                    st.wait(s, mmv[(fs, sc)])
                    if idx >= 2:
                        sdG[idx % 2].wait(s, spv[idx - 2])
                    ss.inc(s.activation(gs[idx % 2][:, :], ps[idx % 2][:, :],
                                        ACT.Sigmoid))
                actv[idx] = ss.n

                @blk.sync
                def _(sy, fs=fs, sc=sc, idx=idx):
                    ss.wait(sy, actv[idx])
                    sdG[idx % 2].inc(sy.dma_start(
                        out=gate_sp[sc * 128:(sc + 1) * 128,
                                    fs * 512:(fs + 1) * 512],
                        in_=gs[idx % 2][:, :]), 16)
                spv[idx] = sdG[idx % 2].n
            if fs + 2 < NF:
                wslab_load(fs + 2)

        @blk.sync
        def _(sy):
            sdG[0].wait(sy)
            sdG[1].wait(sy)

    if stop_after == "B25":
        return _dump_and_stop(nc, [], [("gate_sp", gate_sp)])

    sdW = [Sem(nc, "sdW0"), Sem(nc, "sdW1"), Sem(nc, "sdW2")]
    with ExitStack() as _ctx:
        _e = _ctx.enter_context
        w0 = _e(nc.sbuf_tensor("b3_w0", [128, NT * 512], BF16))
        w1 = _e(nc.sbuf_tensor("b3_w1", [128, NT * 512], BF16))
        kslab = _e(nc.sbuf_tensor("b3_k", [128, NT * 512], BF16))
        wt0 = _e(nc.sbuf_tensor("b3_wt0", [128, D], BF16))
        wt1 = _e(nc.sbuf_tensor("b3_wt1", [128, D], BF16))
        wt2 = _e(nc.sbuf_tensor("b3_wt2", [128, D], BF16))
        l_cols = _e(nc.sbuf_tensor("b3_l", [128, NT], F32))
        pk0 = _e(nc.psum_tensor("b3_pk0", [128, 512], F32))
        pk1 = _e(nc.psum_tensor("b3_pk1", [128, 512], F32))
        ps0 = _e(nc.psum_tensor("b3_ps0", [128, 512], F32))
        ps1 = _e(nc.psum_tensor("b3_ps1", [128, 512], F32))
        ps2 = _e(nc.psum_tensor("b3_ps2", [128, 512], F32))
        pl = _e(nc.psum_tensor("b3_pl", [128, NT], F32))
        blk = _e(nc.Block())
        ws = [w0, w1]
        pks = [pk0, pk1]
        pss = [ps0, ps1, ps2]
        wts = [wt0, wt1, wt2]
        wload = {}
        kmmv = {}
        kevv = {}
        smmv = {}
        expv = {}
        lmmv = {}
        spv = {}

        def wslab_load(fs):
            @blk.gpsimd
            def _(g, fs=fs):
                if fs >= 2:
                    st.wait(g, kmmv[(fs - 2, NT - 1)])
                for c in range(NT):
                    swS[fs % 2].inc(g.dma_start(
                        out=ws[fs % 2][:, c * 512:(c + 1) * 512],
                        in_=wk_h[c * 128:(c + 1) * 128,
                                 fs * 512:(fs + 1) * 512]), 16)
            wload[fs] = swS[fs % 2].n

        wslab_load(0)
        wslab_load(1)
        for fs in range(NF):
            for sc in range(NT):
                @blk.tensor
                def _(t, fs=fs, sc=sc):
                    if sc == 0:
                        swS[fs % 2].wait(t, wload[fs])
                    idx = fs * NT + sc
                    if idx >= 2:
                        sv.wait(t, kevv[idx - 2])
                    for e in range(NT):
                        mi = t.matmul(
                            pks[idx % 2][:, :],
                            bass.AP(xn_fm, e * D + sc * 128,
                                    [[NT * D, 128], [1, 128]]),
                            ws[fs % 2][:, e * 512:(e + 1) * 512],
                            start=(e == 0), stop=(e == NT - 1))
                    st.inc(mi)
                kmmv[(fs, sc)] = st.n

                @blk.vector
                def _(v, fs=fs, sc=sc):
                    idx = fs * NT + sc
                    st.wait(v, kmmv[(fs, sc)])
                    sv.inc(v.tensor_copy(kslab[:, sc * 512:(sc + 1) * 512],
                                         pks[idx % 2][:, :]))
                kevv[fs * NT + sc] = sv.n
            kslab_ready = sv.n
            if fs + 2 < NF:
                wslab_load(fs + 2)

            for fb in range(4):
                gfb = fs * 4 + fb
                for eN in range(NF):
                    gidx = gfb * NF + eN

                    def emit_lmm(gidx2):
                        gfb2, eN2 = gidx2 // NF, gidx2 % NF

                        @blk.tensor
                        def _(t, gfb2=gfb2, eN2=eN2, gidx2=gidx2):
                            ss.wait(t, expv[gidx2])
                            for cc in range(4):
                                ec = eN2 * 4 + cc
                                mi = t.matmul(
                                    pl[:, ec:ec + 1],
                                    wts[gfb2 % 3][:, ec * 128:(ec + 1) * 128],
                                    ones_col[:, :],
                                    start=(gfb2 == 0 and ec == 0),
                                    stop=(gfb2 == NT - 1 and ec == NT - 1))
                            st.inc(mi)
                        lmmv[gidx2] = st.n

                    @blk.tensor
                    def _(t, fs=fs, fb=fb, eN=eN, gidx=gidx,
                          kready=kslab_ready):
                        if fb == 0 and eN == 0:
                            sv.wait(t, kready)
                        if gidx >= 3:
                            ss.wait(t, expv.get(gidx - 3, 0))
                        for sc in range(NT):
                            mi = t.matmul(
                                pss[gidx % 3][:, :],
                                kslab[:, sc * 512 + fb * 128:
                                      sc * 512 + fb * 128 + 128],
                                q_tm[:, sc * D + eN * 512:
                                     sc * D + eN * 512 + 512],
                                start=(sc == 0), stop=(sc == NT - 1))
                        st.inc(mi)
                    smmv[gidx] = st.n

                    @blk.scalar
                    def _(s, gfb=gfb, eN=eN, gidx=gidx):
                        st.wait(s, smmv[gidx])
                        if gfb >= 3:
                            st.wait(s, lmmv.get((gfb - 3) * NF + eN, 0))
                            sdW[(gfb - 3) % 3].wait(s, spv.get(gfb - 3, 0))
                        ss.inc(s.activation(
                            wts[gfb % 3][:, eN * 512:(eN + 1) * 512],
                            pss[gidx % 3][:, :], ACT.Exp, scale=0.125))
                    expv[gidx] = ss.n

                    if gidx >= 2:
                        emit_lmm(gidx - 2)
                    if gidx == NT * NF - 1:
                        emit_lmm(gidx - 1)
                        emit_lmm(gidx)

                @blk.sync
                def _(sy, gfb=gfb):
                    ss.wait(sy, expv[gfb * NF + NF - 1])
                    sdW[gfb % 3].inc(sy.dma_start(out=wT_sp[:, gfb, :],
                                                  in_=wts[gfb % 3][:, :]), 16)
                spv[gfb] = sdW[gfb % 3].n

        @blk.vector
        def _(v):
            st.wait(v)
            iv = sv.inc(v.tensor_copy(l_cols[:, :], pl[:, :]))
            v.wait_ge(sv.h, iv)
            sv.inc(v.reciprocal(r_cols[:, :], l_cols[:, :]))

        @blk.sync
        def _(sy):
            sdW[0].wait(sy)
            sdW[1].wait(sy)
            sdW[2].wait(sy)

    q_ctx.__exit__(None, None, None)
    if stop_after == "B3":
        return _dump_and_stop(nc, [("r_cols", r_cols)], [("wT_sp", wT_sp)])

    pTa = pT_ctx.enter_context(nc.psum_tensor("pTa", [128, 512], F32))
    pTb = pT_ctx.enter_context(nc.psum_tensor("pTb", [128, 512], F32))
    bc_tm = nc.alloc_sbuf_tensor("bc_tm", [B, D], F32, side="right")
    bc_b8s = nc.alloc_sbuf_tensor("bc_b8s", [B, D], F32, side="right")
    wckb = wckb_ctx.enter_context(
        nc.sbuf_tensor("wckb", [128, 16384], BF16, side="right"))
    swV = [Sem(nc, "swV0"), Sem(nc, "swV1")]
    v_ctx = nc.sbuf_tensor("v_fm", [128, NT * D], BF16, side="right")
    v_fm = v_ctx.__enter__()
    with ExitStack() as _ctx:
        _e = _ctx.enter_context
        w0 = _e(nc.sbuf_tensor("b35_w0", [128, NT * 128], BF16))
        w1 = _e(nc.sbuf_tensor("b35_w1", [128, NT * 128], BF16))
        p0 = _e(nc.psum_tensor("b35_p0", [128, 512], F32))
        p1 = _e(nc.psum_tensor("b35_p1", [128, 512], F32))
        blk = _e(nc.Block())
        ws = [w0, w1]
        ps = [p0, p1]
        wload = {}
        mmv = {}
        evv = {}

        def wcol_load(fb):
            @blk.gpsimd
            def _(g, fb=fb):
                if fb >= 2:
                    st.wait(g, mmv[(fb - 2, NF - 1)])
                for c in range(NT):
                    swV[fb % 2].inc(g.dma_start(
                        out=ws[fb % 2][:, c * 128:(c + 1) * 128],
                        in_=wv_h[fb, c * 128:(c + 1) * 128, :]), 16)
            wload[fb] = swV[fb % 2].n

        @blk.sync
        def _(sy):
            sdT.inc(sy.dma_start(out=bc_b8s[:, :], in_=bc_b8_h[:, :]), 16)

        wcol_load(0)
        wcol_load(1)
        for fb in range(NT):
            for sN in range(NF):
                @blk.tensor
                def _(t, fb=fb, sN=sN):
                    if sN == 0:
                        swV[fb % 2].wait(t, wload[fb])
                    idx = fb * NF + sN
                    if idx >= 2:
                        sv.wait(t, evv[idx - 2])
                    for e in range(NT):
                        mi = t.matmul(
                            ps[idx % 2][:, :],
                            ws[fb % 2][:, e * 128:(e + 1) * 128],
                            bass.AP(xn_fm, e * D + sN * 512,
                                    [[NT * D, 128], [1, 512]]),
                            start=(e == 0), stop=(e == NT - 1))
                    st.inc(mi)
                mmv[(fb, sN)] = st.n

                @blk.vector
                def _(v, fb=fb, sN=sN):
                    idx = fb * NF + sN
                    st.wait(v, mmv[(fb, sN)])
                    sv.inc(v.tensor_copy(
                        v_fm[:, fb * D + sN * 512: fb * D + sN * 512 + 512],
                        ps[idx % 2][:, :]))
                evv[fb * NF + sN] = sv.n
            if fb + 2 < NT:
                wcol_load(fb + 2)
            fm_seam(blk)

    xn_ctx.__exit__(None, None, None)
    if stop_after == "B35":
        return _dump_and_stop(nc, [("v_fm", v_fm)])

    swO = Sem(nc, "swO")
    sdC = [Sem(nc, "sdC0"), Sem(nc, "sdC1")]
    sdA = [Sem(nc, "sdA0"), Sem(nc, "sdA1")]
    wo_ctx = nc.sbuf_tensor("wo_bf", [128, NT * D], BF16)
    wo_bf = wo_ctx.__enter__()
    with ExitStack() as _ctx:
        _e = _ctx.enter_context
        c0 = _e(nc.sbuf_tensor("b4_c0", [128, NT * 128], BF16))
        c1 = _e(nc.sbuf_tensor("b4_c1", [128, NT * 128], BF16))
        a0 = _e(nc.sbuf_tensor("b4_a0", [128, D], BF16))
        a1 = _e(nc.sbuf_tensor("b4_a1", [128, D], BF16))
        pp = [_e(nc.psum_tensor("b4_p%d" % i, [128, 1024], F32))
              for i in range(3)]
        blk = _e(nc.Block())
        cs = [c0, c1]
        asb = [a0, a1]
        cload = {}
        mmv = {}
        evv = {}
        spv = {}

        @blk.gpsimd
        def _(g):
            for c in range(NT):
                swO.inc(g.dma_start(
                    out=wo_bf[:, c * D:(c + 1) * D],
                    in_=wo_h[c * 128:(c + 1) * 128, :]), 16)
        wo_loaded = swO.n

        def col_load(ec):
            @blk.sync
            def _(sy, ec=ec):
                if ec >= 2:
                    st.wait(sy, hmm[2 * (ec - 2) + 1])
                sdC[ec % 2].inc(sy.dma_start(
                    out=cs[ec % 2][:, :].rearrange("p (c f) -> p c f", f=128),
                    in_=wT_sp[:, :, ec * 128:(ec + 1) * 128]), 16)
            cload[ec] = sdC[ec % 2].n

        col_load(0)
        col_load(1)
        hmm = {}
        hev = {}
        for ec in range(NT):
            for half in range(2):
                hid = 2 * ec + half

                @blk.tensor
                def _(t, ec=ec, half=half, hid=hid):
                    if half == 0:
                        sdC[ec % 2].wait(t, cload[ec])
                    if hid >= 3:
                        sv.wait(t, hev[hid - 3])
                    pr = pp[hid % 3]
                    for s2 in range(2):
                        sN = half * 2 + s2
                        for fb in range(NT):
                            mi = t.matmul(
                                pr[:, s2 * 512:(s2 + 1) * 512],
                                cs[ec % 2][:, fb * 128:(fb + 1) * 128],
                                v_fm[:, fb * D + sN * 512:
                                     fb * D + sN * 512 + 512],
                                start=(fb == 0), stop=(fb == NT - 1))
                        st.inc(mi)
                hmm[hid] = st.n

                @blk.vector
                def _(v, ec=ec, half=half, hid=hid):
                    st.wait(v, hmm[hid])
                    if half == 0 and ec >= 2:
                        sdA[ec % 2].wait(v, spv[ec - 2])
                    sv.inc(v.tensor_scalar_mul(
                        asb[ec % 2][:, half * 1024:(half + 1) * 1024],
                        pp[hid % 3][:, :], r_cols[:, ec:ec + 1]))
                hev[hid] = sv.n

            @blk.sync
            def _(sy, ec=ec):
                sv.wait(sy, hev[2 * ec + 1])
                sdA[ec % 2].inc(sy.dma_start(out=attn_sp[:, ec, :],
                                             in_=asb[ec % 2][:, :]), 16)
            spv[ec] = sdA[ec % 2].n
            if ec + 2 < NT:
                col_load(ec + 2)
            fm_seam(blk)

        @blk.sync
        def _(sy):
            sdA[0].wait(sy)
            sdA[1].wait(sy)

    v_ctx.__exit__(None, None, None)
    wckb_ctx.close()
    pT_ctx.close()
    if stop_after == "B4":
        return _dump_and_stop(nc, [], [("attn_sp", attn_sp)])

    sdL = [Sem(nc, "sdL0"), Sem(nc, "sdL1")]
    sdO = [Sem(nc, "sdO0"), Sem(nc, "sdO1")]
    sdOG = Sem(nc, "sdOG")
    with ExitStack() as _ctx:
        _e = _ctx.enter_context
        og = _e(nc.sbuf_tensor("b6_og", [128, D], F32))
        ob = _e(nc.sbuf_tensor("b6_ob", [128, D], F32))
        x0 = _e(nc.sbuf_tensor("b6_x0", [128, D], F32))
        x1 = _e(nc.sbuf_tensor("b6_x1", [128, D], F32))
        g0 = _e(nc.sbuf_tensor("b6_g0", [128, D], BF16))
        g1 = _e(nc.sbuf_tensor("b6_g1", [128, D], BF16))
        ac0 = _e(nc.sbuf_tensor("b6_ac0", [128, NT * 128], BF16))
        ac1 = _e(nc.sbuf_tensor("b6_ac1", [128, NT * 128], BF16))
        wk1a = _e(nc.sbuf_tensor("b6_w1a", [128, D], F32))
        wk1b = _e(nc.sbuf_tensor("b6_w1b", [128, D], F32))
        wk2a = _e(nc.sbuf_tensor("b6_w2a", [128, D], F32))
        wk2b = _e(nc.sbuf_tensor("b6_w2b", [128, D], F32))
        o0 = _e(nc.sbuf_tensor("b6_o0", [128, D], F32))
        o1 = _e(nc.sbuf_tensor("b6_o1", [128, D], F32))
        bs0 = _e(nc.sbuf_tensor("b6_st0", [128, 28], F32))
        bs1 = _e(nc.sbuf_tensor("b6_st1", [128, 28], F32))
        pA = _e(nc.psum_tensor("b6_pA", [128, D], F32))
        pB = _e(nc.psum_tensor("b6_pB", [128, D], F32))
        blk = _e(nc.Block())
        xs = [x0, x1]
        gs = [g0, g1]
        acs = [ac0, ac1]
        os_ = [o0, o1]
        wk1s = [wk1a, wk1b]
        wk2s = [wk2a, wk2b]
        bss = [bs0, bs1]
        pp = [pA, pB]
        lval = {}
        mmv = {}
        combv = {}
        outv = {}
        g1v = {}
        g2v = {}
        s4v = {}
        v1v = {}

        @blk.sync
        def _(sy):
            sdOG.inc(sy.dma_start(out=og[:, :], in_=og128_h[:, :]), 16)
            sdOG.inc(sy.dma_start(out=ob[:, :], in_=ob128_h[:, :]), 16)

        @blk.tensor
        def _(t):
            for n in range(NF):
                st.inc(t.matmul(pA[:, 512 * n:512 * (n + 1)], sel[:, :],
                                bc_tm[:, 512 * n:512 * (n + 1)],
                                start=True, stop=True))
        bcmm = st.n

        @blk.vector
        def _(v):
            st.wait(v, bcmm)
            sv.inc(v.tensor_copy(bc_rep[:, :], pA[:, :]))
        bcev = sv.n

        def loads(sc):
            @blk.sync
            def _(sy, sc=sc):
                if sc >= 2:
                    sv.wait(sy, combv[sc - 2])
                sdL[sc % 2].inc(sy.dma_start(
                    out=xs[sc % 2][:, :],
                    in_=x_h[sc * 128:(sc + 1) * 128, :]), 16)
                sdL[sc % 2].inc(sy.dma_start(
                    out=gs[sc % 2][:, :],
                    in_=gate_sp[sc * 128:(sc + 1) * 128, :]), 16)
                sdL[sc % 2].inc(sy.dma_start(
                    out=acs[sc % 2][:, :].rearrange("p (c f) -> p c f", f=128),
                    in_=attn_sp[:, :, sc * 128:(sc + 1) * 128]), 16)
            lval[sc] = sdL[sc % 2].n

        loads(0)
        loads(1)
        for sc in range(NT):
            p = sc % 2
            wk1 = wk1s[p]
            wk2 = wk2s[p]
            bst = bss[p]

            @blk.tensor
            def _(t, sc=sc, p=p):
                if sc == 0:
                    swO.wait(t, wo_loaded)
                    sv.wait(t, bcev)
                sdL[p].wait(t, lval[sc])
                if sc >= 2:
                    sv.wait(t, combv[sc - 2])
                for gN in range(NF):
                    for ec in range(NT):
                        mi = t.matmul(
                            pp[p][:, gN * 512:(gN + 1) * 512],
                            acs[p][:, ec * 128:(ec + 1) * 128],
                            wo_bf[:, ec * D + gN * 512: ec * D + gN * 512 + 512],
                            start=(ec == 0), stop=(ec == NT - 1))
                    st.inc(mi)
            mmv[sc] = st.n

            @blk.scalar
            def _(s, sc=sc, p=p, wk1=wk1, wk2=wk2):
                sdL[p].wait(s, lval[sc])
                if sc >= 2:
                    sg.wait(s, g1v[sc - 2])
                    sv.wait(s, combv[sc - 2])
                i1 = ss.inc(s.copy(wk2[:, :], gs[p][:, :]))
                ss.inc(s.activation(wk1[:, :], xs[p][:, :], ACT.Identity,
                                    bias=nmr_cols[:, sc:sc + 1],
                                    scale=rsig_cols[:, sc:sc + 1]))
            s12 = ss.n

            @blk.gpsimd
            def _(g, sc=sc, p=p, wk2=wk2, s1=s12 - 1):
                ss.wait(g, s1)
                if sc < 2:
                    sv.wait(g, bcev)
                else:
                    sv.wait(g, v1v[sc - 2])
                sg.inc(g.tensor_mul(wk2[:, :], wk2[:, :], bc_rep[:, :]))
            g1v[sc] = sg.n

            @blk.vector
            def _(v, sc=sc, p=p, wk1=wk1, wk2=wk2, bst=bst):
                if sc == 0:
                    sdOG.wait(v)
                ss.wait(v, s12)
                sg.wait(v, g1v[sc])
                if sc >= 2:
                    ss.wait(v, s4v[sc - 2])
                iv = sv.inc(v.tensor_add(wk1[:, :], wk1[:, :], wk2[:, :]))
                v.wait_ge(sv.h, iv)
                for c in range(NF):
                    ist = v.bn_stats(bst[:, 6 * c:6 * c + 6],
                                     wk1[:, 512 * c:512 * (c + 1)])
                iv = sv.inc(ist)
                v.wait_ge(sv.h, iv)
                iv = sv.inc(v.bn_aggr(bst[:, 24:26], bst[:, 0:24]))
                v.wait_ge(sv.h, iv)
                sv.inc(v.tensor_scalar_add(bst[:, 25:26], bst[:, 25:26], 1e-6))
            v1v[sc] = sv.n

            @blk.scalar
            def _(s, sc=sc, bst=bst):
                sv.wait(s, v1v[sc])
                isx = ss.inc(s.activation(bst[:, 25:26], bst[:, 25:26], ACT.Ln))
                s.wait_ge(ss.h, isx)
                ss.inc(s.activation(bst[:, 26:27], bst[:, 25:26], ACT.Exp,
                                    scale=-0.5))
            s3v = ss.n

            @blk.vector
            def _(v, sc=sc, bst=bst, s3=s3v):
                ss.wait(v, s3)
                sv.inc(v.tensor_scalar(bst[:, 27:28], bst[:, 24:25],
                                       bst[:, 26:27], -1.0,
                                       ALU.mult, ALU.mult))
            v3v = sv.n

            @blk.scalar
            def _(s, sc=sc, wk1=wk1, bst=bst, v3=v3v):
                sv.wait(s, v3)
                ss.inc(s.activation(wk1[:, :], wk1[:, :], ACT.Identity,
                                    bias=bst[:, 27:28], scale=bst[:, 26:27]))
            s4v[sc] = ss.n

            @blk.gpsimd
            def _(g, sc=sc, wk1=wk1):
                ss.wait(g, s4v[sc])
                ig = sg.inc(g.tensor_mul(wk1[:, :], wk1[:, :], og[:, :]))
                g.wait_ge(sg.h, ig)
                sg.inc(g.tensor_add(wk1[:, :], wk1[:, :], ob[:, :]))
            g2v[sc] = sg.n

            @blk.vector
            def _(v, sc=sc, p=p, wk1=wk1):
                st.wait(v, mmv[sc])
                if sc >= 2:
                    sdO[p].wait(v, outv[sc - 2])
                iv = sv.inc(v.tensor_add(os_[p][:, :], xs[p][:, :],
                                         pp[p][:, :]))
                sg.wait(v, g2v[sc])
                v.wait_ge(sv.h, iv)
                sv.inc(v.tensor_add(os_[p][:, :], os_[p][:, :], wk1[:, :]))
            combv[sc] = sv.n

            @blk.sync
            def _(sy, sc=sc, p=p):
                sv.wait(sy, combv[sc])
                sdO[p].inc(sy.dma_start(
                    out=out_h[sc * 128:(sc + 1) * 128, :],
                    in_=os_[p][:, :]), 16)
            outv[sc] = sdO[p].n
            if sc + 2 < NT:
                loads(sc + 2)

        @blk.sync
        def _(sy):
            sdO[0].wait(sy)
            sdO[1].wait(sy)

    wo_ctx.__exit__(None, None, None)
    return nc


_CACHE = {}


def kernel(L0_attn_x, L0_attn_Wq, L0_attn_Wk, L0_attn_Wv, L0_attn_Wo,
           L0_ffn_in_gamma, L0_ffn_in_beta, L0_ffn_agg_w, L0_ffn_agg_b,
           L0_ffn_pn_gamma, L0_ffn_pn_beta, L0_ffn_d1_w, L0_ffn_d1_b,
           L0_ffn_d2_w, L0_ffn_d2_b, L0_ffn_bc_w, L0_ffn_bc_b,
           L0_ffn_gate_w, L0_ffn_gate_b, L0_ffn_out_gamma, L0_ffn_out_beta):
    f32 = np.float32
    X = np.ascontiguousarray(np.asarray(L0_attn_x, dtype=f32))
    wq = np.ascontiguousarray(np.asarray(L0_attn_Wq, dtype=f32))
    wk = np.ascontiguousarray(np.asarray(L0_attn_Wk, dtype=f32))
    wv = np.ascontiguousarray(np.asarray(L0_attn_Wv, dtype=f32))
    wo = np.ascontiguousarray(np.asarray(L0_attn_Wo, dtype=f32))
    gatew = np.ascontiguousarray(np.asarray(L0_ffn_gate_w, dtype=f32))
    aggw = np.ascontiguousarray(np.asarray(L0_ffn_agg_w, dtype=f32))
    bcw = np.ascontiguousarray(np.asarray(L0_ffn_bc_w, dtype=f32))
    d1w = np.ascontiguousarray(np.asarray(L0_ffn_d1_w, dtype=f32))
    d2w = np.ascontiguousarray(np.asarray(L0_ffn_d2_w, dtype=f32))

    wv_blk = np.ascontiguousarray(wv.reshape(D, NT, 128).transpose(1, 0, 2))
    lastrows = np.ascontiguousarray(X[:, D - 1, :])

    def rep8(vec, w=D):
        return np.ascontiguousarray(
            np.broadcast_to(np.asarray(vec, dtype=f32).reshape(1, w), (B, w)))

    in_g8 = rep8(L0_ffn_in_gamma)
    in_b8 = rep8(L0_ffn_in_beta)
    pn_g_fm = np.ascontiguousarray(
        np.asarray(L0_ffn_pn_gamma, dtype=f32).reshape(NT, 128).T)
    pn_b_fm = np.ascontiguousarray(
        np.asarray(L0_ffn_pn_beta, dtype=f32).reshape(NT, 128).T)
    agg_b8 = rep8(L0_ffn_agg_b)
    d2b_fm = np.ascontiguousarray(
        np.asarray(L0_ffn_d2_b, dtype=f32).reshape(NT, 128).T)
    bc_b8 = rep8(L0_ffn_bc_b)
    og128 = np.ascontiguousarray(np.broadcast_to(
        np.asarray(L0_ffn_out_gamma, dtype=f32).reshape(1, D), (128, D)))
    ob128 = np.ascontiguousarray(np.broadcast_to(
        np.asarray(L0_ffn_out_beta, dtype=f32).reshape(1, D), (128, D)))
    gate_b_row = np.ascontiguousarray(
        np.asarray(L0_ffn_gate_b, dtype=f32).reshape(1, D))
    d1b = np.asarray(L0_ffn_d1_b, dtype=f32)

    ident = np.eye(128, dtype=BFD)
    ident32 = np.eye(128, dtype=f32)
    ones_col = np.ones((128, 1), dtype=BFD)
    ones_col32 = np.ones((128, 1), dtype=f32)
    ones1 = np.ones((1, 128), dtype=BFD)
    ones1_32 = np.ones((1, 128), dtype=f32)

    if "nc" not in _CACHE:
        _CACHE["nc"] = build_nc()
    nc = _CACHE["nc"]

    in_maps = []
    for b in range(B):
        selb = np.zeros((B, 128), dtype=f32)
        selb[b, :] = 1.0
        in_maps.append({
            "x": X[b],
            "wq": wq, "wk": wk, "wo": wo, "gatew": gatew,
            "aggw_sl": np.ascontiguousarray(aggw[:, b * 256:(b + 1) * 256]),
            "bcw_sl": np.ascontiguousarray(bcw[:, b * 256:(b + 1) * 256]),
            "wv_blk": wv_blk,
            "d1s_pm": np.ascontiguousarray(
                d1w[:, b * HSL:(b + 1) * HSL].reshape(NT, 128, NHC, 128)),
            "d2s_pm": np.ascontiguousarray(
                d2w[b * HSL:(b + 1) * HSL, :].reshape(NHC, 128, NT, 128)),
            "lastrows": lastrows,
            "in_g8": in_g8, "in_b8": in_b8,
            "pn_g_fm": pn_g_fm, "pn_b_fm": pn_b_fm,
            "agg_bsl": np.ascontiguousarray(
                np.broadcast_to(agg_b8[0:1, b * 256:(b + 1) * 256], (B, 256))),
            "d1b_fm": np.ascontiguousarray(
                d1b[b * HSL:(b + 1) * HSL].reshape(NHC, 128).T),
            "d2b_fm": d2b_fm, "bc_b8": bc_b8,
            "og128": og128, "ob128": ob128, "gate_b_row": gate_b_row,
            "sel_rep": selb, "ident": ident, "ident32": ident32,
            "ones_col": ones_col, "ones_col32": ones_col32,
            "ones1": ones1, "ones1_32": ones1_32,
        })

    res = run_bass_kernel_spmd(nc, in_maps, core_ids=list(range(B)))
    out = np.stack([res.results[b]["out"] for b in range(B)], axis=0)
    return out


# version 12 (final)
# speedup vs baseline: 1.1141x; 1.0114x over previous
from contextlib import ExitStack

import numpy as np
import ml_dtypes
import concourse.bass as bass
import concourse.mybir as mybir
from concourse.bass_utils import run_bass_kernel_spmd

F32 = mybir.dt.float32
BF16 = mybir.dt.bfloat16
ACT = mybir.ActivationFunctionType
ALU = mybir.AluOpType
AX = mybir.AxisListType

D = 2048
NT = 16
NF = 4
B = 8
HSL = 1024
NHC = HSL // 128
KSTEPS = 4

BFD = ml_dtypes.bfloat16


class Sem:

    def __init__(self, nc, name):
        self.h = nc.alloc_semaphore(name)
        self.n = 0

    def inc(self, inst, k=1):
        inst.then_inc(self.h, k)
        self.n += k
        return self.n

    def wait(self, eng, val=None):
        eng.wait_ge(self.h, self.n if val is None else val)



def _dump_and_stop(nc, sbuf_list, dram_list=()):
    sdd = Sem(nc, "sdd")
    with nc.Block() as blk:
        @blk.sync
        def _(sy):
            for name, t in sbuf_list:
                hb = nc.dram_tensor("dbg_" + name, list(t.shape), t.dtype,
                                    kind="ExternalOutput")
                sdd.inc(sy.dma_start(out=hb[:, :], in_=t[:, :]), 16)
            for name, t in dram_list:
                hb = nc.dram_tensor("dbg_" + name, list(t.shape), t.dtype,
                                    kind="ExternalOutput")
                ap_in = t.ap() if hasattr(t, "ap") else t
                sdd.inc(sy.dma_start(out=hb.ap(), in_=t.ap()), 16)
            sdd.wait(sy)
    return nc


def build_nc(stop_after=None):
    nc = bass.Bass("TRN2", target_bir_lowering=False)

    x_h = nc.dram_tensor("x", [D, D], F32, kind="ExternalInput")
    wq_h = nc.dram_tensor("wq", [D, D], F32, kind="ExternalInput")
    wk_h = nc.dram_tensor("wk", [D, D], F32, kind="ExternalInput")
    wo_h = nc.dram_tensor("wo", [D, D], F32, kind="ExternalInput")
    gatew_h = nc.dram_tensor("gatew", [D, D], F32, kind="ExternalInput")
    aggw_h = nc.dram_tensor("aggw_sl", [D, 256], F32, kind="ExternalInput")
    bcw_h = nc.dram_tensor("bcw_sl", [D, 256], F32, kind="ExternalInput")
    wv_h = nc.dram_tensor("wv_blk", [NT, D, 128], F32, kind="ExternalInput")
    d1s_h = nc.dram_tensor("d1s_pm", [NT, 128, NHC, 128], F32,
                           kind="ExternalInput")
    d2s_h = nc.dram_tensor("d2s_pm", [NHC, 128, NT, 128], F32,
                           kind="ExternalInput")
    d1c_sp = nc.dram_tensor("d1c_sp", [NT, 128, NHC * 128], BF16)
    d2c_sp = nc.dram_tensor("d2c_sp", [NHC, 128, NT * 128], BF16)
    lastrows_h = nc.dram_tensor("lastrows", [B, D], F32, kind="ExternalInput")
    in_g8_h = nc.dram_tensor("in_g8", [B, D], F32, kind="ExternalInput")
    in_b8_h = nc.dram_tensor("in_b8", [B, D], F32, kind="ExternalInput")
    pn_g_fm_h = nc.dram_tensor("pn_g_fm", [128, NT], F32, kind="ExternalInput")
    pn_b_fm_h = nc.dram_tensor("pn_b_fm", [128, NT], F32, kind="ExternalInput")
    agg_bsl_h = nc.dram_tensor("agg_bsl", [B, 256], F32, kind="ExternalInput")
    d1b_fm_h = nc.dram_tensor("d1b_fm", [128, NHC], F32, kind="ExternalInput")
    d2b_fm_h = nc.dram_tensor("d2b_fm", [128, NT], F32, kind="ExternalInput")
    bc_b8_h = nc.dram_tensor("bc_b8", [B, D], F32, kind="ExternalInput")
    og128_h = nc.dram_tensor("og128", [128, D], F32, kind="ExternalInput")
    ob128_h = nc.dram_tensor("ob128", [128, D], F32, kind="ExternalInput")
    gate_b_h = nc.dram_tensor("gate_b_row", [1, D], F32, kind="ExternalInput")
    sel_h = nc.dram_tensor("sel_rep", [B, 128], F32, kind="ExternalInput")
    ident_h = nc.dram_tensor("ident", [128, 128], BF16, kind="ExternalInput")
    ident32_h = nc.dram_tensor("ident32", [128, 128], F32, kind="ExternalInput")
    onescol32_h = nc.dram_tensor("ones_col32", [128, 1], F32,
                                 kind="ExternalInput")
    ones1_32_h = nc.dram_tensor("ones1_32", [1, 128], F32, kind="ExternalInput")
    onescol_h = nc.dram_tensor("ones_col", [128, 1], BF16, kind="ExternalInput")
    ones1_h = nc.dram_tensor("ones1", [1, 128], BF16, kind="ExternalInput")

    out_h = nc.dram_tensor("out", [D, D], F32, kind="ExternalOutput")

    gate_sp = nc.dram_tensor("gate_sp", [D, D], BF16)
    wT_sp = nc.dram_tensor("wT_sp", [128, NT, D], BF16)
    attn_sp = nc.dram_tensor("attn_sp", [128, NT, D], BF16)
    cc_in = nc.dram_tensor("cc_in", [B, D], F32)
    cc_out = nc.dram_tensor("cc_out", [B, D], F32)
    ag_in = nc.dram_tensor("ag_in", [B, 256], F32)
    ag_out = nc.dram_tensor("ag_out", [B * B, 256], F32)
    ccf_in = nc.dram_tensor("ccf_in", [128, 128], F32)
    ccf_out = nc.dram_tensor("ccf_out", [128, 128], F32)

    ident = nc.alloc_sbuf_tensor("identS", [128, 128], BF16)
    ones_col = nc.alloc_sbuf_tensor("ones_colS", [128, 1], BF16)
    ones1 = nc.alloc_sbuf_tensor("ones1S", [1, 128], BF16)
    sel = nc.alloc_sbuf_tensor("selS", [B, 128], F32)
    mu_cols = nc.alloc_sbuf_tensor("mu_cols", [128, NT], F32)
    rsig_cols = nc.alloc_sbuf_tensor("rsig_cols", [128, NT], F32)
    r_cols = nc.alloc_sbuf_tensor("r_cols", [128, NT], F32)
    nmr_cols = nc.alloc_sbuf_tensor("nmr_cols", [128, NT], F32)
    bc_rep = nc.alloc_sbuf_tensor("bc_rep", [128, D], F32)

    sv = Sem(nc, "sv")
    st = Sem(nc, "st")
    ss = Sem(nc, "ss")
    sg = Sem(nc, "sg")
    scc = Sem(nc, "scc")

    sdT = Sem(nc, "sdT")
    swW = [Sem(nc, "swW0"), Sem(nc, "swW1")]
    swC = Sem(nc, "swC")
    swB = Sem(nc, "swB")

    thought_fm = nc.alloc_sbuf_tensor("thought_fm", [128, NT * B], F32)
    sq_fm = nc.alloc_sbuf_tensor("sq_fm", [128, NT * B], F32)
    id32s = nc.alloc_sbuf_tensor("id32s", [128, 128], F32)
    t1_bf = nc.alloc_sbuf_tensor("t1_bf", [128, NT * B], BF16)
    t2_bf = nc.alloc_sbuf_tensor("t2_bf", [128, NHC * B], BF16)
    th_bf = nc.alloc_sbuf_tensor("th_bf", [128, NT * B], BF16)
    mu8r = nc.alloc_sbuf_tensor("mu8r", [128, B], F32)
    rs8r = nc.alloc_sbuf_tensor("rs8r", [128, B], F32)
    strow = nc.alloc_sbuf_tensor("strow", [1, 32], F32)
    ccfm = nc.alloc_sbuf_tensor("ccfm", [128, 128], F32)
    pngf = nc.alloc_sbuf_tensor("pngf", [128, NT], F32)
    pnbf = nc.alloc_sbuf_tensor("pnbf", [128, NT], F32)
    d1bf = nc.alloc_sbuf_tensor("d1bf", [128, NHC], F32)
    d2bf = nc.alloc_sbuf_tensor("d2bf", [128, NT], F32)
    onc32 = nc.alloc_sbuf_tensor("onc32", [128, 1], F32)
    on1_32 = nc.alloc_sbuf_tensor("on1_32", [1, 128], F32)

    with ExitStack() as _ctx:
        _e = _ctx.enter_context
        t_last = _e(nc.sbuf_tensor("t_last", [B, D], F32))
        t_ing = _e(nc.sbuf_tensor("t_ing", [B, D], F32))
        t_inb = _e(nc.sbuf_tensor("t_inb", [B, D], F32))
        t_aggb = _e(nc.sbuf_tensor("t_aggb", [B, 256], F32))
        twork = _e(nc.sbuf_tensor("t_work", [B, D], F32))
        thought = _e(nc.sbuf_tensor("t_thought", [B, D], F32))
        tbf = _e(nc.sbuf_tensor("t_bf", [B, D], BF16))
        tfm = _e(nc.sbuf_tensor("t_fm", [128, NT * B], BF16))
        tstats = _e(nc.sbuf_tensor("t_stats", [B, 26], F32))
        wck0 = _e(nc.sbuf_tensor("t_wck0", [128, D], BF16))
        wck1 = _e(nc.sbuf_tensor("t_wck1", [128, D], BF16))
        ccst = _e(nc.sbuf_tensor("t_ccst", [B, D], F32))
        ptrT = [_e(nc.psum_tensor("t_ptr%d" % i, [128, B], BF16))
                for i in range(3)]
        pbig = _e(nc.psum_tensor("t_pbig", [B, D], F32))
        blk = _e(nc.Block())
        wck = [wck0, wck1]

        @blk.sync
        def _(sy):
            for dst, srcb in ((t_last, lastrows_h), (t_ing, in_g8_h),
                              (t_inb, in_b8_h), (t_aggb, agg_bsl_h),
                              (id32s, ident32_h)):
                sdT.inc(sy.dma_start(out=dst[:, :], in_=srcb[:, :]), 16)
            for dst, srcb in ((pngf, pn_g_fm_h), (pnbf, pn_b_fm_h),
                              (d1bf, d1b_fm_h), (d2bf, d2b_fm_h),
                              (onc32, onescol32_h), (on1_32, ones1_32_h)):
                sdT.inc(sy.dma_start(out=dst[:, :], in_=srcb[:, :]), 16)
            sdT.inc(sy.dma_start(out=ident[:, :], in_=ident_h[:, :]), 16)
            sdT.inc(sy.dma_start(out=ones_col[:, :], in_=onescol_h[:, :]), 16)
            sdT.inc(sy.dma_start(out=ones1[:, :], in_=ones1_h[:, :]), 16)
            sdT.inc(sy.dma_start(out=sel[:, :], in_=sel_h[:, :]), 16)
        small_loads = sdT.n

        def ln8(src_ap, dst_ap, W, eps, gamma=None, beta=None, pre_wait=None):
            nchunk = (W + 511) // 512

            @blk.vector
            def _(v):
                if pre_wait is not None:
                    pre_wait(v)
                sv.wait(v)
                for c in range(nchunk):
                    ist = v.bn_stats(tstats[:, 6 * c:6 * c + 6],
                                     src_ap[:, 512 * c:512 * (c + 1)])
                iv = sv.inc(ist)
                v.wait_ge(sv.h, iv)
                iv = sv.inc(v.bn_aggr(tstats[:, 24:26], tstats[:, 0:6 * nchunk]))
                v.wait_ge(sv.h, iv)
                sv.inc(v.tensor_scalar_add(tstats[:, 25:26], tstats[:, 25:26],
                                           eps))

            @blk.scalar
            def _(s):
                sv.wait(s)
                isx = ss.inc(s.activation(tstats[:, 25:26], tstats[:, 25:26],
                                          ACT.Ln))
                s.wait_ge(ss.h, isx)
                ss.inc(s.activation(tstats[:, 25:26], tstats[:, 25:26],
                                    ACT.Exp, scale=-0.5))

            @blk.vector
            def _(v):
                ss.wait(v)
                iv = sv.inc(v.tensor_scalar(dst_ap, src_ap, tstats[:, 24:25],
                                            tstats[:, 25:26], ALU.subtract,
                                            ALU.mult))
                if gamma is not None:
                    v.wait_ge(sv.h, iv)
                    iv = sv.inc(v.tensor_mul(dst_ap, dst_ap, gamma))
                if beta is not None:
                    v.wait_ge(sv.h, iv)
                    sv.inc(v.tensor_add(dst_ap, dst_ap, beta))

        def to_fm(src_ap, W, dst, dst_dtype_ps, idap):
            nchunk = W // 128

            @blk.vector
            def _(v):
                sv.wait(v)
                sv.inc(v.tensor_copy(tbf[:, 0:W], src_ap))
            ev_base = sv.n
            tr_base = st.n

            for g4 in range(0, nchunk, 4):
                @blk.tensor
                def _(t, g4=g4):
                    for c in range(g4, min(nchunk, g4 + 4)):
                        if c == 0:
                            sv.wait(t, ev_base)
                        if c >= 3:
                            sv.wait(t, ev_base + c - 2)
                        st.inc(t.transpose(ptrT[c % 3][:, :],
                                           tbf[:, 128 * c:128 * (c + 1)],
                                           ident[0:B, 0:B]))

                @blk.vector
                def _(v, g4=g4):
                    for c in range(g4, min(nchunk, g4 + 4)):
                        v.wait_ge(st.h, tr_base + c + 1)
                        v.tensor_copy(dst[:, B * c:B * (c + 1)],
                                      ptrT[c % 3][:, :]).then_inc(sv.h, 1)
                        sv.n += 1

        def mm8(whbm, nk, W, psum):
            nN = (W + 511) // 512
            load_val = {}
            mmv = {}
            st_base = st.n

            def emit_load(ck):
                @blk.gpsimd
                def _(g, ck=ck):
                    st.wait(g, mmv[ck - 2] if ck >= 2 else st_base)
                    swW[ck % 2].inc(
                        g.dma_start(out=wck[ck % 2][:, 0:W],
                                    in_=whbm[128 * ck:128 * (ck + 1), 0:W]), 16)
                load_val[ck] = swW[ck % 2].n

            emit_load(0)
            if nk > 1:
                emit_load(1)
            for ck in range(nk):
                @blk.tensor
                def _(t, ck=ck):
                    if ck == 0:
                        sv.wait(t)
                    swW[ck % 2].wait(t, load_val[ck])
                    for n in range(nN):
                        w1 = min(W, 512 * (n + 1))
                        mi = t.matmul(psum[:, 512 * n:w1],
                                      tfm[:, B * ck:B * (ck + 1)],
                                      wck[ck % 2][:, 512 * n:w1],
                                      start=(ck == 0), stop=(ck == nk - 1))
                    st.inc(mi)
                mmv[ck] = st.n
                if ck + 2 < nk:
                    emit_load(ck + 2)

        def w_small(v):
            sdT.wait(v, small_loads)
        ln8(t_last[:, :], twork[:, :], D, 1e-5, pre_wait=w_small)
        ln8(twork[:, :], thought[:, :], D, 1e-6, gamma=t_ing[:, :],
            beta=t_inb[:, :])
        to_fm(thought[:, :], D, tfm, None, None)
        mm8(aggw_h, NT, 256, pbig)

        @blk.vector
        def _(v):
            st.wait(v)
            iv = sv.inc(v.tensor_copy(ccst[:, 0:256], pbig[:, 0:256]))
            v.wait_ge(sv.h, iv)
            sv.inc(v.tensor_add(ccst[:, 0:256], ccst[:, 0:256], t_aggb[:, :]))

        @blk.gpsimd
        def _(g):
            sv.wait(g)
            swC.inc(g.dma_start(out=ag_in[:, :], in_=ccst[:, 0:256]), 16)
            swC.wait(g)
            scc.inc(g.collective_compute(
                "AllGather", ALU.bypass, replica_groups=[list(range(B))],
                ins=[ag_in.ap().opt()], outs=[ag_out.ap().opt()]))
    agval = scc.n

    pT_ctx = ExitStack()
    wckb_ctx = ExitStack()
    pTa = pTb = wckb = bc_tm = bc_b8s = None

    def fm_ln(blk):
        @blk.scalar
        def _(s):
            sv.wait(s)
            ss.inc(s.activation(sq_fm[:, :], thought_fm[:, :], ACT.Square))

        @blk.tensor
        def _(t):
            ss.wait(t)
            sv.wait(t)
            st.inc(t.matmul(pTa[0:1, 128:256], onc32[:, :], thought_fm[:, :],
                            start=True, stop=False))
            st.inc(t.matmul(pTa[0:1, 256:384], onc32[:, :], sq_fm[:, :],
                            start=False, stop=True))

        @blk.vector
        def _(v):
            st.wait(v)
            iv = sv.inc(v.reduce_sum(
                strow[0:1, 0:8],
                bass.AP(pTa.ap().tensor, 128, [[512, 1], [1, 8], [8, 16]]),
                axis=AX.X))
            v.wait_ge(sv.h, iv)
            iv = sv.inc(v.reduce_sum(
                strow[0:1, 8:16],
                bass.AP(pTa.ap().tensor, 256, [[512, 1], [1, 8], [8, 16]]),
                axis=AX.X))
            v.wait_ge(sv.h, iv)
            iv = sv.inc(v.tensor_scalar_mul(strow[0:1, 0:8], strow[0:1, 0:8],
                                            1.0 / D))
            v.wait_ge(sv.h, iv)
            iv = sv.inc(v.tensor_scalar_mul(strow[0:1, 8:16], strow[0:1, 8:16],
                                            1.0 / D))
            v.wait_ge(sv.h, iv)
            iv = sv.inc(v.tensor_mul(strow[0:1, 16:24], strow[0:1, 0:8],
                                     strow[0:1, 0:8]))
            v.wait_ge(sv.h, iv)
            iv = sv.inc(v.tensor_sub(strow[0:1, 8:16], strow[0:1, 8:16],
                                     strow[0:1, 16:24]))
            v.wait_ge(sv.h, iv)
            sv.inc(v.tensor_scalar_add(strow[0:1, 8:16], strow[0:1, 8:16],
                                       1e-6))

        @blk.scalar
        def _(s):
            sv.wait(s)
            isx = ss.inc(s.activation(strow[0:1, 8:16], strow[0:1, 8:16],
                                      ACT.Ln))
            s.wait_ge(ss.h, isx)
            ss.inc(s.activation(strow[0:1, 8:16], strow[0:1, 8:16],
                                ACT.Exp, scale=-0.5))

        @blk.vector
        def _(v):
            ss.wait(v)
            iv = sv.inc(v.tensor_mul(strow[0:1, 16:24], strow[0:1, 0:8],
                                     strow[0:1, 8:16]))
            v.wait_ge(sv.h, iv)
            sv.inc(v.tensor_scalar_mul(strow[0:1, 16:24], strow[0:1, 16:24],
                                       -1.0))

        @blk.tensor
        def _(t):
            sv.wait(t)
            st.inc(t.matmul(pTa[:, 384:392], on1_32[:, :], strow[0:1, 16:24],
                            start=True, stop=False))
            st.inc(t.matmul(pTa[:, 392:400], on1_32[:, :], strow[0:1, 8:16],
                            start=False, stop=True))

        @blk.vector
        def _(v):
            st.wait(v)
            iv = sv.inc(v.tensor_copy(mu8r[:, :], pTa[:, 384:392]))
            v.wait_ge(sv.h, iv)
            iv = sv.inc(v.tensor_copy(rs8r[:, :], pTa[:, 392:400]))
            v.wait_ge(sv.h, iv)
            ist = None
            for c in range(NT):
                ist = v.tensor_mul(sq_fm[:, B * c:B * (c + 1)],
                                   thought_fm[:, B * c:B * (c + 1)],
                                   rs8r[:, :])
            iv = sv.inc(ist)
            v.wait_ge(sv.h, iv)
            for c in range(NT):
                ist = v.tensor_add(sq_fm[:, B * c:B * (c + 1)],
                                   sq_fm[:, B * c:B * (c + 1)], mu8r[:, :])
            iv = sv.inc(ist)
            v.wait_ge(sv.h, iv)
            for c in range(NT):
                ist = v.tensor_scalar(t1_bf[:, B * c:B * (c + 1)],
                                      sq_fm[:, B * c:B * (c + 1)],
                                      pngf[:, c:c + 1], pnbf[:, c:c + 1],
                                      ALU.mult, ALU.add)
            sv.inc(ist)

    fmst = {}

    def fm_d1_loads(blk, k):
        @blk.gpsimd
        def _(g, k=k):
            st.wait(g, fmst.get(("d2m", k - 1), st.n))
            swP1.wait(g, NT * 16)
            for c in range(NT):
                swB.inc(g.dma_start(
                    out=wckb[:, c * 1024:(c + 1) * 1024],
                    in_=d1c_sp[c]), 16)
        fmst[("d1l", k)] = swB.n

    def fm_d1_mms(blk, k):
        @blk.tensor
        def _(t, k=k):
            sv.wait(t)
            swB.wait(t, fmst[("d1l", k)])
            for c in range(NT):
                for blkh in range(NHC):
                    mi = t.matmul(
                        pTa[:, blkh * 8:(blkh + 1) * 8],
                        wckb[:, c * 1024 + blkh * 128:
                             c * 1024 + blkh * 128 + 128],
                        t1_bf[:, B * c:B * (c + 1)],
                        start=(c == 0 and blkh == 0),
                        stop=(c == NT - 1 and blkh == NHC - 1))
            st.inc(mi)
        fmst[("d1m", k)] = st.n

        @blk.vector
        def _(v, k=k):
            st.wait(v, fmst[("d1m", k)])
            ist = None
            for blkh in range(NHC):
                ist = v.tensor_scalar_add(t2_bf[:, blkh * 8:(blkh + 1) * 8],
                                          pTa[:, blkh * 8:(blkh + 1) * 8],
                                          d1bf[:, blkh:blkh + 1])
            sv.inc(ist)

        @blk.scalar
        def _(s):
            sv.wait(s)
            ss.inc(s.activation(t2_bf[:, :], t2_bf[:, :], ACT.Gelu))
        fmst[("gelu", k)] = ss.n

    def fm_d2_loads(blk, k):
        @blk.gpsimd
        def _(g, k=k):
            st.wait(g, fmst[("d1m", k)])
            swP2.wait(g, NHC * 16)
            for c in range(NHC):
                swB.inc(g.dma_start(
                    out=wckb[:, c * 2048:(c + 1) * 2048],
                    in_=d2c_sp[c]), 16)
        fmst[("d2l", k)] = swB.n

    def fm_d2_mms(blk, k):
        @blk.tensor
        def _(t, k=k):
            ss.wait(t, fmst[("gelu", k)])
            swB.wait(t, fmst[("d2l", k)])
            for c in range(NHC):
                for blkd in range(NT):
                    mi = t.matmul(
                        pTb[:, blkd * 8:(blkd + 1) * 8],
                        wckb[:, c * 2048 + blkd * 128:
                             c * 2048 + blkd * 128 + 128],
                        t2_bf[:, B * c:B * (c + 1)],
                        start=(c == 0 and blkd == 0),
                        stop=(c == NHC - 1 and blkd == NT - 1))
            st.inc(mi)
        fmst[("d2m", k)] = st.n

        @blk.vector
        def _(v, k=k):
            st.wait(v, fmst[("d2m", k)])
            sv.inc(v.tensor_copy(ccfm[:, :], pTb[:, 0:128]))

        @blk.gpsimd
        def _(g):
            sv.wait(g)
            swC.inc(g.dma_start(out=ccf_in[:, :], in_=ccfm[:, :]), 16)
            swC.wait(g)
            scc.inc(g.collective_compute(
                "AllReduce", ALU.add, replica_groups=[list(range(B))],
                ins=[ccf_in.ap().opt()], outs=[ccf_out.ap().opt()]))
        fmst[("ar", k)] = scc.n

    def fm_update(blk, k):
        @blk.gpsimd
        def _(g, k=k):
            scc.wait(g, fmst[("ar", k)])
            swC.inc(g.dma_start(out=ccfm[:, :], in_=ccf_out[:, :]), 16)
        cw = swC.n

        @blk.vector
        def _(v, k=k, cw=cw):
            swC.wait(v, cw)
            iv = sv.inc(v.tensor_add(thought_fm[:, :], thought_fm[:, :],
                                     ccfm[:, :]))
            v.wait_ge(sv.h, iv)
            ist = None
            for c in range(NT):
                ist = v.tensor_scalar_add(thought_fm[:, B * c:B * (c + 1)],
                                          thought_fm[:, B * c:B * (c + 1)],
                                          d2bf[:, c:c + 1])
            sv.inc(ist)

    def fm_bc_a(blk):
        @blk.vector
        def _(v):
            sv.wait(v)
            sv.inc(v.tensor_copy(th_bf[:, :], thought_fm[:, :]))
        thbv = sv.n

        @blk.gpsimd
        def _(g):
            st.wait(g, fmst[("d2m", KSTEPS - 1)])
            for c in range(NT):
                swB.inc(g.dma_start(
                    out=wckb[:, c * 256:(c + 1) * 256],
                    in_=bcw_h[128 * c:128 * (c + 1), :]), 16)
        bcld = swB.n

        @blk.tensor
        def _(t):
            sv.wait(t, thbv)
            swB.wait(t, bcld)
            for c in range(NT):
                mi = t.matmul(pTa[0:B, 0:256], th_bf[:, B * c:B * (c + 1)],
                              wckb[:, c * 256:(c + 1) * 256],
                              start=(c == 0), stop=(c == NT - 1))
            st.inc(mi)

        @blk.vector
        def _(v):
            st.wait(v)
            sv.inc(v.tensor_copy(bc_tm[:, 0:256], pTa[0:B, 0:256]))

        @blk.gpsimd
        def _(g):
            sv.wait(g)
            swC.inc(g.dma_start(out=ag_in[:, :], in_=bc_tm[:, 0:256]), 16)
            swC.wait(g)
            scc.inc(g.collective_compute(
                "AllGather", ALU.bypass, replica_groups=[list(range(B))],
                ins=[ag_in.ap().opt()], outs=[ag_out.ap().opt()]))

    def fm_bc_b(blk):
        @blk.gpsimd
        def _(g):
            scc.wait(g)
            for r in range(B):
                swC.inc(g.dma_start(out=bc_tm[:, 256 * r:256 * (r + 1)],
                                    in_=ag_out[B * r:B * (r + 1), :]), 16)

        @blk.vector
        def _(v):
            swC.wait(v)
            sdT.wait(v)
            sv.inc(v.tensor_add(bc_tm[:, :], bc_tm[:, :], bc_b8s[:, :]))

    seam_plan = {}
    for k in range(KSTEPS):
        base = 6 * k
        seam_plan.setdefault(base + 0, []).append(
            lambda blk, k=k: fm_d1_loads(blk, k))
        seam_plan.setdefault(base + 3, []).append(
            lambda blk, k=k: (fm_update(blk, k - 1) if k > 0 else None,
                              fm_ln(blk)))
        seam_plan.setdefault(base + 4, []).append(
            lambda blk, k=k: (fm_d1_mms(blk, k), fm_d2_loads(blk, k)))
        seam_plan.setdefault(base + 6, []).append(
            lambda blk, k=k: fm_d2_mms(blk, k))
    seam_plan.setdefault(27, []).append(
        lambda blk: fm_update(blk, KSTEPS - 1))
    seam_plan.setdefault(28, []).append(fm_bc_a)
    seam_plan.setdefault(31, []).append(fm_bc_b)
    seam_i = [0]

    def fm_seam(blk):
        for fn in seam_plan.get(seam_i[0], []):
            fn(blk)
        seam_i[0] += 1

    sdX = [Sem(nc, "sdX0"), Sem(nc, "sdX1")]
    swS = [Sem(nc, "swS0"), Sem(nc, "swS1")]
    swP1 = Sem(nc, "swP1")
    swP2 = Sem(nc, "swP2")
    xn_ctx = nc.sbuf_tensor("xn_fm", [128, NT * D], BF16)
    xn_fm = xn_ctx.__enter__()
    q_ctx = nc.sbuf_tensor("q_tm", [128, NT * D], BF16)
    q_tm = q_ctx.__enter__()
    with ExitStack() as _ctx:
        _e = _ctx.enter_context
        xt0 = _e(nc.sbuf_tensor("b1_x0", [128, D], F32))
        xt1 = _e(nc.sbuf_tensor("b1_x1", [128, D], F32))
        xb0 = _e(nc.sbuf_tensor("b1_bf0", [128, D], BF16))
        xb1 = _e(nc.sbuf_tensor("b1_bf1", [128, D], BF16))
        bst = _e(nc.sbuf_tensor("b1_st", [128, 26], F32))
        w0 = _e(nc.sbuf_tensor("b2_w0", [128, NT * 512], BF16))
        w1 = _e(nc.sbuf_tensor("b2_w1", [128, NT * 512], BF16))
        tp0 = _e(nc.psum_tensor("b1_p0", [128, 128], BF16))
        tp1 = _e(nc.psum_tensor("b1_p1", [128, 128], BF16))
        tp2 = _e(nc.psum_tensor("b1_p2", [128, 128], BF16))
        tp3 = _e(nc.psum_tensor("b1_p3", [128, 128], BF16))
        p0 = _e(nc.psum_tensor("b2_p0", [128, 512], F32))
        p1 = _e(nc.psum_tensor("b2_p1", [128, 512], F32))
        th_stage = _e(nc.sbuf_tensor("b1_thstage", [B, D], F32))
        ptf2 = _e(nc.psum_tensor("b1_ptf2", [128, B], F32))
        blk = _e(nc.Block())
        xts = [xt0, xt1]
        xbs = [xb0, xb1]
        tps = [tp0, tp1, tp2, tp3]
        ws = [w0, w1]
        ps = [p0, p1]
        load_val = {}
        apply_val = {}
        tr_val = {}
        ev_val = {}
        wload = {}
        mmv = {}
        evv = {}

        def wslab_load(fs, whbm):
            @blk.gpsimd
            def _(g, fs=fs):
                if fs >= 2:
                    st.wait(g, mmv[(fs - 2, NT - 1)])
                for c in range(NT):
                    swS[fs % 2].inc(g.dma_start(
                        out=ws[fs % 2][:, c * 512:(c + 1) * 512],
                        in_=whbm[c * 128:(c + 1) * 128,
                                 fs * 512:(fs + 1) * 512]), 16)
            wload[fs] = swS[fs % 2].n

        def emit_load(tt):
            @blk.sync
            def _(sy, tt=tt):
                if tt >= 2:
                    sv.wait(sy, apply_val[tt - 2])
                for q in range(4):
                    sdX[tt % 2].inc(sy.dma_start(
                        out=xts[tt % 2][:, 512 * q:512 * (q + 1)],
                        in_=x_h[128 * tt:128 * (tt + 1),
                                512 * q:512 * (q + 1)]), 16)
            load_val[tt] = sdX[tt % 2].n

        def b1_tile(tt):
            @blk.vector
            def _(v, tt=tt):
                sdX[tt % 2].wait(v, load_val[tt])
                if tt >= 2:
                    st.wait(v, tr_val.get((tt - 2, 15), 0))
                sv.wait(v)
                xt = xts[tt % 2]
                for c in range(NF):
                    ist = v.bn_stats(bst[:, 6 * c:6 * c + 6],
                                     xt[:, 512 * c:512 * (c + 1)])
                iv = sv.inc(ist)
                v.wait_ge(sv.h, iv)
                iv = sv.inc(v.bn_aggr(bst[:, 24:26], bst[:, 0:24]))
                v.wait_ge(sv.h, iv)
                v.tensor_copy(mu_cols[:, tt:tt + 1], bst[:, 24:25])
                sv.inc(v.tensor_scalar_add(bst[:, 25:26], bst[:, 25:26], 1e-5))

            @blk.scalar
            def _(s, tt=tt):
                sv.wait(s)
                isx = ss.inc(s.activation(bst[:, 25:26], bst[:, 25:26], ACT.Ln))
                s.wait_ge(ss.h, isx)
                ss.inc(s.activation(rsig_cols[:, tt:tt + 1], bst[:, 25:26],
                                    ACT.Exp, scale=-0.5))

            @blk.vector
            def _(v, tt=tt):
                ss.wait(v)
                v.tensor_scalar(nmr_cols[:, tt:tt + 1], mu_cols[:, tt:tt + 1],
                                rsig_cols[:, tt:tt + 1], -1.0,
                                ALU.mult, ALU.mult)
                sv.inc(v.tensor_scalar(xbs[tt % 2][:, :], xts[tt % 2][:, :],
                                       mu_cols[:, tt:tt + 1],
                                       rsig_cols[:, tt:tt + 1],
                                       ALU.subtract, ALU.mult))
            apply_val[tt] = sv.n
            if tt + 2 < NT:
                emit_load(tt + 2)

            for grp in range(4):
                @blk.tensor
                def _(t, tt=tt, grp=grp):
                    if grp == 0:
                        sv.wait(t, apply_val[tt])
                    for j in range(4):
                        c = grp * 4 + j
                        if (tt, c) >= (0, 4):
                            psem, pval = ev_val.get(
                                (tt, c - 4), ev_val.get((tt - 1, c + 12),
                                                        (ss, 0)))
                            psem.wait(t, pval)
                        st.inc(t.transpose(tps[c % 4][:, :],
                                           xbs[tt % 2][:, 128 * c:128 * (c + 1)],
                                           ident[:, :]))
                        tr_val[(tt, c)] = st.n

                @blk.scalar
                def _(s, tt=tt, grp=grp):
                    for j in range(0, 4, 2):
                        c = grp * 4 + j
                        st.wait(s, tr_val[(tt, c)])
                        ss.inc(s.activation(
                            bass.AP(xn_fm, c * D + tt * 128,
                                    [[NT * D, 128], [1, 128]]),
                            tps[c % 4][:, :], ACT.Identity))
                        ev_val[(tt, c)] = (ss, ss.n)

                @blk.vector
                def _(v, tt=tt, grp=grp):
                    for j in range(1, 4, 2):
                        c = grp * 4 + j
                        st.wait(v, tr_val[(tt, c)])
                        sv.inc(v.tensor_copy(
                            bass.AP(xn_fm, c * D + tt * 128,
                                    [[NT * D, 128], [1, 128]]),
                            tps[c % 4][:, :]))
                        ev_val[(tt, c)] = (sv, sv.n)

        def b2_tile(fs, sc):
            @blk.tensor
            def _(t, fs=fs, sc=sc):
                if sc == 0:
                    swS[fs % 2].wait(t, wload[fs])
                xsem, xval = ev_val[(sc, 15)]
                xsem.wait(t, xval)
                e0sem, e0val = ev_val[(sc, 14)]
                e0sem.wait(t, e0val)
                idx = fs * NT + sc
                if idx >= 2:
                    sv.wait(t, evv[idx - 2])
                for e in range(NT):
                    mi = t.matmul(
                        ps[idx % 2][:, :],
                        bass.AP(xn_fm, e * D + sc * 128,
                                [[NT * D, 128], [1, 128]]),
                        ws[fs % 2][:, e * 512:(e + 1) * 512],
                        start=(e == 0), stop=(e == NT - 1))
                st.inc(mi)
            mmv[(fs, sc)] = st.n

            @blk.vector
            def _(v, fs=fs, sc=sc):
                idx = fs * NT + sc
                st.wait(v, mmv[(fs, sc)])
                sv.inc(v.tensor_copy(
                    q_tm[:, sc * D + fs * 512: sc * D + fs * 512 + 512],
                    ps[idx % 2][:, :]))
            evv[fs * NT + sc] = sv.n

        def fm_init():
            @blk.gpsimd
            def _(g):
                scc.wait(g, agval)
                for r in range(B):
                    swC.inc(g.dma_start(
                        out=th_stage[:, 256 * r:256 * (r + 1)],
                        in_=ag_out[B * r:B * (r + 1), :]), 16)
            reasm = swC.n
            trn = {}
            ev_base = sv.n
            for g4 in range(0, NT, 4):
                @blk.tensor
                def _(t, g4=g4):
                    if g4 == 0:
                        swC.wait(t, reasm)
                    for c in range(g4, min(NT, g4 + 4)):
                        if c > 0:
                            sv.wait(t, ev_base + c)
                        st.inc(t.transpose(ptf2[:, :],
                                           th_stage[:, 128 * c:128 * (c + 1)],
                                           id32s[0:B, 0:B]))
                        trn[c] = st.n

                @blk.vector
                def _(v, g4=g4):
                    for c in range(g4, min(NT, g4 + 4)):
                        st.wait(v, trn[c])
                        v.tensor_copy(thought_fm[:, B * c:B * (c + 1)],
                                      ptf2[:, :]).then_inc(sv.h, 1)
                        sv.n += 1
                        assert sv.n == ev_base + c + 1

        wslab_load(0, wq_h)
        wslab_load(1, wq_h)
        emit_load(0)
        emit_load(1)
        def precast():
            @blk.gpsimd
            def _(g):
                for c in range(NT):
                    swP1.inc(g.dma_start(
                        out=d1c_sp[c],
                        in_=d1s_h[c].rearrange("p c f -> p (c f)")), 16)
                for c in range(NHC):
                    swP2.inc(g.dma_start(
                        out=d2c_sp[c],
                        in_=d2s_h[c].rearrange("p c f -> p (c f)")), 16)

        for tt in range(NT):
            b1_tile(tt)
            if tt == 6:
                fm_init()
            if tt >= 1:
                b2_tile(0, tt - 1)
        b2_tile(0, NT - 1)
        wslab_load(2, wq_h)
        for fs in range(1, NF):
            for sc in range(NT):
                b2_tile(fs, sc)
            if fs + 2 < NF:
                wslab_load(fs + 2, wq_h)
            if fs == 1:
                precast()

    if stop_after == "B2":
        return _dump_and_stop(nc, [("mu_cols", mu_cols), ("rsig_cols", rsig_cols),
                                   ("xn_fm", xn_fm), ("q_tm", q_tm)])

    sdG = [Sem(nc, "sdG0"), Sem(nc, "sdG1")]
    swGB = Sem(nc, "swGB")
    with ExitStack() as _ctx:
        _e = _ctx.enter_context
        w0 = _e(nc.sbuf_tensor("b25_w0", [128, NT * 512], BF16))
        w1 = _e(nc.sbuf_tensor("b25_w1", [128, NT * 512], BF16))
        g0 = _e(nc.sbuf_tensor("b25_g0", [128, 512], BF16))
        gate_b_bf = _e(nc.sbuf_tensor("gate_b_bfS", [1, D], BF16))
        g1 = _e(nc.sbuf_tensor("b25_g1", [128, 512], BF16))
        p0 = _e(nc.psum_tensor("b25_p0", [128, 512], F32))
        p1 = _e(nc.psum_tensor("b25_p1", [128, 512], F32))
        blk = _e(nc.Block())
        ws = [w0, w1]
        gs = [g0, g1]
        ps = [p0, p1]
        wload = {}
        mmv = {}
        actv = {}
        spv = {}

        @blk.gpsimd
        def _(g):
            swGB.inc(g.dma_start(out=gate_b_bf[:, :], in_=gate_b_h[:, :]), 16)

        def wslab_load(fs):
            @blk.gpsimd
            def _(g, fs=fs):
                if fs >= 2:
                    st.wait(g, mmv[(fs - 2, NT - 1)])
                for c in range(NT):
                    swS[fs % 2].inc(g.dma_start(
                        out=ws[fs % 2][:, c * 512:(c + 1) * 512],
                        in_=gatew_h[c * 128:(c + 1) * 128,
                                 fs * 512:(fs + 1) * 512]), 16)
            wload[fs] = swS[fs % 2].n

        wslab_load(0)
        wslab_load(1)
        for fs in range(NF):
            for sc in range(NT):
                idx = fs * NT + sc

                @blk.tensor
                def _(t, fs=fs, sc=sc, idx=idx):
                    if idx == 0:
                        swGB.wait(t)
                    if sc == 0:
                        swS[fs % 2].wait(t, wload[fs])
                    if idx >= 2:
                        ss.wait(t, actv[idx - 2])
                    for e in range(NT):
                        t.matmul(
                            ps[idx % 2][:, :],
                            bass.AP(xn_fm, e * D + sc * 128,
                                    [[NT * D, 128], [1, 128]]),
                            ws[fs % 2][:, e * 512:(e + 1) * 512],
                            start=(e == 0), stop=False)
                    st.inc(t.matmul(ps[idx % 2][:, :], ones1[:, :],
                                    gate_b_bf[:, fs * 512:(fs + 1) * 512],
                                    start=False, stop=True))
                mmv[(fs, sc)] = st.n

                @blk.scalar
                def _(s, fs=fs, sc=sc, idx=idx):
                    st.wait(s, mmv[(fs, sc)])
                    if idx >= 2:
                        sdG[idx % 2].wait(s, spv[idx - 2])
                    ss.inc(s.activation(gs[idx % 2][:, :], ps[idx % 2][:, :],
                                        ACT.Sigmoid))
                actv[idx] = ss.n

                @blk.sync
                def _(sy, fs=fs, sc=sc, idx=idx):
                    ss.wait(sy, actv[idx])
                    sdG[idx % 2].inc(sy.dma_start(
                        out=gate_sp[sc * 128:(sc + 1) * 128,
                                    fs * 512:(fs + 1) * 512],
                        in_=gs[idx % 2][:, :]), 16)
                spv[idx] = sdG[idx % 2].n
            if fs + 2 < NF:
                wslab_load(fs + 2)

        @blk.sync
        def _(sy):
            sdG[0].wait(sy)
            sdG[1].wait(sy)

    if stop_after == "B25":
        return _dump_and_stop(nc, [], [("gate_sp", gate_sp)])

    sdW = [Sem(nc, "sdW0"), Sem(nc, "sdW1"), Sem(nc, "sdW2")]
    with ExitStack() as _ctx:
        _e = _ctx.enter_context
        w0 = _e(nc.sbuf_tensor("b3_w0", [128, NT * 512], BF16))
        w1 = _e(nc.sbuf_tensor("b3_w1", [128, NT * 512], BF16))
        kslab = _e(nc.sbuf_tensor("b3_k", [128, NT * 512], BF16))
        wt0 = _e(nc.sbuf_tensor("b3_wt0", [128, D], BF16))
        wt1 = _e(nc.sbuf_tensor("b3_wt1", [128, D], BF16))
        wt2 = _e(nc.sbuf_tensor("b3_wt2", [128, D], BF16))
        l_cols = _e(nc.sbuf_tensor("b3_l", [128, NT], F32))
        pk0 = _e(nc.psum_tensor("b3_pk0", [128, 512], F32))
        pk1 = _e(nc.psum_tensor("b3_pk1", [128, 512], F32))
        ps0 = _e(nc.psum_tensor("b3_ps0", [128, 512], F32))
        ps1 = _e(nc.psum_tensor("b3_ps1", [128, 512], F32))
        ps2 = _e(nc.psum_tensor("b3_ps2", [128, 512], F32))
        pl = _e(nc.psum_tensor("b3_pl", [128, NT], F32))
        blk = _e(nc.Block())
        ws = [w0, w1]
        pks = [pk0, pk1]
        pss = [ps0, ps1, ps2]
        wts = [wt0, wt1, wt2]
        wload = {}
        kmmv = {}
        kevv = {}
        smmv = {}
        expv = {}
        lmmv = {}
        spv = {}

        def wslab_load(fs):
            @blk.gpsimd
            def _(g, fs=fs):
                if fs >= 2:
                    st.wait(g, kmmv[(fs - 2, NT - 1)])
                for c in range(NT):
                    swS[fs % 2].inc(g.dma_start(
                        out=ws[fs % 2][:, c * 512:(c + 1) * 512],
                        in_=wk_h[c * 128:(c + 1) * 128,
                                 fs * 512:(fs + 1) * 512]), 16)
            wload[fs] = swS[fs % 2].n

        wslab_load(0)
        wslab_load(1)
        for fs in range(NF):
            for sc in range(NT):
                @blk.tensor
                def _(t, fs=fs, sc=sc):
                    if sc == 0:
                        swS[fs % 2].wait(t, wload[fs])
                    idx = fs * NT + sc
                    if idx >= 2:
                        sv.wait(t, kevv[idx - 2])
                    for e in range(NT):
                        mi = t.matmul(
                            pks[idx % 2][:, :],
                            bass.AP(xn_fm, e * D + sc * 128,
                                    [[NT * D, 128], [1, 128]]),
                            ws[fs % 2][:, e * 512:(e + 1) * 512],
                            start=(e == 0), stop=(e == NT - 1))
                    st.inc(mi)
                kmmv[(fs, sc)] = st.n

                @blk.vector
                def _(v, fs=fs, sc=sc):
                    idx = fs * NT + sc
                    st.wait(v, kmmv[(fs, sc)])
                    sv.inc(v.tensor_copy(kslab[:, sc * 512:(sc + 1) * 512],
                                         pks[idx % 2][:, :]))
                kevv[fs * NT + sc] = sv.n
            kslab_ready = sv.n
            if fs + 2 < NF:
                wslab_load(fs + 2)

            for fb in range(4):
                gfb = fs * 4 + fb
                for eN in range(NF):
                    gidx = gfb * NF + eN

                    def emit_lmm(gidx2):
                        gfb2, eN2 = gidx2 // NF, gidx2 % NF

                        @blk.tensor
                        def _(t, gfb2=gfb2, eN2=eN2, gidx2=gidx2):
                            ss.wait(t, expv[gidx2])
                            for cc in range(4):
                                ec = eN2 * 4 + cc
                                mi = t.matmul(
                                    pl[:, ec:ec + 1],
                                    wts[gfb2 % 3][:, ec * 128:(ec + 1) * 128],
                                    ones_col[:, :],
                                    start=(gfb2 == 0 and ec == 0),
                                    stop=(gfb2 == NT - 1 and ec == NT - 1))
                            st.inc(mi)
                        lmmv[gidx2] = st.n

                    @blk.tensor
                    def _(t, fs=fs, fb=fb, eN=eN, gidx=gidx,
                          kready=kslab_ready):
                        if fb == 0 and eN == 0:
                            sv.wait(t, kready)
                        if gidx >= 3:
                            ss.wait(t, expv.get(gidx - 3, 0))
                        for sc in range(NT):
                            mi = t.matmul(
                                pss[gidx % 3][:, :],
                                kslab[:, sc * 512 + fb * 128:
                                      sc * 512 + fb * 128 + 128],
                                q_tm[:, sc * D + eN * 512:
                                     sc * D + eN * 512 + 512],
                                start=(sc == 0), stop=(sc == NT - 1))
                        st.inc(mi)
                    smmv[gidx] = st.n

                    @blk.scalar
                    def _(s, gfb=gfb, eN=eN, gidx=gidx):
                        st.wait(s, smmv[gidx])
                        if gfb >= 3:
                            st.wait(s, lmmv.get((gfb - 3) * NF + eN, 0))
                            sdW[(gfb - 3) % 3].wait(s, spv.get(gfb - 3, 0))
                        ss.inc(s.activation(
                            wts[gfb % 3][:, eN * 512:(eN + 1) * 512],
                            pss[gidx % 3][:, :], ACT.Exp, scale=0.125))
                    expv[gidx] = ss.n

                    if gidx >= 2:
                        emit_lmm(gidx - 2)
                    if gidx == NT * NF - 1:
                        emit_lmm(gidx - 1)
                        emit_lmm(gidx)

                @blk.sync
                def _(sy, gfb=gfb):
                    ss.wait(sy, expv[gfb * NF + NF - 1])
                    sdW[gfb % 3].inc(sy.dma_start(out=wT_sp[:, gfb, :],
                                                  in_=wts[gfb % 3][:, :]), 16)
                spv[gfb] = sdW[gfb % 3].n

        @blk.vector
        def _(v):
            st.wait(v)
            iv = sv.inc(v.tensor_copy(l_cols[:, :], pl[:, :]))
            v.wait_ge(sv.h, iv)
            sv.inc(v.reciprocal(r_cols[:, :], l_cols[:, :]))

        @blk.sync
        def _(sy):
            sdW[0].wait(sy)
            sdW[1].wait(sy)
            sdW[2].wait(sy)

    q_ctx.__exit__(None, None, None)
    if stop_after == "B3":
        return _dump_and_stop(nc, [("r_cols", r_cols)], [("wT_sp", wT_sp)])

    pTa = pT_ctx.enter_context(nc.psum_tensor("pTa", [128, 512], F32))
    pTb = pT_ctx.enter_context(nc.psum_tensor("pTb", [128, 512], F32))
    bc_tm = nc.alloc_sbuf_tensor("bc_tm", [B, D], F32, side="right")
    bc_b8s = nc.alloc_sbuf_tensor("bc_b8s", [B, D], F32, side="right")
    wckb = wckb_ctx.enter_context(
        nc.sbuf_tensor("wckb", [128, 16384], BF16, side="right"))
    swV = [Sem(nc, "swV0"), Sem(nc, "swV1")]
    v_ctx = nc.sbuf_tensor("v_fm", [128, NT * D], BF16, side="right")
    v_fm = v_ctx.__enter__()
    with ExitStack() as _ctx:
        _e = _ctx.enter_context
        w0 = _e(nc.sbuf_tensor("b35_w0", [128, NT * 128], BF16))
        w1 = _e(nc.sbuf_tensor("b35_w1", [128, NT * 128], BF16))
        p0 = _e(nc.psum_tensor("b35_p0", [128, 512], F32))
        p1 = _e(nc.psum_tensor("b35_p1", [128, 512], F32))
        blk = _e(nc.Block())
        ws = [w0, w1]
        ps = [p0, p1]
        wload = {}
        mmv = {}
        evv = {}

        def wcol_load(fb):
            @blk.gpsimd
            def _(g, fb=fb):
                if fb >= 2:
                    st.wait(g, mmv[(fb - 2, NF - 1)])
                for c in range(NT):
                    swV[fb % 2].inc(g.dma_start(
                        out=ws[fb % 2][:, c * 128:(c + 1) * 128],
                        in_=wv_h[fb, c * 128:(c + 1) * 128, :]), 16)
            wload[fb] = swV[fb % 2].n

        @blk.sync
        def _(sy):
            sdT.inc(sy.dma_start(out=bc_b8s[:, :], in_=bc_b8_h[:, :]), 16)

        wcol_load(0)
        wcol_load(1)
        for fb in range(NT):
            for sN in range(NF):
                @blk.tensor
                def _(t, fb=fb, sN=sN):
                    if sN == 0:
                        swV[fb % 2].wait(t, wload[fb])
                    idx = fb * NF + sN
                    if idx >= 2:
                        sv.wait(t, evv[idx - 2])
                    for e in range(NT):
                        mi = t.matmul(
                            ps[idx % 2][:, :],
                            ws[fb % 2][:, e * 128:(e + 1) * 128],
                            bass.AP(xn_fm, e * D + sN * 512,
                                    [[NT * D, 128], [1, 512]]),
                            start=(e == 0), stop=(e == NT - 1))
                    st.inc(mi)
                mmv[(fb, sN)] = st.n

                @blk.vector
                def _(v, fb=fb, sN=sN):
                    idx = fb * NF + sN
                    st.wait(v, mmv[(fb, sN)])
                    sv.inc(v.tensor_copy(
                        v_fm[:, fb * D + sN * 512: fb * D + sN * 512 + 512],
                        ps[idx % 2][:, :]))
                evv[fb * NF + sN] = sv.n
            if fb + 2 < NT:
                wcol_load(fb + 2)
            fm_seam(blk)

    xn_ctx.__exit__(None, None, None)
    if stop_after == "B35":
        return _dump_and_stop(nc, [("v_fm", v_fm)])

    swO = Sem(nc, "swO")
    sdC = [Sem(nc, "sdC0"), Sem(nc, "sdC1")]
    sdA = [Sem(nc, "sdA0"), Sem(nc, "sdA1")]
    wo_ctx = nc.sbuf_tensor("wo_bf", [128, NT * D], BF16)
    wo_bf = wo_ctx.__enter__()
    with ExitStack() as _ctx:
        _e = _ctx.enter_context
        c0 = _e(nc.sbuf_tensor("b4_c0", [128, NT * 128], BF16))
        c1 = _e(nc.sbuf_tensor("b4_c1", [128, NT * 128], BF16))
        a0 = _e(nc.sbuf_tensor("b4_a0", [128, D], BF16))
        a1 = _e(nc.sbuf_tensor("b4_a1", [128, D], BF16))
        pp = [_e(nc.psum_tensor("b4_p%d" % i, [128, 1024], F32))
              for i in range(3)]
        blk = _e(nc.Block())
        cs = [c0, c1]
        asb = [a0, a1]
        cload = {}
        mmv = {}
        evv = {}
        spv = {}

        @blk.gpsimd
        def _(g):
            for c in range(NT):
                swO.inc(g.dma_start(
                    out=wo_bf[:, c * D:(c + 1) * D],
                    in_=wo_h[c * 128:(c + 1) * 128, :]), 16)
        wo_loaded = swO.n

        def col_load(ec):
            @blk.sync
            def _(sy, ec=ec):
                if ec >= 2:
                    st.wait(sy, hmm[2 * (ec - 2) + 1])
                for hh in range(2):
                    sdC[ec % 2].inc(sy.dma_start(
                        out=cs[ec % 2][:, hh * 1024:(hh + 1) * 1024].rearrange(
                            "p (c f) -> p c f", f=128),
                        in_=wT_sp[:, hh * 8:(hh + 1) * 8,
                                  ec * 128:(ec + 1) * 128]), 16)
            cload[ec] = sdC[ec % 2].n

        col_load(0)
        col_load(1)
        hmm = {}
        hev = {}
        for ec in range(NT):
            for half in range(2):
                hid = 2 * ec + half

                @blk.tensor
                def _(t, ec=ec, half=half, hid=hid):
                    if half == 0:
                        sdC[ec % 2].wait(t, cload[ec])
                    if hid >= 3:
                        sv.wait(t, hev[hid - 3])
                    pr = pp[hid % 3]
                    for s2 in range(2):
                        sN = half * 2 + s2
                        for fb in range(NT):
                            mi = t.matmul(
                                pr[:, s2 * 512:(s2 + 1) * 512],
                                cs[ec % 2][:, fb * 128:(fb + 1) * 128],
                                v_fm[:, fb * D + sN * 512:
                                     fb * D + sN * 512 + 512],
                                start=(fb == 0), stop=(fb == NT - 1))
                        st.inc(mi)
                hmm[hid] = st.n

                @blk.vector
                def _(v, ec=ec, half=half, hid=hid):
                    st.wait(v, hmm[hid])
                    if half == 0 and ec >= 2:
                        sdA[ec % 2].wait(v, spv[ec - 2])
                    sv.inc(v.tensor_scalar_mul(
                        asb[ec % 2][:, half * 1024:(half + 1) * 1024],
                        pp[hid % 3][:, :], r_cols[:, ec:ec + 1]))
                hev[hid] = sv.n

            @blk.sync
            def _(sy, ec=ec):
                sv.wait(sy, hev[2 * ec + 1])
                for hh in range(2):
                    sdA[ec % 2].inc(sy.dma_start(
                        out=attn_sp[:, ec, hh * 1024:(hh + 1) * 1024],
                        in_=asb[ec % 2][:, hh * 1024:(hh + 1) * 1024]), 16)
            spv[ec] = sdA[ec % 2].n
            if ec + 2 < NT:
                col_load(ec + 2)
            fm_seam(blk)

        @blk.sync
        def _(sy):
            sdA[0].wait(sy)
            sdA[1].wait(sy)

    v_ctx.__exit__(None, None, None)
    wckb_ctx.close()
    pT_ctx.close()
    if stop_after == "B4":
        return _dump_and_stop(nc, [], [("attn_sp", attn_sp)])

    sdL = [Sem(nc, "sdL0"), Sem(nc, "sdL1")]
    sdO = [Sem(nc, "sdO0"), Sem(nc, "sdO1")]
    sdOG = Sem(nc, "sdOG")
    with ExitStack() as _ctx:
        _e = _ctx.enter_context
        og = _e(nc.sbuf_tensor("b6_og", [128, D], F32))
        ob = _e(nc.sbuf_tensor("b6_ob", [128, D], F32))
        x0 = _e(nc.sbuf_tensor("b6_x0", [128, D], F32))
        x1 = _e(nc.sbuf_tensor("b6_x1", [128, D], F32))
        g0 = _e(nc.sbuf_tensor("b6_g0", [128, D], BF16))
        g1 = _e(nc.sbuf_tensor("b6_g1", [128, D], BF16))
        ac0 = _e(nc.sbuf_tensor("b6_ac0", [128, NT * 128], BF16))
        ac1 = _e(nc.sbuf_tensor("b6_ac1", [128, NT * 128], BF16))
        wk1a = _e(nc.sbuf_tensor("b6_w1a", [128, D], F32))
        wk1b = _e(nc.sbuf_tensor("b6_w1b", [128, D], F32))
        wk2a = _e(nc.sbuf_tensor("b6_w2a", [128, D], F32))
        wk2b = _e(nc.sbuf_tensor("b6_w2b", [128, D], F32))
        o0 = _e(nc.sbuf_tensor("b6_o0", [128, D], F32))
        o1 = _e(nc.sbuf_tensor("b6_o1", [128, D], F32))
        bs0 = _e(nc.sbuf_tensor("b6_st0", [128, 28], F32))
        bs1 = _e(nc.sbuf_tensor("b6_st1", [128, 28], F32))
        pA = _e(nc.psum_tensor("b6_pA", [128, D], F32))
        pB = _e(nc.psum_tensor("b6_pB", [128, D], F32))
        blk = _e(nc.Block())
        xs = [x0, x1]
        gs = [g0, g1]
        acs = [ac0, ac1]
        os_ = [o0, o1]
        wk1s = [wk1a, wk1b]
        wk2s = [wk2a, wk2b]
        bss = [bs0, bs1]
        pp = [pA, pB]
        lval = {}
        mmv = {}
        ppfree = {}
        combv = {}
        outv = {}
        g1v = {}
        g2v = {}
        s4v = {}
        v1v = {}

        @blk.sync
        def _(sy):
            sdOG.inc(sy.dma_start(out=og[:, :], in_=og128_h[:, :]), 16)
            sdOG.inc(sy.dma_start(out=ob[:, :], in_=ob128_h[:, :]), 16)

        @blk.tensor
        def _(t):
            for n in range(NF):
                st.inc(t.matmul(pA[:, 512 * n:512 * (n + 1)], sel[:, :],
                                bc_tm[:, 512 * n:512 * (n + 1)],
                                start=True, stop=True))
        bcmm = st.n

        @blk.vector
        def _(v):
            st.wait(v, bcmm)
            sv.inc(v.tensor_copy(bc_rep[:, :], pA[:, :]))
        bcev = sv.n

        def loads(sc):
            @blk.sync
            def _(sy, sc=sc):
                if sc >= 2:
                    sv.wait(sy, combv[sc - 2])
                sdL[sc % 2].inc(sy.dma_start(
                    out=xs[sc % 2][:, :],
                    in_=x_h[sc * 128:(sc + 1) * 128, :]), 16)
                sdL[sc % 2].inc(sy.dma_start(
                    out=gs[sc % 2][:, :],
                    in_=gate_sp[sc * 128:(sc + 1) * 128, :]), 16)
                sdL[sc % 2].inc(sy.dma_start(
                    out=acs[sc % 2][:, :].rearrange("p (c f) -> p c f", f=128),
                    in_=attn_sp[:, :, sc * 128:(sc + 1) * 128]), 16)
            lval[sc] = sdL[sc % 2].n

        loads(0)
        loads(1)
        for sc in range(NT):
            p = sc % 2
            wk1 = wk1s[p]
            wk2 = wk2s[p]
            bst = bss[p]

            @blk.tensor
            def _(t, sc=sc, p=p):
                if sc == 0:
                    swO.wait(t, wo_loaded)
                    sv.wait(t, bcev)
                sdL[p].wait(t, lval[sc])
                if sc >= 2:
                    sv.wait(t, ppfree[sc - 2])
                for gN in range(NF):
                    for ec in range(NT):
                        mi = t.matmul(
                            pp[p][:, gN * 512:(gN + 1) * 512],
                            acs[p][:, ec * 128:(ec + 1) * 128],
                            wo_bf[:, ec * D + gN * 512: ec * D + gN * 512 + 512],
                            start=(ec == 0), stop=(ec == NT - 1))
                    st.inc(mi)
            mmv[sc] = st.n

            @blk.scalar
            def _(s, sc=sc, p=p, wk1=wk1, wk2=wk2):
                sdL[p].wait(s, lval[sc])
                if sc >= 2:
                    sg.wait(s, g1v[sc - 2])
                    sv.wait(s, combv[sc - 2])
                i1 = ss.inc(s.copy(wk2[:, :], gs[p][:, :]))
                ss.inc(s.activation(wk1[:, :], xs[p][:, :], ACT.Identity,
                                    bias=nmr_cols[:, sc:sc + 1],
                                    scale=rsig_cols[:, sc:sc + 1]))
            s12 = ss.n

            @blk.gpsimd
            def _(g, sc=sc, p=p, wk2=wk2, s1=s12 - 1):
                ss.wait(g, s1)
                if sc < 2:
                    sv.wait(g, bcev)
                else:
                    sv.wait(g, v1v[sc - 2])
                sg.inc(g.tensor_mul(wk2[:, :], wk2[:, :], bc_rep[:, :]))
            g1v[sc] = sg.n

            @blk.vector
            def _(v, sc=sc, p=p, wk1=wk1, wk2=wk2, bst=bst):
                if sc == 0:
                    sdOG.wait(v)
                ss.wait(v, s12)
                sg.wait(v, g1v[sc])
                if sc >= 2:
                    ss.wait(v, s4v[sc - 2])
                iv = sv.inc(v.tensor_add(wk1[:, :], wk1[:, :], wk2[:, :]))
                v.wait_ge(sv.h, iv)
                for c in range(NF):
                    ist = v.bn_stats(bst[:, 6 * c:6 * c + 6],
                                     wk1[:, 512 * c:512 * (c + 1)])
                iv = sv.inc(ist)
                v.wait_ge(sv.h, iv)
                iv = sv.inc(v.bn_aggr(bst[:, 24:26], bst[:, 0:24]))
                v.wait_ge(sv.h, iv)
                sv.inc(v.tensor_scalar_add(bst[:, 25:26], bst[:, 25:26], 1e-6))
            v1v[sc] = sv.n

            @blk.vector
            def _(v, sc=sc, p=p):
                st.wait(v, mmv[sc])
                if sc >= 2:
                    sdO[p].wait(v, outv[sc - 2])
                sv.inc(v.tensor_add(os_[p][:, :], xs[p][:, :], pp[p][:, :]))
            ppfree[sc] = sv.n

            @blk.scalar
            def _(s, sc=sc, bst=bst):
                sv.wait(s, v1v[sc])
                isx = ss.inc(s.activation(bst[:, 25:26], bst[:, 25:26], ACT.Ln))
                s.wait_ge(ss.h, isx)
                ss.inc(s.activation(bst[:, 26:27], bst[:, 25:26], ACT.Exp,
                                    scale=-0.5))
            s3v = ss.n

            @blk.vector
            def _(v, sc=sc, bst=bst, s3=s3v):
                ss.wait(v, s3)
                sv.inc(v.tensor_scalar(bst[:, 27:28], bst[:, 24:25],
                                       bst[:, 26:27], -1.0,
                                       ALU.mult, ALU.mult))
            v3v = sv.n

            @blk.scalar
            def _(s, sc=sc, wk1=wk1, bst=bst, v3=v3v):
                sv.wait(s, v3)
                ss.inc(s.activation(wk1[:, :], wk1[:, :], ACT.Identity,
                                    bias=bst[:, 27:28], scale=bst[:, 26:27]))
            s4v[sc] = ss.n

            @blk.gpsimd
            def _(g, sc=sc, wk1=wk1):
                ss.wait(g, s4v[sc])
                ig = sg.inc(g.tensor_mul(wk1[:, :], wk1[:, :], og[:, :]))
                g.wait_ge(sg.h, ig)
                sg.inc(g.tensor_add(wk1[:, :], wk1[:, :], ob[:, :]))
            g2v[sc] = sg.n

            @blk.vector
            def _(v, sc=sc, p=p, wk1=wk1):
                sg.wait(v, g2v[sc])
                sv.inc(v.tensor_add(os_[p][:, :], os_[p][:, :], wk1[:, :]))
            combv[sc] = sv.n

            @blk.sync
            def _(sy, sc=sc, p=p):
                sv.wait(sy, combv[sc])
                sdO[p].inc(sy.dma_start(
                    out=out_h[sc * 128:(sc + 1) * 128, :],
                    in_=os_[p][:, :]), 16)
            outv[sc] = sdO[p].n
            if sc + 2 < NT:
                loads(sc + 2)

        @blk.sync
        def _(sy):
            sdO[0].wait(sy)
            sdO[1].wait(sy)

    wo_ctx.__exit__(None, None, None)
    return nc


_CACHE = {}


def kernel(L0_attn_x, L0_attn_Wq, L0_attn_Wk, L0_attn_Wv, L0_attn_Wo,
           L0_ffn_in_gamma, L0_ffn_in_beta, L0_ffn_agg_w, L0_ffn_agg_b,
           L0_ffn_pn_gamma, L0_ffn_pn_beta, L0_ffn_d1_w, L0_ffn_d1_b,
           L0_ffn_d2_w, L0_ffn_d2_b, L0_ffn_bc_w, L0_ffn_bc_b,
           L0_ffn_gate_w, L0_ffn_gate_b, L0_ffn_out_gamma, L0_ffn_out_beta):
    f32 = np.float32
    X = np.ascontiguousarray(np.asarray(L0_attn_x, dtype=f32))
    wq = np.ascontiguousarray(np.asarray(L0_attn_Wq, dtype=f32))
    wk = np.ascontiguousarray(np.asarray(L0_attn_Wk, dtype=f32))
    wv = np.ascontiguousarray(np.asarray(L0_attn_Wv, dtype=f32))
    wo = np.ascontiguousarray(np.asarray(L0_attn_Wo, dtype=f32))
    gatew = np.ascontiguousarray(np.asarray(L0_ffn_gate_w, dtype=f32))
    aggw = np.ascontiguousarray(np.asarray(L0_ffn_agg_w, dtype=f32))
    bcw = np.ascontiguousarray(np.asarray(L0_ffn_bc_w, dtype=f32))
    d1w = np.ascontiguousarray(np.asarray(L0_ffn_d1_w, dtype=f32))
    d2w = np.ascontiguousarray(np.asarray(L0_ffn_d2_w, dtype=f32))

    wv_blk = np.ascontiguousarray(wv.reshape(D, NT, 128).transpose(1, 0, 2))
    lastrows = np.ascontiguousarray(X[:, D - 1, :])

    def rep8(vec, w=D):
        return np.ascontiguousarray(
            np.broadcast_to(np.asarray(vec, dtype=f32).reshape(1, w), (B, w)))

    in_g8 = rep8(L0_ffn_in_gamma)
    in_b8 = rep8(L0_ffn_in_beta)
    pn_g_fm = np.ascontiguousarray(
        np.asarray(L0_ffn_pn_gamma, dtype=f32).reshape(NT, 128).T)
    pn_b_fm = np.ascontiguousarray(
        np.asarray(L0_ffn_pn_beta, dtype=f32).reshape(NT, 128).T)
    agg_b8 = rep8(L0_ffn_agg_b)
    d2b_fm = np.ascontiguousarray(
        np.asarray(L0_ffn_d2_b, dtype=f32).reshape(NT, 128).T)
    bc_b8 = rep8(L0_ffn_bc_b)
    og128 = np.ascontiguousarray(np.broadcast_to(
        np.asarray(L0_ffn_out_gamma, dtype=f32).reshape(1, D), (128, D)))
    ob128 = np.ascontiguousarray(np.broadcast_to(
        np.asarray(L0_ffn_out_beta, dtype=f32).reshape(1, D), (128, D)))
    gate_b_row = np.ascontiguousarray(
        np.asarray(L0_ffn_gate_b, dtype=f32).reshape(1, D))
    d1b = np.asarray(L0_ffn_d1_b, dtype=f32)

    ident = np.eye(128, dtype=BFD)
    ident32 = np.eye(128, dtype=f32)
    ones_col = np.ones((128, 1), dtype=BFD)
    ones_col32 = np.ones((128, 1), dtype=f32)
    ones1 = np.ones((1, 128), dtype=BFD)
    ones1_32 = np.ones((1, 128), dtype=f32)

    if "nc" not in _CACHE:
        _CACHE["nc"] = build_nc()
    nc = _CACHE["nc"]

    in_maps = []
    for b in range(B):
        selb = np.zeros((B, 128), dtype=f32)
        selb[b, :] = 1.0
        in_maps.append({
            "x": X[b],
            "wq": wq, "wk": wk, "wo": wo, "gatew": gatew,
            "aggw_sl": np.ascontiguousarray(aggw[:, b * 256:(b + 1) * 256]),
            "bcw_sl": np.ascontiguousarray(bcw[:, b * 256:(b + 1) * 256]),
            "wv_blk": wv_blk,
            "d1s_pm": np.ascontiguousarray(
                d1w[:, b * HSL:(b + 1) * HSL].reshape(NT, 128, NHC, 128)),
            "d2s_pm": np.ascontiguousarray(
                d2w[b * HSL:(b + 1) * HSL, :].reshape(NHC, 128, NT, 128)),
            "lastrows": lastrows,
            "in_g8": in_g8, "in_b8": in_b8,
            "pn_g_fm": pn_g_fm, "pn_b_fm": pn_b_fm,
            "agg_bsl": np.ascontiguousarray(
                np.broadcast_to(agg_b8[0:1, b * 256:(b + 1) * 256], (B, 256))),
            "d1b_fm": np.ascontiguousarray(
                d1b[b * HSL:(b + 1) * HSL].reshape(NHC, 128).T),
            "d2b_fm": d2b_fm, "bc_b8": bc_b8,
            "og128": og128, "ob128": ob128, "gate_b_row": gate_b_row,
            "sel_rep": selb, "ident": ident, "ident32": ident32,
            "ones_col": ones_col, "ones_col32": ones_col32,
            "ones1": ones1, "ones1_32": ones1_32,
        })

    res = run_bass_kernel_spmd(nc, in_maps, core_ids=list(range(B)))
    out = np.stack([res.results[b]["out"] for b in range(B)], axis=0)
    return out
